# version 10
# baseline (speedup 1.0000x reference)
import math
from contextlib import ExitStack

import numpy as np
import ml_dtypes

import concourse.bass as bass
import concourse.mybir as mybir
from concourse.bass_utils import run_bass_kernel_spmd

F32 = mybir.dt.float32
BF16 = mybir.dt.bfloat16
I32 = mybir.dt.int32
U8 = mybir.dt.uint8
AF = mybir.ActivationFunctionType
ALU = mybir.AluOpType
AX = mybir.AxisListType.X

B, N, D = 2, 4096, 4096
H, HD = 32, 128
II = 11008
T = 1024
TC = T // 128
DC = D // 128
ICN = II // 128
RS = 2 * D
INV_SQRT_HD = 1.0 / math.sqrt(HD)
ENGS = ("sync", "tensor", "vector", "scalar", "gpsimd")
LN_EPS = 1e-5
RMS_EPS = 1e-6


class Sem:
    __slots__ = ("h", "n")

    def __init__(self, h):
        self.h = h
        self.n = 0


class Builder:
    def __init__(self):
        self.nc = bass.Bass("TRN2", target_bir_lowering=False)
        self.q = {e: [] for e in ENGS}
        self.ctx = ExitStack()
        self.bar_k = 0

    def sem(self, name):
        return Sem(self.ctx.enter_context(self.nc.semaphore(name)))

    def semlist(self, name, k):
        return [self.sem(f"{name}{i}") for i in range(k)]

    def sbuf(self, name, shape, dtype):
        return self.ctx.enter_context(self.nc.sbuf_tensor(name, shape, dtype))

    def emit(self, eng, fn):
        self.q[eng].append(fn)

    def wait(self, eng, sem, val=None):
        v = sem.n if val is None else val
        if v > 0:
            self.emit(eng, lambda e, s=sem.h, vv=v: e.wait_ge(s, vv))

    SELF_SYNC = ("vector", "scalar", "gpsimd")

    def run(self, eng, build, sem=None, k=1):
        if eng in self.SELF_SYNC and sem is not None and sem.n > 0:
            self.wait(eng, sem, sem.n)
        if sem is not None:
            sem.n += k
        def f(e, bld=build, s=sem, kk=k):
            ins = bld(e)
            if s is not None:
                ins.then_inc(s.h, kk)
        self.emit(eng, f)
        return sem.n if sem is not None else None

    def dma(self, eng, out, in_, sem, **kw):
        return self.run(eng, lambda e, o=out, i=in_, k=kw: e.dma_start(out=o, in_=i, **k),
                        sem, 16)

    def barrier(self, drain=True):
        self.bar_k += 1
        tgt = 5 * self.bar_k
        for eng in ENGS:
            if drain and eng in ("tensor", "vector", "scalar", "gpsimd"):
                self.emit(eng, lambda e: e.drain())
            self.emit(eng, lambda e, s=self.bar.h: e.sem_inc(s, 1))
            self.emit(eng, lambda e, s=self.bar.h, v=tgt: e.wait_ge(s, v))

    def finalize_block(self):
        nc = self.nc
        with nc.Block() as blk:
            for eng in ENGS:
                fns = self.q[eng]
                getattr(blk, eng)(lambda e, fns=fns: [fn(e) for fn in fns])


def build_program():
    import os as _os
    KT = _os.environ.get("KTRUNC", "")
    b = Builder()
    nc = b.nc

    def din(name, shape, dtype):
        return nc.dram_tensor(name, shape, dtype, kind="ExternalInput").ap()

    x_in = din("x_in", [T, D], F32)
    wq = din("wq", [D, D], F32)
    wk = din("wk", [D, D], F32)
    wv = din("wv", [D, D], F32)
    wo = din("wo", [D, D], F32)
    qwd = din("qwd", [D, II], I32)
    pg_in = din("pg", [ICN, 128, D], U8)
    pu_in = din("pu", [ICN, 128, D], U8)
    rg_in = din("rg", [ICN, D], F32)
    ru_in = din("ru", [ICN, D], F32)
    sgT_in = din("sgT", [32, II], F32)
    zgT_in = din("zgT", [32, II], F32)
    suT_in = din("suT", [32, II], F32)
    zuT_in = din("zuT", [32, II], F32)
    csel_in = din("csel", [128, 255], BF16)
    sd_in = din("sd", [128, 32 * ICN], F32)
    zd_in = din("zd", [128, 32 * ICN], F32)
    bq_in = din("bq_p", [128, 32], F32)
    bk_in = din("bk_p", [128, 32], F32)
    bv_in = din("bv_p", [128, 32], F32)
    bo_in = din("bo_p", [128, 32], F32)
    gq_in = din("gq_c", [128, 1], F32)
    gk_in = din("gk_c", [128, 1], F32)
    id_in = din("ident_bf", [128, 128], BF16)
    idf_in = din("ident_f", [128, 128], F32)
    onb_in = din("ones_bf", [128, 128], BF16)
    onf_in = din("ones_f", [128, 128], F32)

    y_out = nc.dram_tensor("y", [T, D], F32, kind="ExternalOutput").ap()

    cck_in = nc.dram_tensor("cck_in", [D, 1024], BF16).ap()
    cck_out = nc.dram_tensor("cck_out", [4 * D, 1024], BF16).ap()
    ccv_in = nc.dram_tensor("ccv_in", [D, 1024], BF16).ap()
    ccv_out = nc.dram_tensor("ccv_out", [4 * D, 1024], BF16).ap()
    q_dram = nc.dram_tensor("q_dram", [D, T], BF16).ap()
    dump = _os.environ.get("KDUMP")
    if dump:
        x1_dram = nc.dram_tensor("x1_dram", [T, D], F32, kind="ExternalOutput").ap()
        hact_dram = nc.dram_tensor("hact_dram", [II, T], BF16, kind="ExternalOutput").ap()
    else:
        x1_dram = nc.dram_tensor("x1_dram", [T, D], F32).ap()
        hact_dram = nc.dram_tensor("hact_dram", [II, T], BF16).ap()
    wdt_dram = nc.dram_tensor("wdt_dram", [32, 128, II], BF16).ap()

    b.bar = b.sem("bar")
    ld_sy = b.sem("ld_sy")
    s_pe = b.sem("s_pe")
    s_v = b.sem("s_v")
    s_s = b.sem("s_s")
    s_gp = b.sem("s_gp")
    s_cc = b.sem("s_cc")
    ldx = b.semlist("ldx", 2)
    ldw = b.semlist("ldw", 2)
    stk = b.semlist("stk", 2)
    stv = b.semlist("stv", 4)
    ldk = b.semlist("ldk", 2)
    ldv = b.semlist("ldv", 2)
    ldq = b.semlist("ldq", 2)
    ldx4 = b.semlist("ldx4", 4)
    stx1 = b.semlist("stx1", 4)
    ldqg = b.semlist("ldqg", 2)
    ldqu = b.semlist("ldqu", 2)
    ldrg = b.semlist("ldrg", 2)
    ldru = b.semlist("ldru", 2)
    sth = b.semlist("sth", 2)
    ldqd = b.semlist("ldqd", 2)
    stw = b.semlist("stw", 2)
    ldb = b.semlist("ldb", 2)
    ldx1b = b.sem("ldx1b")
    sty4 = b.semlist("sty4", 4)

    ident = b.sbuf("ident_sb", [128, 128], BF16)
    ones_b = b.sbuf("ones_b_sb", [128, 128], BF16)
    ones_f = b.sbuf("ones_f_sb", [128, 128], F32)
    bq_sb = b.sbuf("bq_sb", [128, 32], F32)
    bk_sb = b.sbuf("bk_sb", [128, 32], F32)
    bv_sb = b.sbuf("bv_sb", [128, 32], F32)
    bo_sb = b.sbuf("bo_sb", [128, 32], F32)
    gq_sb = b.sbuf("gq_sb", [128, 1], F32)
    gk_sb = b.sbuf("gk_sb", [128, 1], F32)
    epsc = b.sbuf("epsc", [128, 2], F32)
    idf_in_sb = b.sbuf("idf_sb", [128, 128], F32)
    csel_sb = b.sbuf("csel_sb", [128, 255], BF16)

    for dst, src in ((ident, id_in), (ones_b, onb_in), (ones_f, onf_in),
                     (bq_sb, bq_in), (bk_sb, bk_in), (bv_sb, bv_in),
                     (bo_sb, bo_in), (gk_sb, gk_in), (gq_sb, gq_in),
                     (idf_in_sb, idf_in), (csel_sb, csel_in)):
        b.dma("sync", dst[:], src, ld_sy)
    b.wait("vector", ld_sy)
    b.run("vector", lambda e: e.tensor_scalar(gq_sb[:], gq_sb[:], INV_SQRT_HD,
                                              None, ALU.mult), s_v)
    b.run("vector", lambda e: e.memset(epsc[:, 0:1], LN_EPS), s_v)
    b.run("vector", lambda e: e.memset(epsc[:, 1:2], RMS_EPS), s_v)
    b.barrier()

    def ln_transpose(src_dram, xnT_dst, sc, pre_wait=None):
        xt, xsq, xnb, ps = sc["xt"], sc["xsq"], sc["xnb"], sc["ps"]
        cols = sc["cols"]
        xnT_v = xnT_dst[:].rearrange("p (f t) -> p f t", f=DC)
        pe_done, ev_done, v_apply = [], [], []
        for tau in range(TC):
            par = tau % 2
            if tau == 0 and pre_wait is not None:
                for sem_, val_ in pre_wait:
                    b.wait("sync", sem_, val_)
            if tau >= 2:
                b.wait("sync", s_v, v_apply[tau - 2])
            b.dma("sync", xt[par][:], src_dram[tau * 128:(tau + 1) * 128, :],
                  ldx[par])
            ld_tgt = ldx[par].n
            b.wait("vector", ldx[par], ld_tgt)
            b.run("vector", lambda e, p=par: e.reduce_sum(
                cols[:, 0:1], xt[p][:], axis=AX), s_v)
            b.wait("scalar", ldx[par], ld_tgt)
            if tau >= 1:
                b.wait("scalar", s_v, v_apply[tau - 1])
            b.run("scalar", lambda e, p=par: e.activation(
                xsq[:], xt[p][:], AF.Square), s_s)
            sq_tgt = s_s.n
            b.wait("vector", s_s, sq_tgt)
            b.run("vector", lambda e: e.reduce_sum(cols[:, 2:3], xsq[:], axis=AX),
                  s_v)
            b.run("vector", lambda e: e.tensor_scalar(
                cols[:, 1:2], cols[:, 0:1], 1.0 / D, None, ALU.mult), s_v)
            b.run("vector", lambda e: e.tensor_scalar(
                cols[:, 3:4], cols[:, 2:3], 1.0 / D, None, ALU.mult), s_v)
            b.run("vector", lambda e: e.tensor_tensor(
                cols[:, 6:7], cols[:, 1:2], cols[:, 1:2], ALU.mult), s_v)
            b.run("vector", lambda e: e.tensor_scalar(
                cols[:, 6:7], cols[:, 6:7], -1.0, None, ALU.mult), s_v)
            b.run("vector", lambda e: e.tensor_tensor(
                cols[:, 3:4], cols[:, 3:4], cols[:, 6:7], ALU.add), s_v)
            var_tgt = s_v.n
            b.wait("scalar", s_v, var_tgt)
            b.run("scalar", lambda e: e.activation(
                cols[:, 4:5], cols[:, 3:4], AF.Sqrt, bias=epsc[:, 0:1]), s_s)
            std_tgt = s_s.n
            b.wait("vector", s_s, std_tgt)
            b.run("vector", lambda e: e.reciprocal(cols[:, 5:6], cols[:, 4:5]), s_v)
            if tau >= 2:
                b.wait("vector", s_pe, pe_done[tau - 2])
            b.run("vector", lambda e, p=par: e.tensor_scalar(
                xnb[p][:], xt[p][:], cols[:, 1:2], cols[:, 5:6],
                ALU.subtract, ALU.mult), s_v)
            v_apply.append(s_v.n)
            b.wait("tensor", s_v, v_apply[-1])
            for qd in range(8):
                reg = qd % 2
                gq_i = tau * 8 + qd
                if gq_i >= 2:
                    b.wait("tensor", s_s, ev_done[gq_i - 2])
                for j in range(4):
                    f = qd * 4 + j
                    b.run("tensor", lambda e, p=par, ff=f, r=reg, jj=j: e.matmul(
                        ps[:, r * 512 + jj * 128: r * 512 + (jj + 1) * 128],
                        xnb[p][:, ff * 128:(ff + 1) * 128], ident[:],
                        start=(jj == 0), stop=(jj == 3)),
                        s_pe if j == 3 else None)
                b.wait("scalar", s_pe, s_pe.n)
                b.run("scalar", lambda e, r=reg, qd_=qd, t_=tau: e.copy(
                    xnT_v[:, qd_ * 4:(qd_ + 1) * 4, t_ * 128:(t_ + 1) * 128],
                    ps[:, r * 512:(r + 1) * 512].rearrange(
                        "p (a c) -> p a c", a=4)), s_s)
                ev_done.append(s_s.n)
            pe_done.append(s_pe.n)

    with ExitStack() as sA:
        eA = sA.enter_context
        xnT = eA(nc.sbuf_tensor("xnT", [128, DC * T], BF16))

        with ExitStack() as sA1:
            e1 = sA1.enter_context
            sc1 = dict(
                xt=(e1(nc.sbuf_tensor("xt0", [128, D], F32)),
                    e1(nc.sbuf_tensor("xt1", [128, D], F32))),
                xsq=e1(nc.sbuf_tensor("xsq", [128, D], F32)),
                xnb=(e1(nc.sbuf_tensor("xnb0", [128, D], BF16)),
                     e1(nc.sbuf_tensor("xnb1", [128, D], BF16))),
                cols=e1(nc.sbuf_tensor("cols", [128, 8], F32)),
                ps=e1(nc.psum_tensor("ps_a", [128, 1024], F32)))
            ln_transpose(x_in, xnT, sc1)
        b.barrier()

        wsb = (eA(nc.sbuf_tensor("wsb0", [128, DC * 512], BF16)),
               eA(nc.sbuf_tensor("wsb1", [128, DC * 512], BF16)))
        kraw = [eA(nc.sbuf_tensor(f"kraw{i}", [128, 512], F32)) for i in range(4)]
        k2b = [eA(nc.sbuf_tensor(f"k2b{i}", [128, 512], BF16)) for i in range(2)]
        kp2 = [eA(nc.sbuf_tensor(f"kp2{i}", [128, 512], BF16)) for i in range(2)]
        tmpf = [eA(nc.sbuf_tensor(f"tmpf{i}", [128, 512], F32)) for i in range(2)]
        srow = [eA(nc.sbuf_tensor(f"srow{i}", [1, 512], F32)) for i in range(2)]
        rrow = [eA(nc.sbuf_tensor(f"rrow{i}", [1, 512], F32)) for i in range(2)]
        kbf = [eA(nc.sbuf_tensor(f"kbf{i}", [128, 1024], BF16)) for i in range(2)]
        vpc = [eA(nc.sbuf_tensor(f"vpc{i}", [128, 512], BF16)) for i in range(4)]
        ps_p = eA(nc.psum_tensor("ps_p", [128, 1024], F32))
        ps_r1 = eA(nc.psum_tensor("ps_r1", [128, 512], F32))
        ps_bc = eA(nc.psum_tensor("ps_bc", [128, 512], F32))
        ps_r2 = eA(nc.psum_tensor("ps_r2", [128, 512], F32))
        ps_v2 = eA(nc.psum_tensor("ps_v2", [128, 1024], F32))

        xnT_v = xnT[:].rearrange("p (f t) -> p f t", f=DC)

        def emit_proj(w_dram, bias_sb, g_col, out_kind):
            b.wait("tensor", s_s, s_s.n)
            b.wait("tensor", s_v, s_v.n)
            b.wait("scalar", s_v, s_v.n)
            b.wait("vector", s_s, s_s.n)
            b.wait("vector", stk[0], stk[0].n)
            b.wait("vector", stk[1], stk[1].n)

            pe_base, s_base, v_base = s_pe.n, s_s.n, s_v.n
            pe_entry = s_pe.n
            stk_base = [stk[0].n, stk[1].n]
            ldw_base = [ldw[0].n, ldw[1].n]

            def sq_inc(i):
                return s_base + 2 * i + 1
            def srow_inc(i):
                return s_base + 2 * i + 2
            v_kraw = [0] * 64
            v_rr = [0] * 64
            v_tt = [0] * 64
            v_kbf = [0] * 64
            cnt = v_base
            for i in range(64 + 3):
                if i < 64:
                    cnt += 1
                    v_kraw[i] = cnt
                if 0 <= i - 1 < 64:
                    cnt += 1
                    v_rr[i - 1] = cnt
                if 0 <= i - 3 < 64:
                    cnt += 2
                    v_tt[i - 3] = cnt - 1
                    v_kbf[i - 3] = cnt
            pe_proj = [0] * 64
            pe_sumsq = [0] * 64
            pe_bc = [0] * 64

            for i in range(64 + 4):
                if 0 <= i - 3 < 64:
                    j = i - 3
                    b.wait("tensor", s_v, v_rr[j])
                    if j >= 1:
                        b.wait("tensor", s_v, v_tt[j - 1])
                    b.run("tensor", lambda e, jj=j % 2: e.matmul(
                        ps_bc[:], ones_f[0:1, :], rrow[jj][0:1, :],
                        start=True, stop=True), s_pe)
                    pe_bc[j] = s_pe.n
                if i < 64:
                    h, th = divmod(i, 2)
                    p = h // 4
                    oc = h % 4
                    par = i % 2
                    if th == 0 and oc == 0:
                        b.wait("tensor", ldw[p % 2],
                               ldw_base[p % 2] + 16 * (p // 2 + 1))
                    if i >= 2:
                        b.wait("tensor", s_v, v_kraw[i - 2])
                    for f in range(DC):
                        b.run("tensor", lambda e, pp=p, o=oc, tt=th, ff=f, pr=par:
                              e.matmul(ps_p[:, pr * 512:(pr + 1) * 512],
                                       wsb[pp % 2][:, ff * 512 + o * 128:
                                                   ff * 512 + (o + 1) * 128],
                                       xnT_v[:, ff, tt * 512:(tt + 1) * 512],
                                       start=(ff == 0), stop=(ff == DC - 1)),
                              s_pe if f == DC - 1 else None)
                    pe_proj[i] = s_pe.n
                if 0 <= i - 1 < 64:
                    j = i - 1
                    b.wait("tensor", s_s, sq_inc(j))
                    if j >= 2:
                        b.wait("tensor", s_s, srow_inc(j - 2))
                    b.run("tensor", lambda e, jj=j % 2: e.matmul(
                        ps_r1[0:1, :], ones_b[:, 0:1], k2b[jj][:],
                        start=True, stop=True), s_pe)
                    pe_sumsq[j] = s_pe.n

            for i in range(64):
                b.wait("scalar", s_v, v_kraw[i])
                if i >= 2:
                    b.wait("scalar", s_pe, pe_sumsq[i - 2])
                b.run("scalar", lambda e, i4=i % 4, pr=i % 2: e.activation(
                    k2b[pr][:], kraw[i4][:], AF.Square), s_s)
                b.wait("scalar", s_pe, pe_sumsq[i])
                if i >= 2:
                    b.wait("scalar", s_v, v_rr[i - 2])
                b.run("scalar", lambda e, jj=i % 2: e.activation(
                    srow[jj][0:1, :], ps_r1[0:1, :], AF.Sqrt, scale=1.0 / HD,
                    bias=epsc[0:1, 1:2]), s_s)

            for i in range(64 + 3):
                if i < 64:
                    h, th = divmod(i, 2)
                    b.wait("vector", s_pe, pe_proj[i])
                    if i >= 4:
                        b.wait("vector", s_s, sq_inc(i - 4))
                    b.run("vector", lambda e, hh=h, pr=i % 2, i4=i % 4:
                          e.tensor_scalar(
                              kraw[i4][:], ps_p[:, pr * 512:(pr + 1) * 512],
                              bias_sb[:, hh:hh + 1], None, ALU.add), s_v)
                if 0 <= i - 1 < 64:
                    j = i - 1
                    b.wait("vector", s_s, srow_inc(j))
                    if j >= 2:
                        b.wait("vector", s_pe, pe_bc[j - 2])
                    b.run("vector", lambda e, jj=j % 2: e.reciprocal(
                        rrow[jj][0:1, :], srow[jj][0:1, :]), s_v)
                if 0 <= i - 3 < 64:
                    j = i - 3
                    h, th = divmod(j, 2)
                    b.wait("vector", s_pe, pe_bc[j])
                    b.run("vector", lambda e, i4=j % 4, pr=j % 2: e.tensor_tensor(
                        tmpf[pr][:], kraw[i4][:], ps_bc[:], ALU.mult), s_v)
                    if j >= 4:
                        b.wait("vector", stk[h % 2],
                               stk_base[h % 2] + 16 * (h // 2))
                    b.run("vector", lambda e, hh=h, tt=th, pr=j % 2:
                          e.tensor_scalar(
                              kbf[hh % 2][:, tt * 512:(tt + 1) * 512],
                              tmpf[pr][:], g_col[:, 0:1], None, ALU.mult), s_v)

            w_re = w_dram.rearrange("(f p) o -> p f o", p=128)
            for p in range(8):
                if p >= 2:
                    b.wait("gpsimd", s_pe, pe_proj[(p - 2) * 8 + 7])
                else:
                    b.wait("gpsimd", s_pe, pe_entry)
                b.dma("gpsimd", wsb[p % 2][:].rearrange("p (f o) -> p f o", f=DC),
                      w_re[:, :, p * 512:(p + 1) * 512], ldw[p % 2])

            for h in range(H):
                b.wait("sync", s_v, v_kbf[2 * h + 1])
                dst = (cck_in[h * 128:(h + 1) * 128, :] if out_kind == "k"
                       else q_dram[h * 128:(h + 1) * 128, :])
                b.dma("sync", dst, kbf[h % 2][:], stk[h % 2])

        emit_proj(wk, bk_sb, gk_sb, "k")
        for s_ in stk:
            b.wait("gpsimd", s_, s_.n)
        b.run("gpsimd", lambda e: e.collective_compute(
            "AllGather", ALU.bypass,
            replica_groups=[[0, 1, 2, 3], [4, 5, 6, 7]],
            ins=[cck_in.opt()], outs=[cck_out.opt()]), s_cc, 1)

        b.wait("tensor", s_s, s_s.n)
        b.wait("vector", s_s, s_s.n)
        v_pe_base, v_v_base = s_pe.n, s_v.n
        v_st_base = [s.n for s in stv]
        v_ld_base = [ldw[0].n, ldw[1].n]
        def vg_pe(gi):
            return v_pe_base + gi + 1
        def vg_v(gi, t2):
            return v_v_base + gi * 2 + t2 + 1
        w_re = wv.rearrange("(f p) o -> p f o", p=128)
        for p in range(8):
            if p >= 2:
                b.wait("gpsimd", s_pe, vg_pe((p - 1) * 4 - 1))
            else:
                b.wait("gpsimd", s_pe, v_pe_base)
            b.dma("gpsimd", wsb[p % 2][:].rearrange("p (f o) -> p f o", f=DC),
                  w_re[:, :, p * 512:(p + 1) * 512], ldw[p % 2])
        for p in range(8):
            b.wait("tensor", ldw[p % 2], v_ld_base[p % 2] + 16 * (p // 2 + 1))
            for tcg in range(4):
                gi = p * 4 + tcg
                if gi >= 1:
                    b.wait("tensor", s_v, vg_v(gi - 1, 1))
                for f in range(DC):
                    for t2 in range(2):
                        tc = tcg * 2 + t2
                        b.run("tensor", lambda e, ff=f, t2_=t2, tc_=tc, pp=p:
                              e.matmul(ps_v2[:, t2_ * 512:(t2_ + 1) * 512],
                                       xnT_v[:, ff, tc_ * 128:(tc_ + 1) * 128],
                                       wsb[pp % 2][:, ff * 512:(ff + 1) * 512],
                                       start=(ff == 0), stop=(ff == DC - 1)),
                              s_pe if (f == DC - 1 and t2 == 1) else None)
                pe_tgt = s_pe.n
                for t2 in range(2):
                    tc = tcg * 2 + t2
                    b.wait("vector", s_pe, pe_tgt)
                    vslot = (gi * 2 + t2) % 4
                    if gi * 2 + t2 >= 4:
                        b.wait("vector", stv[vslot],
                               v_st_base[vslot] + 16 * ((gi * 2 + t2) // 4))
                    b.run("vector", lambda e, sl=vslot, t2_=t2: e.tensor_copy(
                        vpc[sl][:], ps_v2[:, t2_ * 512:(t2_ + 1) * 512]), s_v)
                    b.wait("sync", s_v, s_v.n)
                    off = (p // 2) * 1024 + (p % 2) * 512
                    dst = ccv_in[tc * 512:(tc + 1) * 512, :].rearrange(
                        "(t f) c -> t (f c)", t=128)[:, off:off + 512]
                    b.dma("sync", dst, vpc[vslot][:], stv[vslot])

        for s_ in stv:
            b.wait("gpsimd", s_, s_.n)
        b.run("gpsimd", lambda e: e.collective_compute(
            "AllGather", ALU.bypass,
            replica_groups=[[0, 1, 2, 3], [4, 5, 6, 7]],
            ins=[ccv_in.opt()], outs=[ccv_out.opt()]), s_cc, 1)

        emit_proj(wq, bq_sb, gq_sb, "q")
        q_st_final = [stk[0].n, stk[1].n]
        b.barrier()

    with ExitStack() as sB:
        eB = sB.enter_context
        ao = eB(nc.sbuf_tensor("ao", [128, H * T], BF16))

        with ExitStack() as sB1:
            e4 = sB1.enter_context
            probsT = [e4(nc.sbuf_tensor(f"probsT{i}", [128, DC * 512], BF16))
                      for i in range(2)]
            khb = [e4(nc.sbuf_tensor(f"khb{i}", [128, 4096], BF16))
                   for i in range(2)]
            vhb = [e4(nc.sbuf_tensor(f"vhb{i}", [128, 4096], BF16))
                   for i in range(2)]
            qhb = [e4(nc.sbuf_tensor(f"qhb{i}", [128, 1024], BF16))
                   for i in range(2)]
            recb = [e4(nc.sbuf_tensor(f"recb{i}", [128, 512], F32))
                    for i in range(2)]
            tta = [e4(nc.sbuf_tensor(f"tta{i}", [128, 512], F32))
                   for i in range(2)]
            grow = e4(nc.sbuf_tensor("grow", [1, 256], F32))
            nbcol = e4(nc.sbuf_tensor("nbcol", [128, 1], F32))
            srow2 = e4(nc.sbuf_tensor("srow2", [1, 512], F32))
            rrow2 = e4(nc.sbuf_tensor("rrow2", [1, 512], F32))
            ps_sc = e4(nc.psum_tensor("ps_sc", [128, 2048], F32))
            ps_pv = e4(nc.psum_tensor("ps_pv", [128, 1024], F32))
            ps_sum = e4(nc.psum_tensor("ps_sum", [128, 512], F32))
            ps_m = e4(nc.psum_tensor("ps_m", [128, 512], F32))

            cc_r = cck_out.rearrange("(r z) c -> r z c", r=4)

            b.run("tensor", lambda e: e.matmul(
                ps_m[0:1, 0:128], gq_sb[:, 0:1], idf_in_sb[:],
                start=True, stop=True), s_pe)
            b.run("tensor", lambda e: e.matmul(
                ps_m[0:1, 128:256], gk_sb[:, 0:1], idf_in_sb[:],
                start=True, stop=True), s_pe)
            gt_tgt = s_pe.n
            b.wait("scalar", s_pe, gt_tgt)
            b.run("scalar", lambda e: e.copy(grow[0:1, :], ps_m[0:1, 0:256]), s_s)
            g_cp = s_s.n
            b.wait("vector", s_s, g_cp)
            b.run("vector", lambda e: e.tensor_tensor(
                grow[0:1, :], grow[0:1, :], grow[0:1, :], ALU.mult), s_v)
            b.run("vector", lambda e: e.reduce_max(
                grow[0:1, 0:1], grow[0:1, 0:128], axis=AX), s_v)
            b.run("vector", lambda e: e.reduce_max(
                grow[0:1, 1:2], grow[0:1, 128:256], axis=AX), s_v)
            b.run("vector", lambda e: e.tensor_tensor(
                grow[0:1, 0:1], grow[0:1, 0:1], grow[0:1, 1:2], ALU.mult), s_v)
            g_m2 = s_v.n
            b.wait("scalar", s_v, g_m2)
            b.run("scalar", lambda e: e.activation(
                grow[0:1, 2:3], grow[0:1, 0:1], AF.Sqrt), s_s)
            g_sq = s_s.n
            b.wait("vector", s_s, g_sq)
            b.run("vector", lambda e: e.tensor_scalar(
                grow[0:1, 2:3], grow[0:1, 2:3], -128.0, None, ALU.mult), s_v)
            nb_tgt = s_v.n
            b.wait("tensor", s_v, nb_tgt)
            b.run("tensor", lambda e: e.matmul(
                ps_m[:, 0:1], ones_f[0:1, :], grow[0:1, 2:3],
                start=True, stop=True), s_pe)
            bc_all = s_pe.n
            b.wait("scalar", s_pe, bc_all)
            b.run("scalar", lambda e: e.copy(nbcol[:], ps_m[:, 0:1]), s_s)
            nbcol_tgt = s_s.n

            pe_base = s_pe.n
            import os
            if os.environ.get("KDBG"):
                print("P4 pe_base", pe_base, "s_base", s_s.n, "v_base", s_v.n)
            per_nh = 19
            def sc_pair_inc(h, nh, mcg):
                return pe_base + (h * 2 + nh) * per_nh + mcg + 1
            def sums_inc(h, nh):
                return pe_base + (h * 2 + nh) * per_nh + 17
            def pv_inc(h, nh):
                return pe_base + (h * 2 + nh) * per_nh + 18
            def bc2_inc(h, nh):
                return pe_base + (h * 2 + nh) * per_nh + 19
            s_base = s_s.n
            def exp_inc(h, nh, mcg):
                return s_base + (h * 2 + nh) * 18 + mcg + 1
            def srow_cp_inc(h, nh):
                return s_base + (h * 2 + nh) * 18 + 17
            def rec_inc(h, nh):
                return s_base + (h * 2 + nh) * 18 + 18
            v_base = s_v.n
            def rr2_inc(h, nh):
                return v_base + (h * 2 + nh) * 3 + 1
            def aot_inc(h, nh):
                return v_base + (h * 2 + nh) * 3 + 2
            def aow_inc(h, nh):
                return v_base + (h * 2 + nh) * 3 + 3
            ldk_base = [ldk[0].n, ldk[1].n]
            ldv_base = [ldv[0].n, ldv[1].n]
            ldq_base = [ldq[0].n, ldq[1].n]

            b.wait("sync", s_cc, s_cc.n)
            b.wait("sync", stk[0], q_st_final[0])
            b.wait("sync", stk[1], q_st_final[1])
            for h in range(H):
                if h >= 2:
                    b.wait("sync", s_pe, pv_inc(h - 2, 1))
                b.dma("sync", khb[h % 2][:].rearrange("p (r t) -> p r t", r=4),
                      cc_r[:, h * 128:(h + 1) * 128, :].transpose([1, 0, 2]),
                      ldk[h % 2])
                for r in range(4):
                    off = (h // 8) * 1024 + (h % 8) * 128
                    vsrc = ccv_out[r * D:(r + 1) * D, :].rearrange(
                        "(tt t f) c -> tt t (f c)", tt=TC, t=128)[
                        :, :, off:off + 128].transpose([1, 0, 2])
                    b.dma("sync",
                          vhb[h % 2][:, r * 1024:(r + 1) * 1024].rearrange(
                              "p (tt d) -> p tt d", tt=TC),
                          vsrc, ldv[h % 2])
                b.dma("sync", qhb[h % 2][:], q_dram[h * 128:(h + 1) * 128, :],
                      ldq[h % 2])

            for h in range(H):
                b.wait("tensor", ldk[h % 2], ldk_base[h % 2] + 16 * (h // 2 + 1))
                b.wait("tensor", ldv[h % 2], ldv_base[h % 2] + 64 * (h // 2 + 1))
                b.wait("tensor", ldq[h % 2], ldq_base[h % 2] + 16 * (h // 2 + 1))
                for nh in range(2):
                    for mcg in range(16):
                        reg = mcg % 2
                        gidx = (h * 2 + nh) * 16 + mcg
                        if gidx >= 2:
                            pm = gidx - 2
                            hh, rem = divmod(pm, 32)
                            nn, mm = divmod(rem, 16)
                            b.wait("tensor", s_s, exp_inc(hh, nn, mm))
                        for j in range(2):
                            mc = mcg * 2 + j
                            b.run("tensor",
                                  lambda e, hh=h, n_=nh, m_=mc, r=reg, jj=j:
                                  e.matmul(ps_sc[:, r * 1024 + jj * 512:
                                                 r * 1024 + (jj + 1) * 512],
                                           khb[hh % 2][:, m_ * 128:(m_ + 1) * 128],
                                           qhb[hh % 2][:, n_ * 512:(n_ + 1) * 512],
                                           start=True, stop=True),
                                  s_pe if j == 1 else None)
                    b.wait("tensor", s_s, exp_inc(h, nh, 15))
                    if h * 2 + nh >= 1:
                        pi = h * 2 + nh - 1
                        b.wait("tensor", s_s, srow_cp_inc(pi // 2, pi % 2))
                    for mc in range(DC):
                        b.run("tensor", lambda e, m_=mc, pb=(h * 2 + nh) % 2:
                              e.matmul(
                                  ps_sum[0:1, :], ones_b[:, 0:1],
                                  probsT[pb][:, m_ * 512:(m_ + 1) * 512],
                                  start=(m_ == 0), stop=(m_ == DC - 1)),
                              s_pe if mc == DC - 1 else None)
                    if h * 2 + nh >= 2:
                        pi = h * 2 + nh - 2
                        b.wait("tensor", s_v, aot_inc(pi // 2, pi % 2))
                    for mc in range(DC):
                        b.run("tensor", lambda e, hh=h, n_=nh, m_=mc,
                              pb=(h * 2 + nh) % 2: e.matmul(
                                  ps_pv[:, (n_ % 2) * 512:(n_ % 2 + 1) * 512],
                                  vhb[hh % 2][:, m_ * 128:(m_ + 1) * 128],
                                  probsT[pb][:, m_ * 512:(m_ + 1) * 512],
                                  start=(m_ == 0), stop=(m_ == DC - 1)),
                              s_pe if mc == DC - 1 else None)
                    b.wait("tensor", s_v, rr2_inc(h, nh))
                    if h * 2 + nh >= 1:
                        pi = h * 2 + nh - 1
                        b.wait("tensor", s_s, rec_inc(pi // 2, pi % 2))
                    else:
                        b.wait("tensor", s_s, nbcol_tgt)
                    b.run("tensor", lambda e: e.matmul(
                        ps_m[:], ones_f[0:1, :], rrow2[0:1, :],
                        start=True, stop=True), s_pe)

                for nh in range(2):
                    for mcg in range(16):
                        b.wait("scalar", s_pe, sc_pair_inc(h, nh, mcg))
                        if mcg == 0:
                            u = h * 2 + nh
                            if u >= 2:
                                pu = u - 2
                                b.wait("scalar", s_pe,
                                       pv_inc(pu // 2, pu % 2))
                            else:
                                b.wait("scalar", s_s, nbcol_tgt)
                        b.run("scalar", lambda e, m_=mcg, pb=(h * 2 + nh) % 2:
                              e.activation(
                                  probsT[pb][:, m_ * 1024:(m_ + 1) * 1024],
                                  ps_sc[:, (m_ % 2) * 1024:(m_ % 2 + 1) * 1024],
                                  AF.Exp, bias=nbcol[:, 0:1]), s_s)
                    b.wait("scalar", s_pe, sums_inc(h, nh))
                    if h * 2 + nh >= 1:
                        pi = h * 2 + nh - 1
                        b.wait("scalar", s_v, rr2_inc(pi // 2, pi % 2))
                    b.run("scalar", lambda e: e.copy(srow2[0:1, :],
                                                     ps_sum[0:1, :]), s_s)
                    b.wait("scalar", s_pe, bc2_inc(h, nh))
                    b.run("scalar", lambda e, n_=nh: e.copy(recb[n_][:],
                                                            ps_m[:]), s_s)

                for nh in range(2):
                    b.wait("vector", s_s, srow_cp_inc(h, nh))
                    if h * 2 + nh >= 1:
                        pi = h * 2 + nh - 1
                        b.wait("vector", s_pe, bc2_inc(pi // 2, pi % 2))
                    b.run("vector", lambda e: e.reciprocal(rrow2[0:1, :],
                                                           srow2[0:1, :]), s_v)
                    b.wait("vector", s_pe, pv_inc(h, nh))
                    b.wait("vector", s_s, rec_inc(h, nh))
                    b.run("vector", lambda e, n_=nh: e.tensor_tensor(
                        tta[n_][:], ps_pv[:, (n_ % 2) * 512:(n_ % 2 + 1) * 512],
                        recb[n_][:], ALU.mult), s_v)
                    b.run("vector", lambda e, hh=h, n_=nh: e.tensor_scalar(
                        ao[:, hh * 1024 + n_ * 512:hh * 1024 + (n_ + 1) * 512],
                        tta[n_][:], bv_sb[:, hh:hh + 1], None, ALU.add), s_v)
        b.barrier()

        with ExitStack() as sB2:
            e5 = sB2.enter_context
            wob = [e5(nc.sbuf_tensor(f"wob{i}", [128, DC * 512], BF16))
                   for i in range(2)]
            opan = [e5(nc.sbuf_tensor(f"opan{i}", [128, 4096], BF16))
                    for i in range(2)]
            xsl = [e5(nc.sbuf_tensor(f"xsl{i}", [128, 512], F32))
                   for i in range(4)]
            x1p = [e5(nc.sbuf_tensor(f"x1p{i}", [128, 512], F32))
                   for i in range(4)]
            ps_o = e5(nc.psum_tensor("ps_o", [128, 1024], F32))
            ps_t5 = e5(nc.psum_tensor("ps_t5", [128, 1024], F32))

            pe5, s5, v5 = s_pe.n, s_s.n, s_v.n
            ld5 = [s.n for s in ldx4]
            ldg5 = [ldw[0].n, ldw[1].n]
            st5 = [s.n for s in stx1]
            def o_proj_inc(p, i8):
                return pe5 + p * 16 + i8 + 1
            def o_tq_inc(p, tau):
                return pe5 + p * 16 + 8 + tau + 1
            def o_ev_inc(p, i8):
                return v5 + p * 16 + i8 + 1
            def x1_inc(p, tau):
                return v5 + p * 16 + 8 + tau + 1

            wo_re = wo.rearrange("(f p) o -> p f o", p=128)
            for p in range(8):
                if p >= 2:
                    b.wait("gpsimd", s_pe, o_proj_inc(p - 2, 7))
                b.dma("gpsimd", wob[p % 2][:].rearrange("p (f o) -> p f o", f=DC),
                      wo_re[:, :, p * 512:(p + 1) * 512], ldw[p % 2])
            for p in range(8):
                for tau in range(8):
                    gi = p * 8 + tau
                    if gi >= 4:
                        pj = gi - 4
                        b.wait("sync", s_v, x1_inc(pj // 8, pj % 8))
                    b.dma("sync", xsl[gi % 4][:],
                          x_in[tau * 128:(tau + 1) * 128,
                               p * 512:(p + 1) * 512], ldx4[gi % 4])
            for p in range(8):
                b.wait("tensor", ldw[p % 2], ldg5[p % 2] + 16 * (p // 2 + 1))
                for i8 in range(8):
                    oc, th = divmod(i8, 2)
                    gi8 = p * 8 + i8
                    if gi8 >= 2:
                        pj = gi8 - 2
                        b.wait("tensor", s_v, o_ev_inc(pj // 8, pj % 8))
                    for hc in range(DC):
                        b.run("tensor", lambda e, pp=p, o=oc, tt=th, c=hc, r=i8 % 2:
                              e.matmul(ps_o[:, r * 512:(r + 1) * 512],
                                       wob[pp % 2][:, c * 512 + o * 128:
                                                   c * 512 + (o + 1) * 128],
                                       ao[:, c * 1024 + tt * 512:
                                          c * 1024 + (tt + 1) * 512],
                                       start=(c == 0), stop=(c == DC - 1)),
                              s_pe if hc == DC - 1 else None)
                b.wait("tensor", s_v, o_ev_inc(p, 7))
                for tau in range(8):
                    gq_i = p * 8 + tau
                    if gq_i >= 2:
                        pj = gq_i - 2
                        b.wait("tensor", s_v, x1_inc(pj // 8, pj % 8))
                    for oc in range(4):
                        b.run("tensor", lambda e, pp=p, t_=tau, o=oc: e.matmul(
                            ps_t5[:, (t_ % 2) * 512 + o * 128:
                                  (t_ % 2) * 512 + (o + 1) * 128],
                            opan[pp % 2][:, o * 1024 + t_ * 128:
                                         o * 1024 + (t_ + 1) * 128],
                            ident[:], start=(o == 0), stop=(o == 3)),
                            s_pe if oc == 3 else None)
            for p in range(8):
                for i8 in range(8):
                    oc, th = divmod(i8, 2)
                    b.wait("vector", s_pe, o_proj_inc(p, i8))
                    if p >= 2 and i8 == 0:
                        b.wait("vector", s_pe, o_tq_inc(p - 2, 7))
                    b.run("vector", lambda e, pp=p, o=oc, tt=th, r=i8 % 2:
                          e.tensor_scalar(
                              opan[pp % 2][:, o * 1024 + tt * 512:
                                           o * 1024 + (tt + 1) * 512],
                              ps_o[:, r * 512:(r + 1) * 512],
                              bo_sb[:, pp * 4 + o:pp * 4 + o + 1], None,
                              ALU.add), s_v)
                for tau in range(8):
                    gi = p * 8 + tau
                    b.wait("vector", s_pe, o_tq_inc(p, tau))
                    b.wait("vector", ldx4[gi % 4], ld5[gi % 4] + 16 * (gi // 4 + 1))
                    if gi >= 4:
                        b.wait("vector", stx1[gi % 4], st5[gi % 4] + 16 * (gi // 4))
                    b.run("vector", lambda e, t_=tau, g=gi: e.tensor_tensor(
                        x1p[g % 4][:],
                        ps_t5[:, (t_ % 2) * 512:((t_ % 2) + 1) * 512],
                        xsl[g % 4][:], ALU.add), s_v)
            for p in range(8):
                for tau in range(8):
                    gi = p * 8 + tau
                    b.wait("scalar", s_v, x1_inc(p, tau))
                    b.dma("scalar",
                          x1_dram[tau * 128:(tau + 1) * 128,
                                  p * 512:(p + 1) * 512],
                          x1p[gi % 4][:], stx1[gi % 4])
        b.barrier()
    x1_st_final = [s.n for s in stx1]

    with ExitStack() as sD:
        eD = sD.enter_context
        xn2T = eD(nc.sbuf_tensor("xn2T", [128, DC * T], BF16))

        with ExitStack() as sD1:
            e6 = sD1.enter_context
            sc2 = dict(
                xt=(e6(nc.sbuf_tensor("xt0b", [128, D], F32)),
                    e6(nc.sbuf_tensor("xt1b", [128, D], F32))),
                xsq=e6(nc.sbuf_tensor("xsqb", [128, D], F32)),
                xnb=(e6(nc.sbuf_tensor("xnb0b", [128, D], BF16)),
                     e6(nc.sbuf_tensor("xnb1b", [128, D], BF16))),
                cols=e6(nc.sbuf_tensor("colsb", [128, 8], F32)),
                ps=e6(nc.psum_tensor("ps_a2", [128, 1024], F32)))
            ln_transpose(x1_dram, xn2T, sc2,
                         pre_wait=list(zip(stx1, x1_st_final)))
        b.barrier()

        if KT == "3":
            b.dma("sync", y_out, x1_dram, sty4[0])
            b.wait("sync", sty4[0], sty4[0].n)
            b.barrier(drain=False)

        if KT != "3":
            mTg_sb = eD(nc.sbuf_tensor("mTg", [32, II], BF16))
            mTu_sb = eD(nc.sbuf_tensor("mTu", [32, II], BF16))
            srg = [eD(nc.sbuf_tensor(f"srg{i}", [1, D], BF16)) for i in range(2)]
            sru = [eD(nc.sbuf_tensor(f"sru{i}", [1, D], BF16)) for i in range(2)]
            Sx_sb = eD(nc.sbuf_tensor("Sx_sb", [32, T], BF16))

            b.dma("gpsimd", srg[0][:], rg_in[0:1, :], ldrg[0])
            b.dma("gpsimd", sru[0][:], ru_in[0:1, :], ldru[0])
            b.dma("gpsimd", srg[1][:], rg_in[1:2, :], ldrg[1])
            b.dma("gpsimd", sru[1][:], ru_in[1:2, :], ldru[1])

            IH = II // 4
            with ExitStack() as sMT:
                eM = sMT.enter_context
                szf = [eM(nc.sbuf_tensor(f"szf{i}", [32, IH], F32)) for i in range(4)]
                for hf in range(4):
                    sl = slice(hf * IH, (hf + 1) * IH)
                    if hf >= 1:
                        b.wait("scalar", s_v, s_v.n)
                        b.wait("scalar", s_gp, s_gp.n)
                    b.dma("scalar", szf[0][:], sgT_in[:, sl], ld_sy)
                    b.dma("scalar", szf[1][:], zgT_in[:, sl], ld_sy)
                    b.dma("scalar", szf[2][:], suT_in[:, sl], ld_sy)
                    b.dma("scalar", szf[3][:], zuT_in[:, sl], ld_sy)
                    tgt = ld_sy.n
                    b.wait("vector", ld_sy, tgt)
                    b.run("vector", lambda e: e.tensor_tensor(
                        szf[1][:], szf[1][:], szf[0][:], ALU.mult), s_v)
                    b.run("vector", lambda e, s=sl: e.tensor_scalar(
                        mTg_sb[:, s], szf[1][:], -1.0, None, ALU.mult), s_v)
                    b.wait("gpsimd", ld_sy, tgt)
                    b.run("gpsimd", lambda e: e.tensor_tensor(
                        szf[3][:], szf[3][:], szf[2][:], ALU.mult), s_gp)
                    b.run("gpsimd", lambda e, s=sl: e.tensor_scalar(
                        mTu_sb[:, s], szf[3][:], -1.0, None, ALU.mult), s_gp)
                mtg_done = s_v.n
                mtu_done = s_gp.n

            xn2T_v = xn2T[:].rearrange("p (f t) -> p f t", f=DC)
            with ExitStack() as sSX:
                ps_sx = sSX.enter_context(nc.psum_tensor("ps_sx", [128, 1024], F32))
                for th in range(2):
                    for g in range(DC):
                        b.run("tensor", lambda e, g_=g, tt=th: e.matmul(
                            ps_sx[0:32, tt * 512:(tt + 1) * 512],
                            csel_sb[:, 127 - g_:159 - g_],
                            xn2T_v[:, g_, tt * 512:(tt + 1) * 512],
                            start=(g_ == 0), stop=(g_ == DC - 1)),
                            s_pe if g == DC - 1 else None)
                sx_done = s_pe.n
                b.wait("scalar", s_pe, sx_done)
                b.run("scalar", lambda e: e.copy(Sx_sb[:], ps_sx[0:32, :]), s_s)
                sx_cp = s_s.n
        if KT == "2":
            b.wait("sync", s_s, sx_cp)
            b.dma("sync", y_out, x1_dram, sty4[0])
            b.wait("sync", sty4[0], sty4[0].n)
            b.barrier(drain=False)

        if KT not in ("2", "3"):
            qgT = [eD(nc.sbuf_tensor(f"qgT{i}", [128, D], U8)) for i in range(2)]
            quT = [eD(nc.sbuf_tensor(f"quT{i}", [128, D], U8)) for i in range(2)]
            dqgp = [eD(nc.sbuf_tensor(f"dqgp{i}", [128, D], BF16)) for i in range(2)]
            dqup = [eD(nc.sbuf_tensor(f"dqup{i}", [128, D], BF16)) for i in range(2)]
            gsb = [eD(nc.sbuf_tensor(f"gsb{i}", [128, 512], BF16)) for i in range(2)]
            usb = [eD(nc.sbuf_tensor(f"usb{i}", [128, 512], BF16)) for i in range(2)]
            hac = [eD(nc.sbuf_tensor(f"hac{i}", [128, 1024], BF16)) for i in range(2)]
            ps_g = eD(nc.psum_tensor("ps_g", [128, 1024], F32))
            ps_u = eD(nc.psum_tensor("ps_u", [128, 1024], F32))
            ps_bg = eD(nc.psum_tensor("ps_bg", [128, 1024], F32))
            ps_bu = eD(nc.psum_tensor("ps_bu", [128, 1024], F32))

            v6, g6, s6 = s_v.n, s_gp.n, s_s.n

            def vq_done(w, ic, g):
                pos = (g // 4) * 8 + (g % 4) + 1 + (0 if w == 'g' else 4)
                return v6 + ic * 64 + pos

            def hac_done(ic, th):
                return g6 + 2 * ic + th + 1

            def silu_done(ic, th):
                return s6 + ic * 4 + 2 * th + 1

            def uev_done(ic, th):
                return s6 + ic * 4 + 2 * th + 2

            bcg_val = [[0] * 8 for _ in range(ICN)]
            bcu_val = [[0] * 8 for _ in range(ICN)]
            gch_val = [[0, 0] for _ in range(ICN)]
            uch_val = [[0, 0] for _ in range(ICN)]

            def emit_bcasts(ic):
                for j in range(8):
                    k = ic * 8 + j
                    if k >= 2:
                        pic, pj = divmod(k - 2, 8)
                        b.wait("tensor", s_v, vq_done('g', pic, 4 * pj + 3))
                    if j == 0:
                        b.wait("tensor", ldrg[ic % 2], 16 * (ic // 2 + 1))
                        if ic == 0:
                            b.wait("tensor", s_s, sx_cp)
                    b.run("tensor", lambda e, i=ic, jj=j, r=k % 2: e.matmul(
                        ps_bg[:, r * 512:(r + 1) * 512], ones_b[0:1, :],
                        srg[i % 2][0:1, jj * 512:(jj + 1) * 512],
                        start=True, stop=True), s_pe)
                    bcg_val[ic][j] = s_pe.n
                    if k >= 2:
                        pic, pj = divmod(k - 2, 8)
                        b.wait("tensor", s_v, vq_done('u', pic, 4 * pj + 3))
                    if j == 0:
                        b.wait("tensor", ldru[ic % 2], 16 * (ic // 2 + 1))
                    b.run("tensor", lambda e, i=ic, jj=j, r=k % 2: e.matmul(
                        ps_bu[:, r * 512:(r + 1) * 512], ones_b[0:1, :],
                        sru[i % 2][0:1, jj * 512:(jj + 1) * 512],
                        start=True, stop=True), s_pe)
                    bcu_val[ic][j] = s_pe.n

            def emit_chain(w, ic, th):
                ps = ps_g if w == 'g' else ps_u
                mT = mTg_sb if w == 'g' else mTu_sb
                dqp = dqgp if w == 'g' else dqup
                if ic >= 1:
                    b.wait("tensor", s_s,
                           silu_done(ic - 1, th) if w == 'g' else uev_done(ic - 1, th))
                elif th == 0:
                    b.wait("tensor", s_v if w == 'g' else s_gp,
                           mtg_done if w == 'g' else mtu_done)
                b.run("tensor", lambda e, i=ic, tt=th, m=mT, p=ps: e.matmul(
                    p[:, tt * 512:(tt + 1) * 512], m[0:32, i * 128:(i + 1) * 128],
                    Sx_sb[0:32, tt * 512:(tt + 1) * 512],
                    start=True, stop=False))
                for g in range(DC):
                    if th == 0 and g % 4 == 0:
                        b.wait("tensor", s_v, vq_done(w, ic, g + 3))
                    b.run("tensor", lambda e, i=ic, tt=th, g_=g, p=ps, d=dqp:
                          e.matmul(p[:, tt * 512:(tt + 1) * 512],
                                   d[i % 2][:, g_ * 128:(g_ + 1) * 128],
                                   xn2T_v[:, g_, tt * 512:(tt + 1) * 512],
                                   start=False, stop=(g_ == DC - 1)),
                          s_pe if g == DC - 1 else None)
                (gch_val if w == 'g' else uch_val)[ic][th] = s_pe.n

            for ic in range(ICN):
                if ic == 0:
                    emit_bcasts(0)
                emit_chain('g', ic, 0)
                if ic + 1 < ICN:
                    emit_bcasts(ic + 1)
                emit_chain('g', ic, 1)
                emit_chain('u', ic, 0)
                emit_chain('u', ic, 1)

            def dq_op(w, ic, g):
                dqp, qT, psb = ((dqgp, qgT, ps_bg) if w == 'g'
                                else (dqup, quT, ps_bu))
                if g % 4 == 0:
                    b.wait("vector", s_pe,
                           (bcg_val if w == 'g' else bcu_val)[ic][g // 4])
                k = ic * 8 + g // 4
                b.run("vector", lambda e, i=ic, g_=g, r=k % 2, d=dqp, q=qT, p=psb:
                      e.tensor_tensor(
                          d[i % 2][:, g_ * 128:(g_ + 1) * 128],
                          q[i % 2][:, g_ * 128:(g_ + 1) * 128],
                          p[:, r * 512 + (g_ % 4) * 128:
                            r * 512 + (g_ % 4 + 1) * 128], ALU.mult), s_v)

            for ic in range(ICN):
                b.wait("vector", ldqg[ic % 2], 16 * (ic // 2 + 1))
                b.wait("vector", ldqu[ic % 2], 16 * (ic // 2 + 1))
                if ic >= 2:
                    b.wait("vector", s_pe, gch_val[ic - 2][1])
                    b.wait("vector", s_pe, uch_val[ic - 2][1])
                for j in range(8):
                    for g in range(4 * j, 4 * j + 4):
                        dq_op('g', ic, g)
                    for g in range(4 * j, 4 * j + 4):
                        dq_op('u', ic, g)

            for ic in range(ICN):
                if ic + 2 < ICN:
                    b.wait("gpsimd", s_pe, bcu_val[ic][7])
                    b.dma("gpsimd", srg[(ic + 2) % 2][:],
                          rg_in[ic + 2:ic + 3, :], ldrg[(ic + 2) % 2])
                    b.dma("gpsimd", sru[(ic + 2) % 2][:],
                          ru_in[ic + 2:ic + 3, :], ldru[(ic + 2) % 2])
                if ic >= 1:
                    for th in range(2):
                        b.wait("gpsimd", s_s, uev_done(ic - 1, th))
                        if ic >= 3:
                            b.wait("gpsimd", sth[(ic - 1) % 2],
                                   16 * ((ic - 1) // 2))
                        b.run("gpsimd", lambda e, i=ic - 1, tt=th:
                              e.tensor_tensor(
                                  hac[i % 2][:, tt * 512:(tt + 1) * 512],
                                  gsb[tt][:], usb[tt][:], ALU.mult), s_gp)
            for th in range(2):
                b.wait("gpsimd", s_s, uev_done(ICN - 1, th))
                b.wait("gpsimd", sth[(ICN - 1) % 2], 16 * ((ICN - 1) // 2))
                b.run("gpsimd", lambda e, i=ICN - 1, tt=th: e.tensor_tensor(
                    hac[i % 2][:, tt * 512:(tt + 1) * 512],
                    gsb[tt][:], usb[tt][:], ALU.mult), s_gp)

            for ic in range(ICN):
                for th in range(2):
                    b.wait("scalar", s_pe, gch_val[ic][th])
                    if ic >= 1:
                        b.wait("scalar", s_gp, hac_done(ic - 1, th))
                    b.run("scalar", lambda e, tt=th: e.activation(
                        gsb[tt][:], ps_g[:, tt * 512:(tt + 1) * 512], AF.Silu), s_s)
                    b.wait("scalar", s_pe, uch_val[ic][th])
                    b.run("scalar", lambda e, tt=th: e.activation(
                        usb[tt][:], ps_u[:, tt * 512:(tt + 1) * 512], AF.Copy), s_s)
                if ic >= 1:
                    b.dma("scalar", hact_dram[(ic - 1) * 128:ic * 128, :],
                          hac[(ic - 1) % 2][:], sth[(ic - 1) % 2])
            b.wait("scalar", s_gp, s_gp.n)
            b.dma("scalar", hact_dram[(ICN - 1) * 128:ICN * 128, :],
                  hac[(ICN - 1) % 2][:], sth[(ICN - 1) % 2])

            b.wait("sync", s_v, mtg_done)
            b.wait("sync", s_gp, mtu_done)
            for ic in range(ICN):
                if ic >= 2:
                    b.wait("sync", s_v, vq_done('g', ic - 2, DC - 1))
                    b.wait("sync", s_v, vq_done('u', ic - 2, DC - 1))
                b.dma("sync", qgT[ic % 2][:], pg_in[ic], ldqg[ic % 2])
                b.dma("sync", quT[ic % 2][:], pu_in[ic], ldqu[ic % 2])
            b.barrier()

    if KT:
        if KT == "1":
            b.dma("sync", y_out, x1_dram, sty4[0])
            b.wait("sync", sty4[0], sty4[0].n)
            b.barrier(drain=False)
        b.finalize_block()
        b.ctx.close()
        return nc

    with ExitStack() as sF:
        eF = sF.enter_context
        szd = {}
        for nm, src in (("sd", sd_in), ("zd", zd_in)):
            szd[nm] = eF(nc.sbuf_tensor(nm + "_sb", [128, 32 * ICN], F32))
            b.dma("sync", szd[nm][:], src, ld_sy)
        b.wait("vector", ld_sy)
        b.run("vector", lambda e: e.tensor_tensor(
            szd["zd"][:], szd["zd"][:], szd["sd"][:], ALU.mult), s_v)
        b.run("vector", lambda e: e.tensor_scalar(
            szd["zd"][:], szd["zd"][:], -1.0, None, ALU.mult), s_v)
        nzd_tgt = s_v.n

        qd_sb = [eF(nc.sbuf_tensor(f"qd{i}", [128, II], I32)) for i in range(2)]
        dqd = [eF(nc.sbuf_tensor(f"dqd{i}", [128, 128], BF16)) for i in range(8)]
        wdc = [eF(nc.sbuf_tensor(f"wdc{i}", [128, II], BF16)) for i in range(2)]
        ps_w7 = eF(nc.psum_tensor("ps_w7", [128, 2048], F32))

        groups = [list(range(g * 4, min(g * 4 + 4, ICN))) for g in range(22)]
        v_ics = [ic for ic in range(ICN) if ic % 4 != 3]
        p_ics = [ic for ic in range(ICN) if ic % 4 == 3]
        v_ord = {ic: i for i, ic in enumerate(v_ics)}
        p_ord = {ic: i for i, ic in enumerate(p_ics)}
        NV, NP = len(v_ics), len(p_ics)
        pe7 = s_pe.n
        def w7q_inc(hc, g):
            return pe7 + hc * 22 + g + 1
        s7 = s_s.n
        def w7e_inc(hc, e):
            return s7 + hc * 11 + e + 1
        v7 = s_v.n
        g7 = s_gp.n
        def dq7_sem_val(hc, ic):
            if ic % 4 == 3:
                return s_gp, g7 + hc * NP + p_ord[ic] + 1
            return s_v, v7 + hc * NV + v_ord[ic] + 1
        ld7 = [ldqd[0].n, ldqd[1].n]
        st7 = [stw[0].n, stw[1].n]

        for hc in range(32):
            if hc >= 2:
                b.wait("sync", s_v, v7 + (hc - 1) * NV)
                b.wait("sync", s_gp, g7 + (hc - 1) * NP)
            b.dma("sync", qd_sb[hc % 2][:], qwd[hc * 128:(hc + 1) * 128, :],
                  ldqd[hc % 2])
            if hc >= 1:
                b.wait("scalar", s_s, w7e_inc(hc - 1, 10))
                b.dma("scalar", wdt_dram[hc - 1, :, :], wdc[(hc - 1) % 2][:],
                      stw[(hc - 1) % 2])
            for eng, ics_e in (("vector", v_ics), ("gpsimd", p_ics)):
                b.wait(eng, ldqd[hc % 2], ld7[hc % 2] + 16 * (hc // 2 + 1))
                if hc == 0:
                    b.wait(eng, s_v, nzd_tgt)
                sem = s_v if eng == "vector" else s_gp
                for ic in ics_e:
                    if ic >= 8:
                        b.wait(eng, s_pe, w7q_inc(hc, (ic - 8) // 4))
                    elif hc >= 1:
                        b.wait(eng, s_pe, w7q_inc(hc - 1, 21))
                    b.run(eng, lambda e, h_=hc, i=ic: e.tensor_scalar(
                        dqd[i % 8][:], qd_sb[h_ % 2][:, i * 128:(i + 1) * 128],
                        szd["sd"][:, h_ * ICN + i:h_ * ICN + i + 1],
                        szd["zd"][:, h_ * ICN + i:h_ * ICN + i + 1],
                        ALU.mult, ALU.add), sem)
            for g, ics in enumerate(groups):
                G = hc * 22 + g
                if G >= 4:
                    pj = G - 4
                    b.wait("tensor", s_s, w7e_inc(pj // 22, (pj % 22) // 2))
                for j, ic in enumerate(ics):
                    dsem, dval = dq7_sem_val(hc, ic)
                    b.wait("tensor", dsem, dval)
                    b.run("tensor", lambda e, i=ic, jj=j, r=G % 4, n_=len(ics):
                          e.matmul(ps_w7[:, r * 512 + jj * 128:
                                         r * 512 + (jj + 1) * 128],
                                   dqd[i % 8][:], ident[:],
                                   start=(jj == 0), stop=(jj == n_ - 1)),
                          s_pe if j == len(ics) - 1 else None)
                if g % 2 == 1:
                    e_ = g // 2
                    b.wait("scalar", s_pe, w7q_inc(hc, g))
                    if e_ == 0 and hc >= 2:
                        b.wait("scalar", stw[hc % 2],
                               st7[hc % 2] + 16 * (hc // 2))
                    g0 = 2 * e_
                    width = (len(groups[g0]) + len(groups[g0 + 1])) * 128
                    r0 = (hc * 22 + g0) % 4
                    b.run("scalar", lambda e, h_=hc, c0=g0 * 512, w_=width,
                          rr=r0: e.copy(
                              wdc[h_ % 2][:, c0:c0 + w_],
                              ps_w7[:, rr * 512:rr * 512 + w_]), s_s)
        b.wait("scalar", s_s, s_s.n)
        b.dma("scalar", wdt_dram[31, :, :], wdc[1][:], stw[1])
        b.barrier()
    wdt_st_final = [stw[0].n, stw[1].n]

    with ExitStack() as sG:
        eG = sG.enter_context
        wdts = [eG(nc.sbuf_tensor(f"wdts{i}", [128, 8 * 512], BF16))
                for i in range(2)]
        hsl = [eG(nc.sbuf_tensor(f"hsl{i}", [128, 4 * 512], BF16))
               for i in range(2)]
        dnsb = eG(nc.sbuf_tensor("dnsb", [128, 8 * 512], BF16))
        x1s = [eG(nc.sbuf_tensor(f"x1s{i}", [128, 1024], F32)) for i in range(4)]
        ypc = [eG(nc.sbuf_tensor(f"ypc{i}", [128, 1024], F32)) for i in range(4)]
        ps_d = eG(nc.psum_tensor("ps_d", [128, 8 * 512], F32))

        hact_r = hact_dram.rearrange("(ic p) t -> ic p t", p=128)
        icbs = [list(range(g * 4, min(g * 4 + 4, ICN))) for g in range(22)]
        pe8 = s_pe.n
        def b_inc(pss, bi):
            return pe8 + pss * 30 + bi + 1
        def tq8_inc(pss, tau, half):
            return pe8 + pss * 30 + 22 + tau * 2 + half + 1
        s8 = s_s.n
        def dne_inc(pss, hc8):
            return s8 + pss * 8 + hc8 + 1
        v8 = s_v.n
        def y_inc(pss, tau):
            return v8 + pss * 8 + 2 * tau + 2
        ld8b = [ldb[0].n, ldb[1].n]
        ld8x = ldx1b.n

        b.wait("sync", stw[0], wdt_st_final[0])
        b.wait("sync", stw[1], wdt_st_final[1])
        b.wait("sync", sth[0], sth[0].n)
        b.wait("sync", sth[1], sth[1].n)
        for s_, v_ in zip(stx1, x1_st_final):
            b.wait("sync", s_, v_)
        for pss in range(8):
            th, grp = divmod(pss, 4)
            for bi, ics in enumerate(icbs):
                gi = pss * 22 + bi
                if gi >= 2:
                    pj = gi - 2
                    b.wait("sync", s_pe, b_inc(pj // 22, pj % 22))
                w = len(ics)
                b.dma("sync",
                      wdts[bi % 2][:, 0:8 * w * 128].rearrange(
                          "p (c i) -> p c i", c=8),
                      wdt_dram[grp * 8:(grp + 1) * 8, :,
                               ics[0] * 128:ics[0] * 128 + w * 128]
                      .transpose([1, 0, 2]), ldb[bi % 2])
                b.dma("sync",
                      hsl[bi % 2][:, 0:w * 512].rearrange(
                          "p (i t) -> p i t", i=w),
                      hact_r[ics[0]:ics[0] + w, :, th * 512:(th + 1) * 512]
                      .transpose([1, 0, 2]), ldb[bi % 2])
            for tau in range(4):
                if pss >= 1:
                    b.wait("sync", s_v, y_inc(pss - 1, tau))
                b.dma("sync", x1s[tau][:],
                      x1_dram[th * 512 + tau * 128:th * 512 + (tau + 1) * 128,
                              grp * 1024:(grp + 1) * 1024], ldx1b)
            for bi, ics in enumerate(icbs):
                gb = pss * 22 + bi
                b.wait("tensor", ldb[bi % 2], ld8b[bi % 2] + 32 * (gb // 2 + 1))
                if bi == 0 and pss >= 1:
                    b.wait("tensor", s_v, y_inc(pss - 1, 3))
                w = len(ics)
                for j, ic in enumerate(ics):
                    for hc8 in range(8):
                        b.run("tensor", lambda e, b_=bi, jj=j, c8=hc8, i=ic,
                              w_=w:
                              e.matmul(ps_d[:, c8 * 512:(c8 + 1) * 512],
                                       wdts[b_ % 2][:, c8 * w_ * 128 + jj * 128:
                                                    c8 * w_ * 128 + (jj + 1) * 128],
                                       hsl[b_ % 2][:, jj * 512:(jj + 1) * 512],
                                       start=(i == 0), stop=(i == ICN - 1)),
                              s_pe if (j == w - 1 and hc8 == 7) else None)
            for hc8 in range(8):
                b.wait("scalar", s_pe, b_inc(pss, 21))
                b.run("scalar", lambda e, c8=hc8: e.copy(
                    dnsb[:, c8 * 512:(c8 + 1) * 512],
                    ps_d[:, c8 * 512:(c8 + 1) * 512]), s_s)
            for tau in range(4):
                for half in range(2):
                    b.wait("tensor", s_s, dne_inc(pss, 7))
                    if tau * 2 + half >= 2:
                        b.wait("tensor", s_v, y_inc(pss, tau - 1))
                    for j in range(4):
                        hc8 = half * 4 + j
                        b.run("tensor", lambda e, c8=hc8, jj=j, hf=half, t_=tau:
                              e.matmul(ps_d[:, hf * 512 + jj * 128:
                                            hf * 512 + (jj + 1) * 128],
                                       dnsb[:, c8 * 512 + t_ * 128:
                                            c8 * 512 + (t_ + 1) * 128],
                                       ident[:], start=(jj == 0), stop=(jj == 3)),
                              s_pe if j == 3 else None)
                b.wait("vector", s_pe, tq8_inc(pss, tau, 1))
                b.wait("vector", ldx1b, ld8x + 64 * (pss + 1))
                if pss >= 1:
                    b.wait("vector", sty4[tau], 16 * pss)
                for half in range(2):
                    b.run("vector", lambda e, t_=tau, hf=half: e.tensor_tensor(
                        ypc[t_][:, hf * 512:(hf + 1) * 512],
                        ps_d[:, hf * 512:(hf + 1) * 512],
                        x1s[t_][:, hf * 512:(hf + 1) * 512], ALU.add), s_v)
                b.wait("scalar", s_v, y_inc(pss, tau))
                b.dma("scalar",
                      y_out[th * 512 + tau * 128:th * 512 + (tau + 1) * 128,
                            grp * 1024:(grp + 1) * 1024],
                      ypc[tau][:], sty4[tau])
        for s_ in sty4:
            b.wait("sync", s_, s_.n)
        b.barrier(drain=False)

    b.finalize_block()
    b.ctx.close()
    return nc


_NC_CACHE = None


def kernel(**inputs):
    global _NC_CACHE
    xs = np.ascontiguousarray(np.asarray(inputs["L0_attn_hidden_states"]))
    ident = np.eye(128, dtype=ml_dtypes.bfloat16)
    ones_b = np.ones((128, 128), dtype=ml_dtypes.bfloat16)
    ones_f = np.ones((128, 128), dtype=np.float32)

    def pcol(v):
        return np.ascontiguousarray(np.asarray(v, np.float32).reshape(32, 128).T)

    def prow_d(v):
        return np.ascontiguousarray(
            np.asarray(v, np.float32).reshape(32, 128, ICN)
            .transpose(1, 0, 2).reshape(128, -1))

    def panels_gu(qw):
        q = np.asarray(qw).astype(np.uint8)
        return np.ascontiguousarray(
            q.reshape(ICN, 128, 32, 128).transpose(0, 3, 2, 1))

    def srows_gu(s):
        return np.ascontiguousarray(
            np.asarray(s, np.float32).reshape(ICN, 128, 32)
            .transpose(0, 2, 1).reshape(ICN, D))

    csel = np.zeros((128, 255), dtype=ml_dtypes.bfloat16)
    csel[:, 127] = 1

    common = dict(
        wq=np.ascontiguousarray(np.asarray(inputs["L0_attn_Wq"], np.float32)),
        wk=np.ascontiguousarray(np.asarray(inputs["L0_attn_Wk"], np.float32)),
        wv=np.ascontiguousarray(np.asarray(inputs["L0_attn_Wv"], np.float32)),
        wo=np.ascontiguousarray(np.asarray(inputs["L0_attn_Wo"], np.float32)),
        qwd=np.ascontiguousarray(np.asarray(inputs["L0_ffn_qw_down"], np.int32)),
        pg=panels_gu(inputs["L0_ffn_qw_gate"]),
        pu=panels_gu(inputs["L0_ffn_qw_up"]),
        rg=srows_gu(inputs["L0_ffn_s_gate"]),
        ru=srows_gu(inputs["L0_ffn_s_up"]),
        sgT=np.ascontiguousarray(np.asarray(inputs["L0_ffn_s_gate"], np.float32).T),
        zgT=np.ascontiguousarray(np.asarray(inputs["L0_ffn_z_gate"], np.float32).T),
        suT=np.ascontiguousarray(np.asarray(inputs["L0_ffn_s_up"], np.float32).T),
        zuT=np.ascontiguousarray(np.asarray(inputs["L0_ffn_z_up"], np.float32).T),
        csel=csel,
        sd=prow_d(inputs["L0_ffn_s_down"]), zd=prow_d(inputs["L0_ffn_z_down"]),
        bq_p=pcol(inputs["L0_attn_bq"]), bk_p=pcol(inputs["L0_attn_bk"]),
        bv_p=pcol(inputs["L0_attn_bv"]), bo_p=pcol(inputs["L0_attn_bo"]),
        gq_c=np.ascontiguousarray(
            np.asarray(inputs["L0_attn_gq"], np.float32).reshape(128, 1)),
        gk_c=np.ascontiguousarray(
            np.asarray(inputs["L0_attn_gk"], np.float32).reshape(128, 1)),
        ident_bf=ident, ones_bf=ones_b, ones_f=ones_f,
        ident_f=np.eye(128, dtype=np.float32),
    )
    in_maps = []
    for c in range(8):
        bb, r = divmod(c, 4)
        m = dict(common)
        m["x_in"] = np.ascontiguousarray(xs[bb, r * T:(r + 1) * T, :], np.float32)
        in_maps.append(m)

    if _NC_CACHE is None:
        _NC_CACHE = build_program()
    res = run_bass_kernel_spmd(_NC_CACHE, in_maps, core_ids=list(range(8)))
    out = np.empty((B, N, D), np.float32)
    for c in range(8):
        bb, r = divmod(c, 4)
        out[bb, r * T:(r + 1) * T, :] = res.results[c]["y"]
    return out


# version 11
# speedup vs baseline: 1.1357x; 1.1357x over previous
import math
from contextlib import ExitStack

import numpy as np
import ml_dtypes

import concourse.bass as bass
import concourse.mybir as mybir
from concourse.bass_utils import run_bass_kernel_spmd

F32 = mybir.dt.float32
BF16 = mybir.dt.bfloat16
I32 = mybir.dt.int32
U8 = mybir.dt.uint8
AF = mybir.ActivationFunctionType
ALU = mybir.AluOpType
AX = mybir.AxisListType.X

B, N, D = 2, 4096, 4096
H, HD = 32, 128
II = 11008
T = 1024
TC = T // 128
DC = D // 128
ICN = II // 128
RS = 2 * D
INV_SQRT_HD = 1.0 / math.sqrt(HD)
ENGS = ("sync", "tensor", "vector", "scalar", "gpsimd")
LN_EPS = 1e-5
RMS_EPS = 1e-6


class Sem:
    __slots__ = ("h", "n")

    def __init__(self, h):
        self.h = h
        self.n = 0


class Builder:
    def __init__(self):
        self.nc = bass.Bass("TRN2", target_bir_lowering=False)
        self.q = {e: [] for e in ENGS}
        self.ctx = ExitStack()
        self.bar_k = 0

    def sem(self, name):
        return Sem(self.ctx.enter_context(self.nc.semaphore(name)))

    def semlist(self, name, k):
        return [self.sem(f"{name}{i}") for i in range(k)]

    def sbuf(self, name, shape, dtype):
        return self.ctx.enter_context(self.nc.sbuf_tensor(name, shape, dtype))

    def emit(self, eng, fn):
        self.q[eng].append(fn)

    def wait(self, eng, sem, val=None):
        v = sem.n if val is None else val
        if v > 0:
            self.emit(eng, lambda e, s=sem.h, vv=v: e.wait_ge(s, vv))

    SELF_SYNC = ("vector", "scalar", "gpsimd")

    def run(self, eng, build, sem=None, k=1):
        if eng in self.SELF_SYNC and sem is not None and sem.n > 0:
            self.wait(eng, sem, sem.n)
        if sem is not None:
            sem.n += k
        def f(e, bld=build, s=sem, kk=k):
            ins = bld(e)
            if s is not None:
                ins.then_inc(s.h, kk)
        self.emit(eng, f)
        return sem.n if sem is not None else None

    def dma(self, eng, out, in_, sem, **kw):
        return self.run(eng, lambda e, o=out, i=in_, k=kw: e.dma_start(out=o, in_=i, **k),
                        sem, 16)

    def barrier(self, drain=True):
        self.bar_k += 1
        tgt = 5 * self.bar_k
        for eng in ENGS:
            if drain and eng in ("tensor", "vector", "scalar", "gpsimd"):
                self.emit(eng, lambda e: e.drain())
            self.emit(eng, lambda e, s=self.bar.h: e.sem_inc(s, 1))
            self.emit(eng, lambda e, s=self.bar.h, v=tgt: e.wait_ge(s, v))

    def finalize_block(self):
        nc = self.nc
        with nc.Block() as blk:
            for eng in ENGS:
                fns = self.q[eng]
                getattr(blk, eng)(lambda e, fns=fns: [fn(e) for fn in fns])


def build_program():
    import os as _os
    KT = _os.environ.get("KTRUNC", "")
    b = Builder()
    nc = b.nc

    def din(name, shape, dtype):
        return nc.dram_tensor(name, shape, dtype, kind="ExternalInput").ap()

    x_in = din("x_in", [T, D], F32)
    wq = din("wq", [D, D], F32)
    wk = din("wk", [D, D], F32)
    wv = din("wv", [D, D], F32)
    wo = din("wo", [D, D], F32)
    qwd = din("qwd", [D, II], I32)
    pg_in = din("pg", [ICN, 128, D], U8)
    pu_in = din("pu", [ICN, 128, D], U8)
    rg_in = din("rg", [ICN, D], F32)
    ru_in = din("ru", [ICN, D], F32)
    sgT_in = din("sgT", [32, II], F32)
    zgT_in = din("zgT", [32, II], F32)
    suT_in = din("suT", [32, II], F32)
    zuT_in = din("zuT", [32, II], F32)
    csel_in = din("csel", [128, 255], BF16)
    sd_in = din("sd", [128, 32 * ICN], F32)
    zd_in = din("zd", [128, 32 * ICN], F32)
    bq_in = din("bq_p", [128, 32], F32)
    bk_in = din("bk_p", [128, 32], F32)
    bv_in = din("bv_p", [128, 32], F32)
    bo_in = din("bo_p", [128, 32], F32)
    gq_in = din("gq_c", [128, 1], F32)
    gk_in = din("gk_c", [128, 1], F32)
    id_in = din("ident_bf", [128, 128], BF16)
    idf_in = din("ident_f", [128, 128], F32)
    onb_in = din("ones_bf", [128, 128], BF16)
    onf_in = din("ones_f", [128, 128], F32)

    y_out = nc.dram_tensor("y", [T, D], F32, kind="ExternalOutput").ap()

    cck_in = nc.dram_tensor("cck_in", [D, 1024], BF16).ap()
    cck_out = nc.dram_tensor("cck_out", [4 * D, 1024], BF16).ap()
    ccv_in = nc.dram_tensor("ccv_in", [D, 1024], BF16).ap()
    ccv_out = nc.dram_tensor("ccv_out", [4 * D, 1024], BF16).ap()
    q_dram = nc.dram_tensor("q_dram", [D, T], BF16).ap()
    dump = _os.environ.get("KDUMP")
    if dump:
        x1_dram = nc.dram_tensor("x1_dram", [T, D], F32, kind="ExternalOutput").ap()
        hact_dram = nc.dram_tensor("hact_dram", [II, T], BF16, kind="ExternalOutput").ap()
    else:
        x1_dram = nc.dram_tensor("x1_dram", [T, D], F32).ap()
        hact_dram = nc.dram_tensor("hact_dram", [II, T], BF16).ap()
    wdt_dram = nc.dram_tensor("wdt_dram", [32, 128, II], BF16).ap()

    b.bar = b.sem("bar")
    ld_sy = b.sem("ld_sy")
    s_pe = b.sem("s_pe")
    s_v = b.sem("s_v")
    s_s = b.sem("s_s")
    s_gp = b.sem("s_gp")
    s_cc = b.sem("s_cc")
    ldx = b.semlist("ldx", 2)
    ldw = b.semlist("ldw", 2)
    stk = b.semlist("stk", 2)
    stv = b.semlist("stv", 4)
    ldk = b.semlist("ldk", 2)
    ldv = b.semlist("ldv", 2)
    ldq = b.semlist("ldq", 2)
    ldx4 = b.semlist("ldx4", 4)
    stx1 = b.semlist("stx1", 4)
    ldqg = b.semlist("ldqg", 2)
    ldqu = b.semlist("ldqu", 2)
    ldrg = b.semlist("ldrg", 2)
    ldru = b.semlist("ldru", 2)
    sth = b.semlist("sth", 2)
    ldqd = b.semlist("ldqd", 2)
    stw = b.semlist("stw", 2)
    ldb = b.semlist("ldb", 2)
    ldx1b = b.sem("ldx1b")
    sty4 = b.semlist("sty4", 4)

    ident = b.sbuf("ident_sb", [128, 128], BF16)
    ones_b = b.sbuf("ones_b_sb", [128, 128], BF16)
    ones_f = b.sbuf("ones_f_sb", [128, 128], F32)
    bq_sb = b.sbuf("bq_sb", [128, 32], F32)
    bk_sb = b.sbuf("bk_sb", [128, 32], F32)
    bv_sb = b.sbuf("bv_sb", [128, 32], F32)
    bo_sb = b.sbuf("bo_sb", [128, 32], F32)
    gq_sb = b.sbuf("gq_sb", [128, 1], F32)
    gk_sb = b.sbuf("gk_sb", [128, 1], F32)
    epsc = b.sbuf("epsc", [128, 2], F32)
    idf_in_sb = b.sbuf("idf_sb", [128, 128], F32)
    csel_sb = b.sbuf("csel_sb", [128, 255], BF16)

    for dst, src in ((ident, id_in), (ones_b, onb_in), (ones_f, onf_in),
                     (bq_sb, bq_in), (bk_sb, bk_in), (bv_sb, bv_in),
                     (bo_sb, bo_in), (gk_sb, gk_in), (gq_sb, gq_in),
                     (idf_in_sb, idf_in), (csel_sb, csel_in)):
        b.dma("sync", dst[:], src, ld_sy)
    b.wait("vector", ld_sy)
    b.run("vector", lambda e: e.tensor_scalar(gq_sb[:], gq_sb[:], INV_SQRT_HD,
                                              None, ALU.mult), s_v)
    b.run("vector", lambda e: e.memset(epsc[:, 0:1], LN_EPS), s_v)
    b.run("vector", lambda e: e.memset(epsc[:, 1:2], RMS_EPS), s_v)
    b.barrier()

    def ln_transpose(src_dram, xnT_dst, sc, pre_wait=None):
        xt, xsq, xnb, ps = sc["xt"], sc["xsq"], sc["xnb"], sc["ps"]
        cols = sc["cols"]
        xnT_v = xnT_dst[:].rearrange("p (f t) -> p f t", f=DC)
        pe_done, ev_done, v_apply = [], [], []
        for tau in range(TC):
            par = tau % 2
            if tau == 0 and pre_wait is not None:
                for sem_, val_ in pre_wait:
                    b.wait("sync", sem_, val_)
                    b.wait("scalar", sem_, val_)
            if tau >= 2:
                b.wait("sync", s_v, v_apply[tau - 2])
            b.dma("sync", xt[par][:, 0:D // 2],
                  src_dram[tau * 128:(tau + 1) * 128, 0:D // 2], ldx[par])
            b.dma("scalar", xt[par][:, D // 2:D],
                  src_dram[tau * 128:(tau + 1) * 128, D // 2:D], ldx[par])
            ld_tgt = ldx[par].n
            b.wait("vector", ldx[par], ld_tgt)
            b.run("vector", lambda e, p=par: e.reduce_sum(
                cols[:, 0:1], xt[p][:], axis=AX), s_v)
            b.wait("scalar", ldx[par], ld_tgt)
            if tau >= 1:
                b.wait("scalar", s_v, v_apply[tau - 1])
            b.run("scalar", lambda e, p=par: e.activation(
                xsq[:], xt[p][:], AF.Square), s_s)
            sq_tgt = s_s.n
            b.wait("vector", s_s, sq_tgt)
            b.run("vector", lambda e: e.reduce_sum(cols[:, 2:3], xsq[:], axis=AX),
                  s_v)
            b.run("vector", lambda e: e.tensor_scalar(
                cols[:, 1:2], cols[:, 0:1], 1.0 / D, None, ALU.mult), s_v)
            b.run("vector", lambda e: e.tensor_scalar(
                cols[:, 3:4], cols[:, 2:3], 1.0 / D, None, ALU.mult), s_v)
            b.run("vector", lambda e: e.tensor_tensor(
                cols[:, 6:7], cols[:, 1:2], cols[:, 1:2], ALU.mult), s_v)
            b.run("vector", lambda e: e.tensor_scalar(
                cols[:, 6:7], cols[:, 6:7], -1.0, None, ALU.mult), s_v)
            b.run("vector", lambda e: e.tensor_tensor(
                cols[:, 3:4], cols[:, 3:4], cols[:, 6:7], ALU.add), s_v)
            var_tgt = s_v.n
            b.wait("scalar", s_v, var_tgt)
            b.run("scalar", lambda e: e.activation(
                cols[:, 4:5], cols[:, 3:4], AF.Sqrt, bias=epsc[:, 0:1]), s_s)
            std_tgt = s_s.n
            b.wait("vector", s_s, std_tgt)
            b.run("vector", lambda e: e.reciprocal(cols[:, 5:6], cols[:, 4:5]), s_v)
            if tau >= 2:
                b.wait("vector", s_pe, pe_done[tau - 2])
            b.run("vector", lambda e, p=par: e.tensor_scalar(
                xnb[p][:], xt[p][:], cols[:, 1:2], cols[:, 5:6],
                ALU.subtract, ALU.mult), s_v)
            v_apply.append(s_v.n)
            b.wait("tensor", s_v, v_apply[-1])
            for qd in range(8):
                reg = qd % 2
                gq_i = tau * 8 + qd
                if gq_i >= 2:
                    b.wait("tensor", s_s, ev_done[gq_i - 2])
                for j in range(4):
                    f = qd * 4 + j
                    b.run("tensor", lambda e, p=par, ff=f, r=reg, jj=j: e.matmul(
                        ps[:, r * 512 + jj * 128: r * 512 + (jj + 1) * 128],
                        xnb[p][:, ff * 128:(ff + 1) * 128], ident[:],
                        start=(jj == 0), stop=(jj == 3)),
                        s_pe if j == 3 else None)
                b.wait("scalar", s_pe, s_pe.n)
                b.run("scalar", lambda e, r=reg, qd_=qd, t_=tau: e.copy(
                    xnT_v[:, qd_ * 4:(qd_ + 1) * 4, t_ * 128:(t_ + 1) * 128],
                    ps[:, r * 512:(r + 1) * 512].rearrange(
                        "p (a c) -> p a c", a=4)), s_s)
                ev_done.append(s_s.n)
            pe_done.append(s_pe.n)

    with ExitStack() as sA:
        eA = sA.enter_context
        xnT = eA(nc.sbuf_tensor("xnT", [128, DC * T], BF16))

        with ExitStack() as sA1:
            e1 = sA1.enter_context
            sc1 = dict(
                xt=(e1(nc.sbuf_tensor("xt0", [128, D], F32)),
                    e1(nc.sbuf_tensor("xt1", [128, D], F32))),
                xsq=e1(nc.sbuf_tensor("xsq", [128, D], F32)),
                xnb=(e1(nc.sbuf_tensor("xnb0", [128, D], BF16)),
                     e1(nc.sbuf_tensor("xnb1", [128, D], BF16))),
                cols=e1(nc.sbuf_tensor("cols", [128, 8], F32)),
                ps=e1(nc.psum_tensor("ps_a", [128, 1024], F32)))
            ln_transpose(x_in, xnT, sc1)
        b.barrier()

        wsb = (eA(nc.sbuf_tensor("wsb0", [128, DC * 512], BF16)),
               eA(nc.sbuf_tensor("wsb1", [128, DC * 512], BF16)))
        kraw = [eA(nc.sbuf_tensor(f"kraw{i}", [128, 512], F32)) for i in range(4)]
        k2b = [eA(nc.sbuf_tensor(f"k2b{i}", [128, 512], BF16)) for i in range(2)]
        kp2 = [eA(nc.sbuf_tensor(f"kp2{i}", [128, 512], BF16)) for i in range(2)]
        tmpf = [eA(nc.sbuf_tensor(f"tmpf{i}", [128, 512], F32)) for i in range(2)]
        srow = [eA(nc.sbuf_tensor(f"srow{i}", [1, 512], F32)) for i in range(2)]
        rrow = [eA(nc.sbuf_tensor(f"rrow{i}", [1, 512], F32)) for i in range(2)]
        kbf = [eA(nc.sbuf_tensor(f"kbf{i}", [128, 1024], BF16)) for i in range(2)]
        vpc = [eA(nc.sbuf_tensor(f"vpc{i}", [128, 512], BF16)) for i in range(4)]
        ps_p = eA(nc.psum_tensor("ps_p", [128, 1024], F32))
        ps_r1 = eA(nc.psum_tensor("ps_r1", [128, 512], F32))
        ps_bc = eA(nc.psum_tensor("ps_bc", [128, 512], F32))
        ps_r2 = eA(nc.psum_tensor("ps_r2", [128, 512], F32))
        ps_v2 = eA(nc.psum_tensor("ps_v2", [128, 1024], F32))

        xnT_v = xnT[:].rearrange("p (f t) -> p f t", f=DC)

        def emit_proj(w_dram, bias_sb, g_col, out_kind):
            b.wait("tensor", s_s, s_s.n)
            b.wait("tensor", s_v, s_v.n)
            b.wait("scalar", s_v, s_v.n)
            b.wait("vector", s_s, s_s.n)
            b.wait("vector", stk[0], stk[0].n)
            b.wait("vector", stk[1], stk[1].n)

            pe_base, s_base, v_base = s_pe.n, s_s.n, s_v.n
            pe_entry = s_pe.n
            stk_base = [stk[0].n, stk[1].n]
            ldw_base = [ldw[0].n, ldw[1].n]

            def sq_inc(i):
                return s_base + 2 * i + 1
            def srow_inc(i):
                return s_base + 2 * i + 2
            v_kraw = [0] * 64
            v_rr = [0] * 64
            v_tt = [0] * 64
            v_kbf = [0] * 64
            cnt = v_base
            for i in range(64 + 3):
                if i < 64:
                    cnt += 1
                    v_kraw[i] = cnt
                if 0 <= i - 1 < 64:
                    cnt += 1
                    v_rr[i - 1] = cnt
                if 0 <= i - 3 < 64:
                    cnt += 2
                    v_tt[i - 3] = cnt - 1
                    v_kbf[i - 3] = cnt
            pe_proj = [0] * 64
            pe_sumsq = [0] * 64
            pe_bc = [0] * 64

            for i in range(64 + 4):
                if 0 <= i - 3 < 64:
                    j = i - 3
                    b.wait("tensor", s_v, v_rr[j])
                    if j >= 1:
                        b.wait("tensor", s_v, v_tt[j - 1])
                    b.run("tensor", lambda e, jj=j % 2: e.matmul(
                        ps_bc[:], ones_f[0:1, :], rrow[jj][0:1, :],
                        start=True, stop=True), s_pe)
                    pe_bc[j] = s_pe.n
                if i < 64:
                    h, th = divmod(i, 2)
                    p = h // 4
                    oc = h % 4
                    par = i % 2
                    if th == 0 and oc == 0:
                        b.wait("tensor", ldw[p % 2],
                               ldw_base[p % 2] + 16 * (p // 2 + 1))
                    if i >= 2:
                        b.wait("tensor", s_v, v_kraw[i - 2])
                    for f in range(DC):
                        b.run("tensor", lambda e, pp=p, o=oc, tt=th, ff=f, pr=par:
                              e.matmul(ps_p[:, pr * 512:(pr + 1) * 512],
                                       wsb[pp % 2][:, ff * 512 + o * 128:
                                                   ff * 512 + (o + 1) * 128],
                                       xnT_v[:, ff, tt * 512:(tt + 1) * 512],
                                       start=(ff == 0), stop=(ff == DC - 1)),
                              s_pe if f == DC - 1 else None)
                    pe_proj[i] = s_pe.n
                if 0 <= i - 1 < 64:
                    j = i - 1
                    b.wait("tensor", s_s, sq_inc(j))
                    if j >= 2:
                        b.wait("tensor", s_s, srow_inc(j - 2))
                    b.run("tensor", lambda e, jj=j % 2: e.matmul(
                        ps_r1[0:1, :], ones_b[:, 0:1], k2b[jj][:],
                        start=True, stop=True), s_pe)
                    pe_sumsq[j] = s_pe.n

            for i in range(64):
                b.wait("scalar", s_v, v_kraw[i])
                if i >= 2:
                    b.wait("scalar", s_pe, pe_sumsq[i - 2])
                b.run("scalar", lambda e, i4=i % 4, pr=i % 2: e.activation(
                    k2b[pr][:], kraw[i4][:], AF.Square), s_s)
                b.wait("scalar", s_pe, pe_sumsq[i])
                if i >= 2:
                    b.wait("scalar", s_v, v_rr[i - 2])
                b.run("scalar", lambda e, jj=i % 2: e.activation(
                    srow[jj][0:1, :], ps_r1[0:1, :], AF.Sqrt, scale=1.0 / HD,
                    bias=epsc[0:1, 1:2]), s_s)

            for i in range(64 + 3):
                if i < 64:
                    h, th = divmod(i, 2)
                    b.wait("vector", s_pe, pe_proj[i])
                    if i >= 4:
                        b.wait("vector", s_s, sq_inc(i - 4))
                    b.run("vector", lambda e, hh=h, pr=i % 2, i4=i % 4:
                          e.tensor_scalar(
                              kraw[i4][:], ps_p[:, pr * 512:(pr + 1) * 512],
                              bias_sb[:, hh:hh + 1], None, ALU.add), s_v)
                if 0 <= i - 1 < 64:
                    j = i - 1
                    b.wait("vector", s_s, srow_inc(j))
                    if j >= 2:
                        b.wait("vector", s_pe, pe_bc[j - 2])
                    b.run("vector", lambda e, jj=j % 2: e.reciprocal(
                        rrow[jj][0:1, :], srow[jj][0:1, :]), s_v)
                if 0 <= i - 3 < 64:
                    j = i - 3
                    h, th = divmod(j, 2)
                    b.wait("vector", s_pe, pe_bc[j])
                    b.run("vector", lambda e, i4=j % 4, pr=j % 2: e.tensor_tensor(
                        tmpf[pr][:], kraw[i4][:], ps_bc[:], ALU.mult), s_v)
                    if j >= 4:
                        b.wait("vector", stk[h % 2],
                               stk_base[h % 2] + 16 * (h // 2))
                    b.run("vector", lambda e, hh=h, tt=th, pr=j % 2:
                          e.tensor_scalar(
                              kbf[hh % 2][:, tt * 512:(tt + 1) * 512],
                              tmpf[pr][:], g_col[:, 0:1], None, ALU.mult), s_v)

            w_re = w_dram.rearrange("(f p) o -> p f o", p=128)
            for p in range(8):
                if p >= 2:
                    b.wait("gpsimd", s_pe, pe_proj[(p - 2) * 8 + 7])
                else:
                    b.wait("gpsimd", s_pe, pe_entry)
                b.dma("gpsimd", wsb[p % 2][:].rearrange("p (f o) -> p f o", f=DC),
                      w_re[:, :, p * 512:(p + 1) * 512], ldw[p % 2])

            for h in range(H):
                b.wait("sync", s_v, v_kbf[2 * h + 1])
                dst = (cck_in[h * 128:(h + 1) * 128, :] if out_kind == "k"
                       else q_dram[h * 128:(h + 1) * 128, :])
                b.dma("sync", dst, kbf[h % 2][:], stk[h % 2])

        emit_proj(wk, bk_sb, gk_sb, "k")
        for s_ in stk:
            b.wait("gpsimd", s_, s_.n)
        b.run("gpsimd", lambda e: e.collective_compute(
            "AllGather", ALU.bypass,
            replica_groups=[[0, 1, 2, 3], [4, 5, 6, 7]],
            ins=[cck_in.opt()], outs=[cck_out.opt()]), s_cc, 1)

        b.wait("tensor", s_s, s_s.n)
        b.wait("vector", s_s, s_s.n)
        v_pe_base, v_v_base = s_pe.n, s_v.n
        v_st_base = [s.n for s in stv]
        v_ld_base = [ldw[0].n, ldw[1].n]
        def vg_pe(gi):
            return v_pe_base + gi + 1
        def vg_v(gi, t2):
            return v_v_base + gi * 2 + t2 + 1
        w_re = wv.rearrange("(f p) o -> p f o", p=128)
        for p in range(8):
            if p >= 2:
                b.wait("gpsimd", s_pe, vg_pe((p - 1) * 4 - 1))
            else:
                b.wait("gpsimd", s_pe, v_pe_base)
            b.dma("gpsimd", wsb[p % 2][:].rearrange("p (f o) -> p f o", f=DC),
                  w_re[:, :, p * 512:(p + 1) * 512], ldw[p % 2])
        for p in range(8):
            b.wait("tensor", ldw[p % 2], v_ld_base[p % 2] + 16 * (p // 2 + 1))
            for tcg in range(4):
                gi = p * 4 + tcg
                if gi >= 1:
                    b.wait("tensor", s_v, vg_v(gi - 1, 1))
                for f in range(DC):
                    for t2 in range(2):
                        tc = tcg * 2 + t2
                        b.run("tensor", lambda e, ff=f, t2_=t2, tc_=tc, pp=p:
                              e.matmul(ps_v2[:, t2_ * 512:(t2_ + 1) * 512],
                                       xnT_v[:, ff, tc_ * 128:(tc_ + 1) * 128],
                                       wsb[pp % 2][:, ff * 512:(ff + 1) * 512],
                                       start=(ff == 0), stop=(ff == DC - 1)),
                              s_pe if (f == DC - 1 and t2 == 1) else None)
                pe_tgt = s_pe.n
                for t2 in range(2):
                    tc = tcg * 2 + t2
                    b.wait("vector", s_pe, pe_tgt)
                    vslot = (gi * 2 + t2) % 4
                    if gi * 2 + t2 >= 4:
                        b.wait("vector", stv[vslot],
                               v_st_base[vslot] + 16 * ((gi * 2 + t2) // 4))
                    b.run("vector", lambda e, sl=vslot, t2_=t2: e.tensor_copy(
                        vpc[sl][:], ps_v2[:, t2_ * 512:(t2_ + 1) * 512]), s_v)
                    b.wait("sync", s_v, s_v.n)
                    off = (p // 2) * 1024 + (p % 2) * 512
                    dst = ccv_in[tc * 512:(tc + 1) * 512, :].rearrange(
                        "(t f) c -> t (f c)", t=128)[:, off:off + 512]
                    b.dma("sync", dst, vpc[vslot][:], stv[vslot])

        for s_ in stv:
            b.wait("gpsimd", s_, s_.n)
        b.run("gpsimd", lambda e: e.collective_compute(
            "AllGather", ALU.bypass,
            replica_groups=[[0, 1, 2, 3], [4, 5, 6, 7]],
            ins=[ccv_in.opt()], outs=[ccv_out.opt()]), s_cc, 1)

        emit_proj(wq, bq_sb, gq_sb, "q")
        q_st_final = [stk[0].n, stk[1].n]
        b.barrier()

    with ExitStack() as sB:
        eB = sB.enter_context
        ao = eB(nc.sbuf_tensor("ao", [128, H * T], BF16))

        with ExitStack() as sB1:
            e4 = sB1.enter_context
            probsT = [e4(nc.sbuf_tensor(f"probsT{i}", [128, DC * 512], BF16))
                      for i in range(2)]
            khb = [e4(nc.sbuf_tensor(f"khb{i}", [128, 4096], BF16))
                   for i in range(2)]
            vhb = [e4(nc.sbuf_tensor(f"vhb{i}", [128, 4096], BF16))
                   for i in range(2)]
            qhb = [e4(nc.sbuf_tensor(f"qhb{i}", [128, 1024], BF16))
                   for i in range(2)]
            recb = [e4(nc.sbuf_tensor(f"recb{i}", [128, 512], F32))
                    for i in range(2)]
            tta = [e4(nc.sbuf_tensor(f"tta{i}", [128, 512], F32))
                   for i in range(2)]
            grow = e4(nc.sbuf_tensor("grow", [1, 256], F32))
            nbcol = e4(nc.sbuf_tensor("nbcol", [128, 1], F32))
            srow2 = e4(nc.sbuf_tensor("srow2", [1, 512], F32))
            rrow2 = e4(nc.sbuf_tensor("rrow2", [1, 512], F32))
            ps_sc = e4(nc.psum_tensor("ps_sc", [128, 2048], F32))
            ps_pv = e4(nc.psum_tensor("ps_pv", [128, 1024], F32))
            ps_sum = e4(nc.psum_tensor("ps_sum", [128, 512], F32))
            ps_m = e4(nc.psum_tensor("ps_m", [128, 512], F32))

            cc_r = cck_out.rearrange("(r z) c -> r z c", r=4)

            b.run("tensor", lambda e: e.matmul(
                ps_m[0:1, 0:128], gq_sb[:, 0:1], idf_in_sb[:],
                start=True, stop=True), s_pe)
            b.run("tensor", lambda e: e.matmul(
                ps_m[0:1, 128:256], gk_sb[:, 0:1], idf_in_sb[:],
                start=True, stop=True), s_pe)
            gt_tgt = s_pe.n
            b.wait("scalar", s_pe, gt_tgt)
            b.run("scalar", lambda e: e.copy(grow[0:1, :], ps_m[0:1, 0:256]), s_s)
            g_cp = s_s.n
            b.wait("vector", s_s, g_cp)
            b.run("vector", lambda e: e.tensor_tensor(
                grow[0:1, :], grow[0:1, :], grow[0:1, :], ALU.mult), s_v)
            b.run("vector", lambda e: e.reduce_max(
                grow[0:1, 0:1], grow[0:1, 0:128], axis=AX), s_v)
            b.run("vector", lambda e: e.reduce_max(
                grow[0:1, 1:2], grow[0:1, 128:256], axis=AX), s_v)
            b.run("vector", lambda e: e.tensor_tensor(
                grow[0:1, 0:1], grow[0:1, 0:1], grow[0:1, 1:2], ALU.mult), s_v)
            g_m2 = s_v.n
            b.wait("scalar", s_v, g_m2)
            b.run("scalar", lambda e: e.activation(
                grow[0:1, 2:3], grow[0:1, 0:1], AF.Sqrt), s_s)
            g_sq = s_s.n
            b.wait("vector", s_s, g_sq)
            b.run("vector", lambda e: e.tensor_scalar(
                grow[0:1, 2:3], grow[0:1, 2:3], -128.0, None, ALU.mult), s_v)
            nb_tgt = s_v.n
            b.wait("tensor", s_v, nb_tgt)
            b.run("tensor", lambda e: e.matmul(
                ps_m[:, 0:1], ones_f[0:1, :], grow[0:1, 2:3],
                start=True, stop=True), s_pe)
            bc_all = s_pe.n
            b.wait("scalar", s_pe, bc_all)
            b.run("scalar", lambda e: e.copy(nbcol[:], ps_m[:, 0:1]), s_s)
            nbcol_tgt = s_s.n

            pe_base = s_pe.n
            import os
            if os.environ.get("KDBG"):
                print("P4 pe_base", pe_base, "s_base", s_s.n, "v_base", s_v.n)
            per_nh = 19
            def sc_pair_inc(h, nh, mcg):
                return pe_base + (h * 2 + nh) * per_nh + mcg + 1
            def sums_inc(h, nh):
                return pe_base + (h * 2 + nh) * per_nh + 17
            def pv_inc(h, nh):
                return pe_base + (h * 2 + nh) * per_nh + 18
            def bc2_inc(h, nh):
                return pe_base + (h * 2 + nh) * per_nh + 19
            s_base = s_s.n
            def exp_inc(h, nh, mcg):
                return s_base + (h * 2 + nh) * 18 + mcg + 1
            def srow_cp_inc(h, nh):
                return s_base + (h * 2 + nh) * 18 + 17
            def rec_inc(h, nh):
                return s_base + (h * 2 + nh) * 18 + 18
            v_base = s_v.n
            def rr2_inc(h, nh):
                return v_base + (h * 2 + nh) * 3 + 1
            def aot_inc(h, nh):
                return v_base + (h * 2 + nh) * 3 + 2
            def aow_inc(h, nh):
                return v_base + (h * 2 + nh) * 3 + 3
            ldk_base = [ldk[0].n, ldk[1].n]
            ldv_base = [ldv[0].n, ldv[1].n]
            ldq_base = [ldq[0].n, ldq[1].n]

            b.wait("sync", s_cc, s_cc.n)
            b.wait("sync", stk[0], q_st_final[0])
            b.wait("sync", stk[1], q_st_final[1])
            for h in range(H):
                if h >= 2:
                    b.wait("sync", s_pe, pv_inc(h - 2, 1))
                b.dma("sync", khb[h % 2][:].rearrange("p (r t) -> p r t", r=4),
                      cc_r[:, h * 128:(h + 1) * 128, :].transpose([1, 0, 2]),
                      ldk[h % 2])
                for r in range(4):
                    off = (h // 8) * 1024 + (h % 8) * 128
                    vsrc = ccv_out[r * D:(r + 1) * D, :].rearrange(
                        "(tt t f) c -> tt t (f c)", tt=TC, t=128)[
                        :, :, off:off + 128].transpose([1, 0, 2])
                    b.dma("sync",
                          vhb[h % 2][:, r * 1024:(r + 1) * 1024].rearrange(
                              "p (tt d) -> p tt d", tt=TC),
                          vsrc, ldv[h % 2])
                b.dma("sync", qhb[h % 2][:], q_dram[h * 128:(h + 1) * 128, :],
                      ldq[h % 2])

            for h in range(H):
                b.wait("tensor", ldk[h % 2], ldk_base[h % 2] + 16 * (h // 2 + 1))
                b.wait("tensor", ldv[h % 2], ldv_base[h % 2] + 64 * (h // 2 + 1))
                b.wait("tensor", ldq[h % 2], ldq_base[h % 2] + 16 * (h // 2 + 1))
                for nh in range(2):
                    for mcg in range(16):
                        reg = mcg % 2
                        gidx = (h * 2 + nh) * 16 + mcg
                        if gidx >= 2:
                            pm = gidx - 2
                            hh, rem = divmod(pm, 32)
                            nn, mm = divmod(rem, 16)
                            b.wait("tensor", s_s, exp_inc(hh, nn, mm))
                        for j in range(2):
                            mc = mcg * 2 + j
                            b.run("tensor",
                                  lambda e, hh=h, n_=nh, m_=mc, r=reg, jj=j:
                                  e.matmul(ps_sc[:, r * 1024 + jj * 512:
                                                 r * 1024 + (jj + 1) * 512],
                                           khb[hh % 2][:, m_ * 128:(m_ + 1) * 128],
                                           qhb[hh % 2][:, n_ * 512:(n_ + 1) * 512],
                                           start=True, stop=True),
                                  s_pe if j == 1 else None)
                    b.wait("tensor", s_s, exp_inc(h, nh, 15))
                    if h * 2 + nh >= 1:
                        pi = h * 2 + nh - 1
                        b.wait("tensor", s_s, srow_cp_inc(pi // 2, pi % 2))
                    for mc in range(DC):
                        b.run("tensor", lambda e, m_=mc, pb=(h * 2 + nh) % 2:
                              e.matmul(
                                  ps_sum[0:1, :], ones_b[:, 0:1],
                                  probsT[pb][:, m_ * 512:(m_ + 1) * 512],
                                  start=(m_ == 0), stop=(m_ == DC - 1)),
                              s_pe if mc == DC - 1 else None)
                    if h * 2 + nh >= 2:
                        pi = h * 2 + nh - 2
                        b.wait("tensor", s_v, aot_inc(pi // 2, pi % 2))
                    for mc in range(DC):
                        b.run("tensor", lambda e, hh=h, n_=nh, m_=mc,
                              pb=(h * 2 + nh) % 2: e.matmul(
                                  ps_pv[:, (n_ % 2) * 512:(n_ % 2 + 1) * 512],
                                  vhb[hh % 2][:, m_ * 128:(m_ + 1) * 128],
                                  probsT[pb][:, m_ * 512:(m_ + 1) * 512],
                                  start=(m_ == 0), stop=(m_ == DC - 1)),
                              s_pe if mc == DC - 1 else None)
                    b.wait("tensor", s_v, rr2_inc(h, nh))
                    if h * 2 + nh >= 1:
                        pi = h * 2 + nh - 1
                        b.wait("tensor", s_s, rec_inc(pi // 2, pi % 2))
                    else:
                        b.wait("tensor", s_s, nbcol_tgt)
                    b.run("tensor", lambda e: e.matmul(
                        ps_m[:], ones_f[0:1, :], rrow2[0:1, :],
                        start=True, stop=True), s_pe)

                for nh in range(2):
                    for mcg in range(16):
                        b.wait("scalar", s_pe, sc_pair_inc(h, nh, mcg))
                        if mcg == 0:
                            u = h * 2 + nh
                            if u >= 2:
                                pu = u - 2
                                b.wait("scalar", s_pe,
                                       pv_inc(pu // 2, pu % 2))
                            else:
                                b.wait("scalar", s_s, nbcol_tgt)
                        b.run("scalar", lambda e, m_=mcg, pb=(h * 2 + nh) % 2:
                              e.activation(
                                  probsT[pb][:, m_ * 1024:(m_ + 1) * 1024],
                                  ps_sc[:, (m_ % 2) * 1024:(m_ % 2 + 1) * 1024],
                                  AF.Exp, bias=nbcol[:, 0:1]), s_s)
                    b.wait("scalar", s_pe, sums_inc(h, nh))
                    if h * 2 + nh >= 1:
                        pi = h * 2 + nh - 1
                        b.wait("scalar", s_v, rr2_inc(pi // 2, pi % 2))
                    b.run("scalar", lambda e: e.copy(srow2[0:1, :],
                                                     ps_sum[0:1, :]), s_s)
                    b.wait("scalar", s_pe, bc2_inc(h, nh))
                    b.run("scalar", lambda e, n_=nh: e.copy(recb[n_][:],
                                                            ps_m[:]), s_s)

                for nh in range(2):
                    b.wait("vector", s_s, srow_cp_inc(h, nh))
                    if h * 2 + nh >= 1:
                        pi = h * 2 + nh - 1
                        b.wait("vector", s_pe, bc2_inc(pi // 2, pi % 2))
                    b.run("vector", lambda e: e.reciprocal(rrow2[0:1, :],
                                                           srow2[0:1, :]), s_v)
                    b.wait("vector", s_pe, pv_inc(h, nh))
                    b.wait("vector", s_s, rec_inc(h, nh))
                    b.run("vector", lambda e, n_=nh: e.tensor_tensor(
                        tta[n_][:], ps_pv[:, (n_ % 2) * 512:(n_ % 2 + 1) * 512],
                        recb[n_][:], ALU.mult), s_v)
                    b.run("vector", lambda e, hh=h, n_=nh: e.tensor_scalar(
                        ao[:, hh * 1024 + n_ * 512:hh * 1024 + (n_ + 1) * 512],
                        tta[n_][:], bv_sb[:, hh:hh + 1], None, ALU.add), s_v)
        b.barrier()

        with ExitStack() as sB2:
            e5 = sB2.enter_context
            wob = [e5(nc.sbuf_tensor(f"wob{i}", [128, DC * 512], BF16))
                   for i in range(2)]
            opan = [e5(nc.sbuf_tensor(f"opan{i}", [128, 4096], BF16))
                    for i in range(2)]
            xsl = [e5(nc.sbuf_tensor(f"xsl{i}", [128, 512], F32))
                   for i in range(4)]
            x1p = [e5(nc.sbuf_tensor(f"x1p{i}", [128, 512], F32))
                   for i in range(4)]
            ps_o = e5(nc.psum_tensor("ps_o", [128, 1024], F32))
            ps_t5 = e5(nc.psum_tensor("ps_t5", [128, 1024], F32))

            pe5, s5, v5 = s_pe.n, s_s.n, s_v.n
            ld5 = [s.n for s in ldx4]
            ldg5 = [ldw[0].n, ldw[1].n]
            st5 = [s.n for s in stx1]
            def o_proj_inc(p, i8):
                return pe5 + p * 16 + i8 + 1
            def o_tq_inc(p, tau):
                return pe5 + p * 16 + 8 + tau + 1
            def o_ev_inc(p, i8):
                return v5 + p * 16 + i8 + 1
            def x1_inc(p, tau):
                return v5 + p * 16 + 8 + tau + 1

            wo_re = wo.rearrange("(f p) o -> p f o", p=128)
            for p in range(8):
                if p >= 2:
                    b.wait("gpsimd", s_pe, o_proj_inc(p - 2, 7))
                b.dma("gpsimd", wob[p % 2][:].rearrange("p (f o) -> p f o", f=DC),
                      wo_re[:, :, p * 512:(p + 1) * 512], ldw[p % 2])
            for p in range(8):
                for tau in range(8):
                    gi = p * 8 + tau
                    if gi >= 4:
                        pj = gi - 4
                        b.wait("sync", s_v, x1_inc(pj // 8, pj % 8))
                    b.dma("sync", xsl[gi % 4][:],
                          x_in[tau * 128:(tau + 1) * 128,
                               p * 512:(p + 1) * 512], ldx4[gi % 4])
            for p in range(8):
                b.wait("tensor", ldw[p % 2], ldg5[p % 2] + 16 * (p // 2 + 1))
                for i8 in range(8):
                    oc, th = divmod(i8, 2)
                    gi8 = p * 8 + i8
                    if gi8 >= 2:
                        pj = gi8 - 2
                        b.wait("tensor", s_v, o_ev_inc(pj // 8, pj % 8))
                    for hc in range(DC):
                        b.run("tensor", lambda e, pp=p, o=oc, tt=th, c=hc, r=i8 % 2:
                              e.matmul(ps_o[:, r * 512:(r + 1) * 512],
                                       wob[pp % 2][:, c * 512 + o * 128:
                                                   c * 512 + (o + 1) * 128],
                                       ao[:, c * 1024 + tt * 512:
                                          c * 1024 + (tt + 1) * 512],
                                       start=(c == 0), stop=(c == DC - 1)),
                              s_pe if hc == DC - 1 else None)
                b.wait("tensor", s_v, o_ev_inc(p, 7))
                for tau in range(8):
                    gq_i = p * 8 + tau
                    if gq_i >= 2:
                        pj = gq_i - 2
                        b.wait("tensor", s_v, x1_inc(pj // 8, pj % 8))
                    for oc in range(4):
                        b.run("tensor", lambda e, pp=p, t_=tau, o=oc: e.matmul(
                            ps_t5[:, (t_ % 2) * 512 + o * 128:
                                  (t_ % 2) * 512 + (o + 1) * 128],
                            opan[pp % 2][:, o * 1024 + t_ * 128:
                                         o * 1024 + (t_ + 1) * 128],
                            ident[:], start=(o == 0), stop=(o == 3)),
                            s_pe if oc == 3 else None)
            for p in range(8):
                for i8 in range(8):
                    oc, th = divmod(i8, 2)
                    b.wait("vector", s_pe, o_proj_inc(p, i8))
                    if p >= 2 and i8 == 0:
                        b.wait("vector", s_pe, o_tq_inc(p - 2, 7))
                    b.run("vector", lambda e, pp=p, o=oc, tt=th, r=i8 % 2:
                          e.tensor_scalar(
                              opan[pp % 2][:, o * 1024 + tt * 512:
                                           o * 1024 + (tt + 1) * 512],
                              ps_o[:, r * 512:(r + 1) * 512],
                              bo_sb[:, pp * 4 + o:pp * 4 + o + 1], None,
                              ALU.add), s_v)
                for tau in range(8):
                    gi = p * 8 + tau
                    b.wait("vector", s_pe, o_tq_inc(p, tau))
                    b.wait("vector", ldx4[gi % 4], ld5[gi % 4] + 16 * (gi // 4 + 1))
                    if gi >= 4:
                        b.wait("vector", stx1[gi % 4], st5[gi % 4] + 16 * (gi // 4))
                    b.run("vector", lambda e, t_=tau, g=gi: e.tensor_tensor(
                        x1p[g % 4][:],
                        ps_t5[:, (t_ % 2) * 512:((t_ % 2) + 1) * 512],
                        xsl[g % 4][:], ALU.add), s_v)
            for p in range(8):
                for tau in range(8):
                    gi = p * 8 + tau
                    b.wait("scalar", s_v, x1_inc(p, tau))
                    b.dma("scalar",
                          x1_dram[tau * 128:(tau + 1) * 128,
                                  p * 512:(p + 1) * 512],
                          x1p[gi % 4][:], stx1[gi % 4])
        b.barrier()
    x1_st_final = [s.n for s in stx1]

    with ExitStack() as sD:
        eD = sD.enter_context
        xn2T = eD(nc.sbuf_tensor("xn2T", [128, DC * T], BF16))

        with ExitStack() as sD1:
            e6 = sD1.enter_context
            sc2 = dict(
                xt=(e6(nc.sbuf_tensor("xt0b", [128, D], F32)),
                    e6(nc.sbuf_tensor("xt1b", [128, D], F32))),
                xsq=e6(nc.sbuf_tensor("xsqb", [128, D], F32)),
                xnb=(e6(nc.sbuf_tensor("xnb0b", [128, D], BF16)),
                     e6(nc.sbuf_tensor("xnb1b", [128, D], BF16))),
                cols=e6(nc.sbuf_tensor("colsb", [128, 8], F32)),
                ps=e6(nc.psum_tensor("ps_a2", [128, 1024], F32)))
            ln_transpose(x1_dram, xn2T, sc2,
                         pre_wait=list(zip(stx1, x1_st_final)))
        b.barrier()

        if KT == "3":
            b.dma("sync", y_out, x1_dram, sty4[0])
            b.wait("sync", sty4[0], sty4[0].n)
            b.barrier(drain=False)

        if KT != "3":
            mTg_sb = eD(nc.sbuf_tensor("mTg", [32, II], BF16))
            mTu_sb = eD(nc.sbuf_tensor("mTu", [32, II], BF16))
            srg = [eD(nc.sbuf_tensor(f"srg{i}", [1, D], BF16)) for i in range(2)]
            sru = [eD(nc.sbuf_tensor(f"sru{i}", [1, D], BF16)) for i in range(2)]
            Sx_sb = eD(nc.sbuf_tensor("Sx_sb", [32, T], BF16))

            b.dma("gpsimd", srg[0][:], rg_in[0:1, :], ldrg[0])
            b.dma("gpsimd", sru[0][:], ru_in[0:1, :], ldru[0])
            b.dma("gpsimd", srg[1][:], rg_in[1:2, :], ldrg[1])
            b.dma("gpsimd", sru[1][:], ru_in[1:2, :], ldru[1])

            IH = II // 8
            with ExitStack() as sMT:
                eM = sMT.enter_context
                szf = [eM(nc.sbuf_tensor(f"szf{i}", [32, IH], F32))
                       for i in range(8)]
                vq_hist, gq_hist = [], []
                for hf in range(8):
                    sl = slice(hf * IH, (hf + 1) * IH)
                    pb = (hf % 2) * 4
                    if hf >= 2:
                        b.wait("scalar", s_v, vq_hist[hf - 2])
                        b.wait("scalar", s_gp, gq_hist[hf - 2])
                    b.dma("scalar", szf[pb + 0][:], sgT_in[:, sl], ld_sy)
                    b.dma("scalar", szf[pb + 1][:], zgT_in[:, sl], ld_sy)
                    b.dma("scalar", szf[pb + 2][:], suT_in[:, sl], ld_sy)
                    b.dma("scalar", szf[pb + 3][:], zuT_in[:, sl], ld_sy)
                    tgt = ld_sy.n
                    b.wait("vector", ld_sy, tgt)
                    b.run("vector", lambda e, p=pb: e.tensor_tensor(
                        szf[p + 1][:], szf[p + 1][:], szf[p + 0][:],
                        ALU.mult), s_v)
                    b.run("vector", lambda e, s=sl, p=pb: e.tensor_scalar(
                        mTg_sb[:, s], szf[p + 1][:], -1.0, None, ALU.mult), s_v)
                    vq_hist.append(s_v.n)
                    b.wait("gpsimd", ld_sy, tgt)
                    b.run("gpsimd", lambda e, p=pb: e.tensor_tensor(
                        szf[p + 3][:], szf[p + 3][:], szf[p + 2][:],
                        ALU.mult), s_gp)
                    b.run("gpsimd", lambda e, s=sl, p=pb: e.tensor_scalar(
                        mTu_sb[:, s], szf[p + 3][:], -1.0, None, ALU.mult), s_gp)
                    gq_hist.append(s_gp.n)
                mtg_done = s_v.n
                mtu_done = s_gp.n

            xn2T_v = xn2T[:].rearrange("p (f t) -> p f t", f=DC)
            with ExitStack() as sSX:
                ps_sx = sSX.enter_context(nc.psum_tensor("ps_sx", [128, 1024], F32))
                for th in range(2):
                    for g in range(DC):
                        b.run("tensor", lambda e, g_=g, tt=th: e.matmul(
                            ps_sx[0:32, tt * 512:(tt + 1) * 512],
                            csel_sb[:, 127 - g_:159 - g_],
                            xn2T_v[:, g_, tt * 512:(tt + 1) * 512],
                            start=(g_ == 0), stop=(g_ == DC - 1)),
                            s_pe if g == DC - 1 else None)
                sx_done = s_pe.n
                b.wait("scalar", s_pe, sx_done)
                b.run("scalar", lambda e: e.copy(Sx_sb[:], ps_sx[0:32, :]), s_s)
                sx_cp = s_s.n
        if KT == "2":
            b.wait("sync", s_s, sx_cp)
            b.dma("sync", y_out, x1_dram, sty4[0])
            b.wait("sync", sty4[0], sty4[0].n)
            b.barrier(drain=False)

        if KT not in ("2", "3"):
            qgT = [eD(nc.sbuf_tensor(f"qgT{i}", [128, D], U8)) for i in range(2)]
            quT = [eD(nc.sbuf_tensor(f"quT{i}", [128, D], U8)) for i in range(2)]
            dqgp = [eD(nc.sbuf_tensor(f"dqgp{i}", [128, D], BF16)) for i in range(2)]
            dqup = [eD(nc.sbuf_tensor(f"dqup{i}", [128, D], BF16)) for i in range(2)]
            gsb = [eD(nc.sbuf_tensor(f"gsb{i}", [128, 512], BF16)) for i in range(2)]
            usb = [eD(nc.sbuf_tensor(f"usb{i}", [128, 512], BF16)) for i in range(2)]
            hac = [eD(nc.sbuf_tensor(f"hac{i}", [128, 1024], BF16)) for i in range(2)]
            ps_g = eD(nc.psum_tensor("ps_g", [128, 1024], F32))
            ps_u = eD(nc.psum_tensor("ps_u", [128, 1024], F32))
            ps_bg = eD(nc.psum_tensor("ps_bg", [128, 1024], F32))
            ps_bu = eD(nc.psum_tensor("ps_bu", [128, 1024], F32))

            v6, g6, s6 = s_v.n, s_gp.n, s_s.n

            def vq_done(w, ic, g):
                pos = (g // 4) * 8 + (g % 4) + 1 + (0 if w == 'g' else 4)
                return v6 + ic * 64 + pos

            def hac_done(ic, th):
                return g6 + 2 * ic + th + 1

            def silu_done(ic, th):
                return s6 + ic * 4 + 2 * th + 1

            def uev_done(ic, th):
                return s6 + ic * 4 + 2 * th + 2

            bcg_val = [[0] * 8 for _ in range(ICN)]
            bcu_val = [[0] * 8 for _ in range(ICN)]
            gch_val = [[0, 0] for _ in range(ICN)]
            uch_val = [[0, 0] for _ in range(ICN)]

            def emit_bcasts(ic):
                for j in range(8):
                    k = ic * 8 + j
                    if k >= 2:
                        pic, pj = divmod(k - 2, 8)
                        b.wait("tensor", s_v, vq_done('g', pic, 4 * pj + 3))
                    if j == 0:
                        b.wait("tensor", ldrg[ic % 2], 16 * (ic // 2 + 1))
                        if ic == 0:
                            b.wait("tensor", s_s, sx_cp)
                    b.run("tensor", lambda e, i=ic, jj=j, r=k % 2: e.matmul(
                        ps_bg[:, r * 512:(r + 1) * 512], ones_b[0:1, :],
                        srg[i % 2][0:1, jj * 512:(jj + 1) * 512],
                        start=True, stop=True), s_pe)
                    bcg_val[ic][j] = s_pe.n
                    if k >= 2:
                        pic, pj = divmod(k - 2, 8)
                        b.wait("tensor", s_v, vq_done('u', pic, 4 * pj + 3))
                    if j == 0:
                        b.wait("tensor", ldru[ic % 2], 16 * (ic // 2 + 1))
                    b.run("tensor", lambda e, i=ic, jj=j, r=k % 2: e.matmul(
                        ps_bu[:, r * 512:(r + 1) * 512], ones_b[0:1, :],
                        sru[i % 2][0:1, jj * 512:(jj + 1) * 512],
                        start=True, stop=True), s_pe)
                    bcu_val[ic][j] = s_pe.n

            def emit_chain(w, ic, th):
                ps = ps_g if w == 'g' else ps_u
                mT = mTg_sb if w == 'g' else mTu_sb
                dqp = dqgp if w == 'g' else dqup
                if ic >= 1:
                    b.wait("tensor", s_s,
                           silu_done(ic - 1, th) if w == 'g' else uev_done(ic - 1, th))
                elif th == 0:
                    b.wait("tensor", s_v if w == 'g' else s_gp,
                           mtg_done if w == 'g' else mtu_done)
                b.run("tensor", lambda e, i=ic, tt=th, m=mT, p=ps: e.matmul(
                    p[:, tt * 512:(tt + 1) * 512], m[0:32, i * 128:(i + 1) * 128],
                    Sx_sb[0:32, tt * 512:(tt + 1) * 512],
                    start=True, stop=False))
                for g in range(DC):
                    if th == 0 and g % 4 == 0:
                        b.wait("tensor", s_v, vq_done(w, ic, g + 3))
                    b.run("tensor", lambda e, i=ic, tt=th, g_=g, p=ps, d=dqp:
                          e.matmul(p[:, tt * 512:(tt + 1) * 512],
                                   d[i % 2][:, g_ * 128:(g_ + 1) * 128],
                                   xn2T_v[:, g_, tt * 512:(tt + 1) * 512],
                                   start=False, stop=(g_ == DC - 1)),
                          s_pe if g == DC - 1 else None)
                (gch_val if w == 'g' else uch_val)[ic][th] = s_pe.n

            for ic in range(ICN):
                if ic == 0:
                    emit_bcasts(0)
                emit_chain('g', ic, 0)
                if ic + 1 < ICN:
                    emit_bcasts(ic + 1)
                emit_chain('g', ic, 1)
                emit_chain('u', ic, 0)
                emit_chain('u', ic, 1)

            def dq_op(w, ic, g):
                dqp, qT, psb = ((dqgp, qgT, ps_bg) if w == 'g'
                                else (dqup, quT, ps_bu))
                if g % 4 == 0:
                    b.wait("vector", s_pe,
                           (bcg_val if w == 'g' else bcu_val)[ic][g // 4])
                k = ic * 8 + g // 4
                b.run("vector", lambda e, i=ic, g_=g, r=k % 2, d=dqp, q=qT, p=psb:
                      e.tensor_tensor(
                          d[i % 2][:, g_ * 128:(g_ + 1) * 128],
                          q[i % 2][:, g_ * 128:(g_ + 1) * 128],
                          p[:, r * 512 + (g_ % 4) * 128:
                            r * 512 + (g_ % 4 + 1) * 128], ALU.mult), s_v)

            for ic in range(ICN):
                b.wait("vector", ldqg[ic % 2], 16 * (ic // 2 + 1))
                b.wait("vector", ldqu[ic % 2], 16 * (ic // 2 + 1))
                if ic >= 2:
                    b.wait("vector", s_pe, gch_val[ic - 2][1])
                    b.wait("vector", s_pe, uch_val[ic - 2][1])
                for j in range(8):
                    for g in range(4 * j, 4 * j + 4):
                        dq_op('g', ic, g)
                    for g in range(4 * j, 4 * j + 4):
                        dq_op('u', ic, g)

            for ic in range(ICN):
                if ic + 2 < ICN:
                    b.wait("gpsimd", s_pe, bcu_val[ic][7])
                    b.dma("gpsimd", srg[(ic + 2) % 2][:],
                          rg_in[ic + 2:ic + 3, :], ldrg[(ic + 2) % 2])
                    b.dma("gpsimd", sru[(ic + 2) % 2][:],
                          ru_in[ic + 2:ic + 3, :], ldru[(ic + 2) % 2])
                if ic >= 1:
                    for th in range(2):
                        b.wait("gpsimd", s_s, uev_done(ic - 1, th))
                        if ic >= 3:
                            b.wait("gpsimd", sth[(ic - 1) % 2],
                                   16 * ((ic - 1) // 2))
                        b.run("gpsimd", lambda e, i=ic - 1, tt=th:
                              e.tensor_tensor(
                                  hac[i % 2][:, tt * 512:(tt + 1) * 512],
                                  gsb[tt][:], usb[tt][:], ALU.mult), s_gp)
            for th in range(2):
                b.wait("gpsimd", s_s, uev_done(ICN - 1, th))
                b.wait("gpsimd", sth[(ICN - 1) % 2], 16 * ((ICN - 1) // 2))
                b.run("gpsimd", lambda e, i=ICN - 1, tt=th: e.tensor_tensor(
                    hac[i % 2][:, tt * 512:(tt + 1) * 512],
                    gsb[tt][:], usb[tt][:], ALU.mult), s_gp)

            for ic in range(ICN):
                for th in range(2):
                    b.wait("scalar", s_pe, gch_val[ic][th])
                    if ic >= 1:
                        b.wait("scalar", s_gp, hac_done(ic - 1, th))
                    b.run("scalar", lambda e, tt=th: e.activation(
                        gsb[tt][:], ps_g[:, tt * 512:(tt + 1) * 512], AF.Silu), s_s)
                    b.wait("scalar", s_pe, uch_val[ic][th])
                    b.run("scalar", lambda e, tt=th: e.activation(
                        usb[tt][:], ps_u[:, tt * 512:(tt + 1) * 512], AF.Copy), s_s)
                if ic >= 1:
                    b.dma("scalar", hact_dram[(ic - 1) * 128:ic * 128, :],
                          hac[(ic - 1) % 2][:], sth[(ic - 1) % 2])
            b.wait("scalar", s_gp, s_gp.n)
            b.dma("scalar", hact_dram[(ICN - 1) * 128:ICN * 128, :],
                  hac[(ICN - 1) % 2][:], sth[(ICN - 1) % 2])

            b.wait("sync", s_v, mtg_done)
            b.wait("sync", s_gp, mtu_done)
            for ic in range(ICN):
                if ic >= 2:
                    b.wait("sync", s_v, vq_done('g', ic - 2, DC - 1))
                    b.wait("sync", s_v, vq_done('u', ic - 2, DC - 1))
                b.dma("sync", qgT[ic % 2][:], pg_in[ic], ldqg[ic % 2])
                b.dma("sync", quT[ic % 2][:], pu_in[ic], ldqu[ic % 2])
            b.barrier()

    if KT:
        if KT == "1":
            b.dma("sync", y_out, x1_dram, sty4[0])
            b.wait("sync", sty4[0], sty4[0].n)
            b.barrier(drain=False)
        b.finalize_block()
        b.ctx.close()
        return nc

    with ExitStack() as sF:
        eF = sF.enter_context
        szd = {}
        for nm, src in (("sd", sd_in), ("zd", zd_in)):
            szd[nm] = eF(nc.sbuf_tensor(nm + "_sb", [128, 32 * ICN], F32))
            b.dma("sync", szd[nm][:], src, ld_sy)
        b.wait("vector", ld_sy)
        b.run("vector", lambda e: e.tensor_tensor(
            szd["zd"][:], szd["zd"][:], szd["sd"][:], ALU.mult), s_v)
        b.run("vector", lambda e: e.tensor_scalar(
            szd["zd"][:], szd["zd"][:], -1.0, None, ALU.mult), s_v)
        nzd_tgt = s_v.n

        qd_sb = [eF(nc.sbuf_tensor(f"qd{i}", [128, II], I32)) for i in range(2)]
        dqd = [eF(nc.sbuf_tensor(f"dqd{i}", [128, 128], BF16)) for i in range(8)]
        wdc = [eF(nc.sbuf_tensor(f"wdc{i}", [128, II], BF16)) for i in range(2)]
        ps_w7 = eF(nc.psum_tensor("ps_w7", [128, 2048], F32))

        groups = [list(range(g * 4, min(g * 4 + 4, ICN))) for g in range(22)]
        v_ics = [ic for ic in range(ICN) if ic % 4 != 3]
        p_ics = [ic for ic in range(ICN) if ic % 4 == 3]
        v_ord = {ic: i for i, ic in enumerate(v_ics)}
        p_ord = {ic: i for i, ic in enumerate(p_ics)}
        NV, NP = len(v_ics), len(p_ics)
        pe7 = s_pe.n
        def w7q_inc(hc, g):
            return pe7 + hc * 22 + g + 1
        s7 = s_s.n
        def w7e_inc(hc, e):
            return s7 + hc * 11 + e + 1
        v7 = s_v.n
        g7 = s_gp.n
        def dq7_sem_val(hc, ic):
            if ic % 4 == 3:
                return s_gp, g7 + hc * NP + p_ord[ic] + 1
            return s_v, v7 + hc * NV + v_ord[ic] + 1
        ld7 = [ldqd[0].n, ldqd[1].n]
        st7 = [stw[0].n, stw[1].n]

        for hc in range(32):
            if hc >= 2:
                b.wait("sync", s_v, v7 + (hc - 1) * NV)
                b.wait("sync", s_gp, g7 + (hc - 1) * NP)
            b.dma("sync", qd_sb[hc % 2][:], qwd[hc * 128:(hc + 1) * 128, :],
                  ldqd[hc % 2])
            if hc >= 1:
                b.wait("scalar", s_s, w7e_inc(hc - 1, 10))
                b.dma("scalar", wdt_dram[hc - 1, :, :], wdc[(hc - 1) % 2][:],
                      stw[(hc - 1) % 2])
            for eng, ics_e in (("vector", v_ics), ("gpsimd", p_ics)):
                b.wait(eng, ldqd[hc % 2], ld7[hc % 2] + 16 * (hc // 2 + 1))
                if hc == 0:
                    b.wait(eng, s_v, nzd_tgt)
                sem = s_v if eng == "vector" else s_gp
                for ic in ics_e:
                    if ic >= 8:
                        b.wait(eng, s_pe, w7q_inc(hc, (ic - 8) // 4))
                    elif hc >= 1:
                        b.wait(eng, s_pe, w7q_inc(hc - 1, 21))
                    b.run(eng, lambda e, h_=hc, i=ic: e.tensor_scalar(
                        dqd[i % 8][:], qd_sb[h_ % 2][:, i * 128:(i + 1) * 128],
                        szd["sd"][:, h_ * ICN + i:h_ * ICN + i + 1],
                        szd["zd"][:, h_ * ICN + i:h_ * ICN + i + 1],
                        ALU.mult, ALU.add), sem)
            for g, ics in enumerate(groups):
                G = hc * 22 + g
                if G >= 4:
                    pj = G - 4
                    b.wait("tensor", s_s, w7e_inc(pj // 22, (pj % 22) // 2))
                for j, ic in enumerate(ics):
                    dsem, dval = dq7_sem_val(hc, ic)
                    b.wait("tensor", dsem, dval)
                    b.run("tensor", lambda e, i=ic, jj=j, r=G % 4, n_=len(ics):
                          e.matmul(ps_w7[:, r * 512 + jj * 128:
                                         r * 512 + (jj + 1) * 128],
                                   dqd[i % 8][:], ident[:],
                                   start=(jj == 0), stop=(jj == n_ - 1)),
                          s_pe if j == len(ics) - 1 else None)
                if g % 2 == 1:
                    e_ = g // 2
                    b.wait("scalar", s_pe, w7q_inc(hc, g))
                    if e_ == 0 and hc >= 2:
                        b.wait("scalar", stw[hc % 2],
                               st7[hc % 2] + 16 * (hc // 2))
                    g0 = 2 * e_
                    width = (len(groups[g0]) + len(groups[g0 + 1])) * 128
                    r0 = (hc * 22 + g0) % 4
                    b.run("scalar", lambda e, h_=hc, c0=g0 * 512, w_=width,
                          rr=r0: e.copy(
                              wdc[h_ % 2][:, c0:c0 + w_],
                              ps_w7[:, rr * 512:rr * 512 + w_]), s_s)
        b.wait("scalar", s_s, s_s.n)
        b.dma("scalar", wdt_dram[31, :, :], wdc[1][:], stw[1])
        b.barrier()
    wdt_st_final = [stw[0].n, stw[1].n]

    with ExitStack() as sG:
        eG = sG.enter_context
        wdts = [eG(nc.sbuf_tensor(f"wdts{i}", [128, 8 * 512], BF16))
                for i in range(2)]
        hsl = [eG(nc.sbuf_tensor(f"hsl{i}", [128, 4 * 512], BF16))
               for i in range(2)]
        dnsb = eG(nc.sbuf_tensor("dnsb", [128, 8 * 512], BF16))
        x1s = [eG(nc.sbuf_tensor(f"x1s{i}", [128, 1024], F32)) for i in range(4)]
        ypc = [eG(nc.sbuf_tensor(f"ypc{i}", [128, 1024], F32)) for i in range(4)]
        ps_d = eG(nc.psum_tensor("ps_d", [128, 8 * 512], F32))

        hact_r = hact_dram.rearrange("(ic p) t -> ic p t", p=128)
        icbs = [list(range(g * 4, min(g * 4 + 4, ICN))) for g in range(22)]
        pe8 = s_pe.n
        def b_inc(pss, bi):
            return pe8 + pss * 30 + bi + 1
        def tq8_inc(pss, tau, half):
            return pe8 + pss * 30 + 22 + tau * 2 + half + 1
        s8 = s_s.n
        def dne_inc(pss, hc8):
            return s8 + pss * 8 + hc8 + 1
        v8 = s_v.n
        def y_inc(pss, tau):
            return v8 + pss * 8 + 2 * tau + 2
        ld8b = [ldb[0].n, ldb[1].n]
        ld8x = ldx1b.n

        b.wait("sync", stw[0], wdt_st_final[0])
        b.wait("sync", stw[1], wdt_st_final[1])
        b.wait("sync", sth[0], sth[0].n)
        b.wait("sync", sth[1], sth[1].n)
        for s_, v_ in zip(stx1, x1_st_final):
            b.wait("sync", s_, v_)
        for pss in range(8):
            th, grp = divmod(pss, 4)
            for bi, ics in enumerate(icbs):
                gi = pss * 22 + bi
                if gi >= 2:
                    pj = gi - 2
                    b.wait("sync", s_pe, b_inc(pj // 22, pj % 22))
                w = len(ics)
                b.dma("sync",
                      wdts[bi % 2][:, 0:8 * w * 128].rearrange(
                          "p (c i) -> p c i", c=8),
                      wdt_dram[grp * 8:(grp + 1) * 8, :,
                               ics[0] * 128:ics[0] * 128 + w * 128]
                      .transpose([1, 0, 2]), ldb[bi % 2])
                b.dma("sync",
                      hsl[bi % 2][:, 0:w * 512].rearrange(
                          "p (i t) -> p i t", i=w),
                      hact_r[ics[0]:ics[0] + w, :, th * 512:(th + 1) * 512]
                      .transpose([1, 0, 2]), ldb[bi % 2])
            for tau in range(4):
                if pss >= 1:
                    b.wait("sync", s_v, y_inc(pss - 1, tau))
                b.dma("sync", x1s[tau][:],
                      x1_dram[th * 512 + tau * 128:th * 512 + (tau + 1) * 128,
                              grp * 1024:(grp + 1) * 1024], ldx1b)
            for bi, ics in enumerate(icbs):
                gb = pss * 22 + bi
                b.wait("tensor", ldb[bi % 2], ld8b[bi % 2] + 32 * (gb // 2 + 1))
                if bi == 0 and pss >= 1:
                    b.wait("tensor", s_v, y_inc(pss - 1, 3))
                w = len(ics)
                for j, ic in enumerate(ics):
                    for hc8 in range(8):
                        b.run("tensor", lambda e, b_=bi, jj=j, c8=hc8, i=ic,
                              w_=w:
                              e.matmul(ps_d[:, c8 * 512:(c8 + 1) * 512],
                                       wdts[b_ % 2][:, c8 * w_ * 128 + jj * 128:
                                                    c8 * w_ * 128 + (jj + 1) * 128],
                                       hsl[b_ % 2][:, jj * 512:(jj + 1) * 512],
                                       start=(i == 0), stop=(i == ICN - 1)),
                              s_pe if (j == w - 1 and hc8 == 7) else None)
            for hc8 in range(8):
                b.wait("scalar", s_pe, b_inc(pss, 21))
                b.run("scalar", lambda e, c8=hc8: e.copy(
                    dnsb[:, c8 * 512:(c8 + 1) * 512],
                    ps_d[:, c8 * 512:(c8 + 1) * 512]), s_s)
            for tau in range(4):
                for half in range(2):
                    b.wait("tensor", s_s, dne_inc(pss, 7))
                    if tau * 2 + half >= 2:
                        b.wait("tensor", s_v, y_inc(pss, tau - 1))
                    for j in range(4):
                        hc8 = half * 4 + j
                        b.run("tensor", lambda e, c8=hc8, jj=j, hf=half, t_=tau:
                              e.matmul(ps_d[:, hf * 512 + jj * 128:
                                            hf * 512 + (jj + 1) * 128],
                                       dnsb[:, c8 * 512 + t_ * 128:
                                            c8 * 512 + (t_ + 1) * 128],
                                       ident[:], start=(jj == 0), stop=(jj == 3)),
                              s_pe if j == 3 else None)
                b.wait("vector", s_pe, tq8_inc(pss, tau, 1))
                b.wait("vector", ldx1b, ld8x + 64 * (pss + 1))
                if pss >= 1:
                    b.wait("vector", sty4[tau], 16 * pss)
                for half in range(2):
                    b.run("vector", lambda e, t_=tau, hf=half: e.tensor_tensor(
                        ypc[t_][:, hf * 512:(hf + 1) * 512],
                        ps_d[:, hf * 512:(hf + 1) * 512],
                        x1s[t_][:, hf * 512:(hf + 1) * 512], ALU.add), s_v)
                b.wait("scalar", s_v, y_inc(pss, tau))
                b.dma("scalar",
                      y_out[th * 512 + tau * 128:th * 512 + (tau + 1) * 128,
                            grp * 1024:(grp + 1) * 1024],
                      ypc[tau][:], sty4[tau])
        for s_ in sty4:
            b.wait("sync", s_, s_.n)
        b.barrier(drain=False)

    b.finalize_block()
    b.ctx.close()
    return nc


_NC_CACHE = None


def kernel(**inputs):
    global _NC_CACHE
    xs = np.ascontiguousarray(np.asarray(inputs["L0_attn_hidden_states"]))
    ident = np.eye(128, dtype=ml_dtypes.bfloat16)
    ones_b = np.ones((128, 128), dtype=ml_dtypes.bfloat16)
    ones_f = np.ones((128, 128), dtype=np.float32)

    def pcol(v):
        return np.ascontiguousarray(np.asarray(v, np.float32).reshape(32, 128).T)

    def prow_d(v):
        return np.ascontiguousarray(
            np.asarray(v, np.float32).reshape(32, 128, ICN)
            .transpose(1, 0, 2).reshape(128, -1))

    def panels_gu(qw):
        q = np.asarray(qw).astype(np.uint8)
        return np.ascontiguousarray(
            q.reshape(ICN, 128, 32, 128).transpose(0, 3, 2, 1))

    def srows_gu(s):
        return np.ascontiguousarray(
            np.asarray(s, np.float32).reshape(ICN, 128, 32)
            .transpose(0, 2, 1).reshape(ICN, D))

    csel = np.zeros((128, 255), dtype=ml_dtypes.bfloat16)
    csel[:, 127] = 1

    common = dict(
        wq=np.ascontiguousarray(np.asarray(inputs["L0_attn_Wq"], np.float32)),
        wk=np.ascontiguousarray(np.asarray(inputs["L0_attn_Wk"], np.float32)),
        wv=np.ascontiguousarray(np.asarray(inputs["L0_attn_Wv"], np.float32)),
        wo=np.ascontiguousarray(np.asarray(inputs["L0_attn_Wo"], np.float32)),
        qwd=np.ascontiguousarray(np.asarray(inputs["L0_ffn_qw_down"], np.int32)),
        pg=panels_gu(inputs["L0_ffn_qw_gate"]),
        pu=panels_gu(inputs["L0_ffn_qw_up"]),
        rg=srows_gu(inputs["L0_ffn_s_gate"]),
        ru=srows_gu(inputs["L0_ffn_s_up"]),
        sgT=np.ascontiguousarray(np.asarray(inputs["L0_ffn_s_gate"], np.float32).T),
        zgT=np.ascontiguousarray(np.asarray(inputs["L0_ffn_z_gate"], np.float32).T),
        suT=np.ascontiguousarray(np.asarray(inputs["L0_ffn_s_up"], np.float32).T),
        zuT=np.ascontiguousarray(np.asarray(inputs["L0_ffn_z_up"], np.float32).T),
        csel=csel,
        sd=prow_d(inputs["L0_ffn_s_down"]), zd=prow_d(inputs["L0_ffn_z_down"]),
        bq_p=pcol(inputs["L0_attn_bq"]), bk_p=pcol(inputs["L0_attn_bk"]),
        bv_p=pcol(inputs["L0_attn_bv"]), bo_p=pcol(inputs["L0_attn_bo"]),
        gq_c=np.ascontiguousarray(
            np.asarray(inputs["L0_attn_gq"], np.float32).reshape(128, 1)),
        gk_c=np.ascontiguousarray(
            np.asarray(inputs["L0_attn_gk"], np.float32).reshape(128, 1)),
        ident_bf=ident, ones_bf=ones_b, ones_f=ones_f,
        ident_f=np.eye(128, dtype=np.float32),
    )
    in_maps = []
    for c in range(8):
        bb, r = divmod(c, 4)
        m = dict(common)
        m["x_in"] = np.ascontiguousarray(xs[bb, r * T:(r + 1) * T, :], np.float32)
        in_maps.append(m)

    if _NC_CACHE is None:
        _NC_CACHE = build_program()
    res = run_bass_kernel_spmd(_NC_CACHE, in_maps, core_ids=list(range(8)))
    out = np.empty((B, N, D), np.float32)
    for c in range(8):
        bb, r = divmod(c, 4)
        out[bb, r * T:(r + 1) * T, :] = res.results[c]["y"]
    return out


# version 12 (final)
# speedup vs baseline: 1.1452x; 1.0083x over previous
import math
from contextlib import ExitStack

import numpy as np
import ml_dtypes

import concourse.bass as bass
import concourse.mybir as mybir
from concourse.bass_utils import run_bass_kernel_spmd

F32 = mybir.dt.float32
BF16 = mybir.dt.bfloat16
I32 = mybir.dt.int32
U8 = mybir.dt.uint8
AF = mybir.ActivationFunctionType
ALU = mybir.AluOpType
AX = mybir.AxisListType.X

B, N, D = 2, 4096, 4096
H, HD = 32, 128
II = 11008
T = 1024
TC = T // 128
DC = D // 128
ICN = II // 128
RS = 2 * D
INV_SQRT_HD = 1.0 / math.sqrt(HD)
ENGS = ("sync", "tensor", "vector", "scalar", "gpsimd")
LN_EPS = 1e-5
RMS_EPS = 1e-6


class Sem:
    __slots__ = ("h", "n")

    def __init__(self, h):
        self.h = h
        self.n = 0


class Builder:
    def __init__(self):
        self.nc = bass.Bass("TRN2", target_bir_lowering=False)
        self.q = {e: [] for e in ENGS}
        self.ctx = ExitStack()
        self.bar_k = 0

    def sem(self, name):
        return Sem(self.ctx.enter_context(self.nc.semaphore(name)))

    def semlist(self, name, k):
        return [self.sem(f"{name}{i}") for i in range(k)]

    def sbuf(self, name, shape, dtype):
        return self.ctx.enter_context(self.nc.sbuf_tensor(name, shape, dtype))

    def emit(self, eng, fn):
        self.q[eng].append(fn)

    def wait(self, eng, sem, val=None):
        v = sem.n if val is None else val
        if v > 0:
            self.emit(eng, lambda e, s=sem.h, vv=v: e.wait_ge(s, vv))

    SELF_SYNC = ("vector", "scalar", "gpsimd")

    def run(self, eng, build, sem=None, k=1):
        if eng in self.SELF_SYNC and sem is not None and sem.n > 0:
            self.wait(eng, sem, sem.n)
        if sem is not None:
            sem.n += k
        def f(e, bld=build, s=sem, kk=k):
            ins = bld(e)
            if s is not None:
                ins.then_inc(s.h, kk)
        self.emit(eng, f)
        return sem.n if sem is not None else None

    def dma(self, eng, out, in_, sem, **kw):
        return self.run(eng, lambda e, o=out, i=in_, k=kw: e.dma_start(out=o, in_=i, **k),
                        sem, 16)

    def barrier(self, drain=True):
        self.bar_k += 1
        tgt = 5 * self.bar_k
        for eng in ENGS:
            if drain and eng in ("tensor", "vector", "scalar", "gpsimd"):
                self.emit(eng, lambda e: e.drain())
            self.emit(eng, lambda e, s=self.bar.h: e.sem_inc(s, 1))
            self.emit(eng, lambda e, s=self.bar.h, v=tgt: e.wait_ge(s, v))

    def finalize_block(self):
        nc = self.nc
        with nc.Block() as blk:
            for eng in ENGS:
                fns = self.q[eng]
                getattr(blk, eng)(lambda e, fns=fns: [fn(e) for fn in fns])


def build_program():
    import os as _os
    KT = _os.environ.get("KTRUNC", "")
    b = Builder()
    nc = b.nc

    def din(name, shape, dtype):
        return nc.dram_tensor(name, shape, dtype, kind="ExternalInput").ap()

    x_in = din("x_in", [T, D], F32)
    wq = din("wq", [D, D], F32)
    wk = din("wk", [D, D], F32)
    wv = din("wv", [D, D], F32)
    wo = din("wo", [D, D], F32)
    qwd = din("qwd", [D, II], I32)
    pg_in = din("pg", [ICN, 128, D], U8)
    pu_in = din("pu", [ICN, 128, D], U8)
    rg_in = din("rg", [ICN, D], F32)
    ru_in = din("ru", [ICN, D], F32)
    sgT_in = din("sgT", [32, II], F32)
    zgT_in = din("zgT", [32, II], F32)
    suT_in = din("suT", [32, II], F32)
    zuT_in = din("zuT", [32, II], F32)
    csel_in = din("csel", [128, 255], BF16)
    sd_in = din("sd", [128, 32 * ICN], F32)
    zd_in = din("zd", [128, 32 * ICN], F32)
    bq_in = din("bq_p", [128, 32], F32)
    bk_in = din("bk_p", [128, 32], F32)
    bv_in = din("bv_p", [128, 32], F32)
    bo_in = din("bo_p", [128, 32], F32)
    gq_in = din("gq_c", [128, 1], F32)
    gk_in = din("gk_c", [128, 1], F32)
    id_in = din("ident_bf", [128, 128], BF16)
    idf_in = din("ident_f", [128, 128], F32)
    onb_in = din("ones_bf", [128, 128], BF16)
    onf_in = din("ones_f", [128, 128], F32)

    y_out = nc.dram_tensor("y", [T, D], F32, kind="ExternalOutput").ap()

    cck_in = nc.dram_tensor("cck_in", [D, 1024], BF16).ap()
    cck_out = nc.dram_tensor("cck_out", [4 * D, 1024], BF16).ap()
    ccv_in = nc.dram_tensor("ccv_in", [D, 1024], BF16).ap()
    ccv_out = nc.dram_tensor("ccv_out", [4 * D, 1024], BF16).ap()
    q_dram = nc.dram_tensor("q_dram", [D, T], BF16).ap()
    dump = _os.environ.get("KDUMP")
    if dump:
        x1_dram = nc.dram_tensor("x1_dram", [T, D], F32, kind="ExternalOutput").ap()
        hact_dram = nc.dram_tensor("hact_dram", [II, T], BF16, kind="ExternalOutput").ap()
    else:
        x1_dram = nc.dram_tensor("x1_dram", [T, D], F32).ap()
        hact_dram = nc.dram_tensor("hact_dram", [II, T], BF16).ap()
    wdt_dram = nc.dram_tensor("wdt_dram", [32, 128, II], BF16).ap()

    b.bar = b.sem("bar")
    ld_sy = b.sem("ld_sy")
    s_pe = b.sem("s_pe")
    s_v = b.sem("s_v")
    s_s = b.sem("s_s")
    s_gp = b.sem("s_gp")
    s_cc = b.sem("s_cc")
    ldx = b.semlist("ldx", 2)
    ldw = b.semlist("ldw", 2)
    stk = b.semlist("stk", 2)
    stv = b.semlist("stv", 4)
    ldk = b.semlist("ldk", 2)
    ldv = b.semlist("ldv", 2)
    ldq = b.semlist("ldq", 2)
    ldx4 = b.semlist("ldx4", 4)
    stx1 = b.semlist("stx1", 4)
    ldqg = b.semlist("ldqg", 2)
    ldqu = b.semlist("ldqu", 2)
    ldrg = b.semlist("ldrg", 2)
    ldru = b.semlist("ldru", 2)
    sth = b.semlist("sth", 2)
    ldqd = b.semlist("ldqd", 2)
    stw = b.semlist("stw", 2)
    ldb = b.semlist("ldb", 2)
    ldx1b = b.sem("ldx1b")
    sty4 = b.semlist("sty4", 4)

    ident = b.sbuf("ident_sb", [128, 128], BF16)
    ones_b = b.sbuf("ones_b_sb", [128, 128], BF16)
    ones_f = b.sbuf("ones_f_sb", [128, 128], F32)
    bq_sb = b.sbuf("bq_sb", [128, 32], F32)
    bk_sb = b.sbuf("bk_sb", [128, 32], F32)
    bv_sb = b.sbuf("bv_sb", [128, 32], F32)
    bo_sb = b.sbuf("bo_sb", [128, 32], F32)
    gq_sb = b.sbuf("gq_sb", [128, 1], F32)
    gk_sb = b.sbuf("gk_sb", [128, 1], F32)
    epsc = b.sbuf("epsc", [128, 2], F32)
    idf_in_sb = b.sbuf("idf_sb", [128, 128], F32)
    csel_sb = b.sbuf("csel_sb", [128, 255], BF16)

    for dst, src in ((ident, id_in), (ones_b, onb_in), (ones_f, onf_in),
                     (bq_sb, bq_in), (bk_sb, bk_in), (bv_sb, bv_in),
                     (bo_sb, bo_in), (gk_sb, gk_in), (gq_sb, gq_in),
                     (idf_in_sb, idf_in), (csel_sb, csel_in)):
        b.dma("sync", dst[:], src, ld_sy)
    b.wait("vector", ld_sy)
    b.run("vector", lambda e: e.tensor_scalar(gq_sb[:], gq_sb[:], INV_SQRT_HD,
                                              None, ALU.mult), s_v)
    b.run("vector", lambda e: e.memset(epsc[:, 0:1], LN_EPS), s_v)
    b.run("vector", lambda e: e.memset(epsc[:, 1:2], RMS_EPS), s_v)
    b.barrier()

    def ln_transpose(src_dram, xnT_dst, sc, pre_wait=None):
        xt, xsq, xnb, ps = sc["xt"], sc["xsq"], sc["xnb"], sc["ps"]
        cols = sc["cols"]
        xnT_v = xnT_dst[:].rearrange("p (f t) -> p f t", f=DC)
        pe_done, ev_done, v_apply = [], [], []
        for tau in range(TC):
            par = tau % 2
            if tau == 0 and pre_wait is not None:
                for sem_, val_ in pre_wait:
                    b.wait("sync", sem_, val_)
            if tau >= 2:
                b.wait("sync", s_v, v_apply[tau - 2])
            b.dma("sync", xt[par][:], src_dram[tau * 128:(tau + 1) * 128, :],
                  ldx[par])
            ld_tgt = ldx[par].n
            b.wait("vector", ldx[par], ld_tgt)
            b.run("vector", lambda e, p=par: e.reduce_sum(
                cols[:, 0:1], xt[p][:], axis=AX), s_v)
            b.wait("scalar", ldx[par], ld_tgt)
            if tau >= 1:
                b.wait("scalar", s_v, v_apply[tau - 1])
            b.run("scalar", lambda e, p=par: e.activation(
                xsq[:], xt[p][:], AF.Square), s_s)
            sq_tgt = s_s.n
            b.wait("vector", s_s, sq_tgt)
            b.run("vector", lambda e: e.reduce_sum(cols[:, 2:3], xsq[:], axis=AX),
                  s_v)
            b.run("vector", lambda e: e.tensor_scalar(
                cols[:, 1:2], cols[:, 0:1], 1.0 / D, None, ALU.mult), s_v)
            b.run("vector", lambda e: e.tensor_scalar(
                cols[:, 3:4], cols[:, 2:3], 1.0 / D, None, ALU.mult), s_v)
            b.run("vector", lambda e: e.tensor_tensor(
                cols[:, 6:7], cols[:, 1:2], cols[:, 1:2], ALU.mult), s_v)
            b.run("vector", lambda e: e.tensor_scalar(
                cols[:, 6:7], cols[:, 6:7], -1.0, None, ALU.mult), s_v)
            b.run("vector", lambda e: e.tensor_tensor(
                cols[:, 3:4], cols[:, 3:4], cols[:, 6:7], ALU.add), s_v)
            var_tgt = s_v.n
            b.wait("scalar", s_v, var_tgt)
            b.run("scalar", lambda e: e.activation(
                cols[:, 4:5], cols[:, 3:4], AF.Sqrt, bias=epsc[:, 0:1]), s_s)
            std_tgt = s_s.n
            b.wait("vector", s_s, std_tgt)
            b.run("vector", lambda e: e.reciprocal(cols[:, 5:6], cols[:, 4:5]), s_v)
            if tau >= 2:
                b.wait("vector", s_pe, pe_done[tau - 2])
            b.run("vector", lambda e, p=par: e.tensor_scalar(
                xnb[p][:], xt[p][:], cols[:, 1:2], cols[:, 5:6],
                ALU.subtract, ALU.mult), s_v)
            v_apply.append(s_v.n)
            b.wait("tensor", s_v, v_apply[-1])
            for qd in range(8):
                reg = qd % 2
                gq_i = tau * 8 + qd
                if gq_i >= 2:
                    b.wait("tensor", s_s, ev_done[gq_i - 2])
                for j in range(4):
                    f = qd * 4 + j
                    b.run("tensor", lambda e, p=par, ff=f, r=reg, jj=j: e.matmul(
                        ps[:, r * 512 + jj * 128: r * 512 + (jj + 1) * 128],
                        xnb[p][:, ff * 128:(ff + 1) * 128], ident[:],
                        start=(jj == 0), stop=(jj == 3)),
                        s_pe if j == 3 else None)
                b.wait("scalar", s_pe, s_pe.n)
                b.run("scalar", lambda e, r=reg, qd_=qd, t_=tau: e.copy(
                    xnT_v[:, qd_ * 4:(qd_ + 1) * 4, t_ * 128:(t_ + 1) * 128],
                    ps[:, r * 512:(r + 1) * 512].rearrange(
                        "p (a c) -> p a c", a=4)), s_s)
                ev_done.append(s_s.n)
            pe_done.append(s_pe.n)

    with ExitStack() as sA:
        eA = sA.enter_context
        xnT = eA(nc.sbuf_tensor("xnT", [128, DC * T], BF16))

        with ExitStack() as sA1:
            e1 = sA1.enter_context
            sc1 = dict(
                xt=(e1(nc.sbuf_tensor("xt0", [128, D], F32)),
                    e1(nc.sbuf_tensor("xt1", [128, D], F32))),
                xsq=e1(nc.sbuf_tensor("xsq", [128, D], F32)),
                xnb=(e1(nc.sbuf_tensor("xnb0", [128, D], BF16)),
                     e1(nc.sbuf_tensor("xnb1", [128, D], BF16))),
                cols=e1(nc.sbuf_tensor("cols", [128, 8], F32)),
                ps=e1(nc.psum_tensor("ps_a", [128, 1024], F32)))
            ln_transpose(x_in, xnT, sc1)
        b.barrier()

        wsb = (eA(nc.sbuf_tensor("wsb0", [128, DC * 512], BF16)),
               eA(nc.sbuf_tensor("wsb1", [128, DC * 512], BF16)))
        kraw = [eA(nc.sbuf_tensor(f"kraw{i}", [128, 512], F32)) for i in range(4)]
        k2b = [eA(nc.sbuf_tensor(f"k2b{i}", [128, 512], BF16)) for i in range(2)]
        kp2 = [eA(nc.sbuf_tensor(f"kp2{i}", [128, 512], BF16)) for i in range(2)]
        tmpf = [eA(nc.sbuf_tensor(f"tmpf{i}", [128, 512], F32)) for i in range(2)]
        srow = [eA(nc.sbuf_tensor(f"srow{i}", [1, 512], F32)) for i in range(2)]
        rrow = [eA(nc.sbuf_tensor(f"rrow{i}", [1, 512], F32)) for i in range(2)]
        kbf = [eA(nc.sbuf_tensor(f"kbf{i}", [128, 1024], BF16)) for i in range(2)]
        vpc = [eA(nc.sbuf_tensor(f"vpc{i}", [128, 512], BF16)) for i in range(4)]
        ps_p = eA(nc.psum_tensor("ps_p", [128, 1024], F32))
        ps_r1 = eA(nc.psum_tensor("ps_r1", [128, 512], F32))
        ps_bc = eA(nc.psum_tensor("ps_bc", [128, 512], F32))
        ps_r2 = eA(nc.psum_tensor("ps_r2", [128, 512], F32))
        ps_v2 = eA(nc.psum_tensor("ps_v2", [128, 1024], F32))

        xnT_v = xnT[:].rearrange("p (f t) -> p f t", f=DC)

        def emit_proj(w_dram, bias_sb, g_col, out_kind):
            b.wait("tensor", s_s, s_s.n)
            b.wait("tensor", s_v, s_v.n)
            b.wait("scalar", s_v, s_v.n)
            b.wait("vector", s_s, s_s.n)
            b.wait("vector", stk[0], stk[0].n)
            b.wait("vector", stk[1], stk[1].n)

            pe_base, s_base, v_base = s_pe.n, s_s.n, s_v.n
            pe_entry = s_pe.n
            stk_base = [stk[0].n, stk[1].n]
            ldw_base = [ldw[0].n, ldw[1].n]

            def sq_inc(i):
                return s_base + 2 * i + 1
            def srow_inc(i):
                return s_base + 2 * i + 2
            v_kraw = [0] * 64
            v_rr = [0] * 64
            v_tt = [0] * 64
            v_kbf = [0] * 64
            cnt = v_base
            for i in range(64 + 3):
                if i < 64:
                    cnt += 1
                    v_kraw[i] = cnt
                if 0 <= i - 1 < 64:
                    cnt += 1
                    v_rr[i - 1] = cnt
                if 0 <= i - 3 < 64:
                    cnt += 2
                    v_tt[i - 3] = cnt - 1
                    v_kbf[i - 3] = cnt
            pe_proj = [0] * 64
            pe_sumsq = [0] * 64
            pe_bc = [0] * 64

            for i in range(64 + 4):
                if 0 <= i - 3 < 64:
                    j = i - 3
                    b.wait("tensor", s_v, v_rr[j])
                    if j >= 1:
                        b.wait("tensor", s_v, v_tt[j - 1])
                    b.run("tensor", lambda e, jj=j % 2: e.matmul(
                        ps_bc[:], ones_f[0:1, :], rrow[jj][0:1, :],
                        start=True, stop=True), s_pe)
                    pe_bc[j] = s_pe.n
                if i < 64:
                    h, th = divmod(i, 2)
                    p = h // 4
                    oc = h % 4
                    par = i % 2
                    if th == 0 and oc == 0:
                        b.wait("tensor", ldw[p % 2],
                               ldw_base[p % 2] + 16 * (p // 2 + 1))
                    if i >= 2:
                        b.wait("tensor", s_v, v_kraw[i - 2])
                    for f in range(DC):
                        b.run("tensor", lambda e, pp=p, o=oc, tt=th, ff=f, pr=par:
                              e.matmul(ps_p[:, pr * 512:(pr + 1) * 512],
                                       wsb[pp % 2][:, ff * 512 + o * 128:
                                                   ff * 512 + (o + 1) * 128],
                                       xnT_v[:, ff, tt * 512:(tt + 1) * 512],
                                       start=(ff == 0), stop=(ff == DC - 1)),
                              s_pe if f == DC - 1 else None)
                    pe_proj[i] = s_pe.n
                if 0 <= i - 1 < 64:
                    j = i - 1
                    b.wait("tensor", s_s, sq_inc(j))
                    if j >= 2:
                        b.wait("tensor", s_s, srow_inc(j - 2))
                    b.run("tensor", lambda e, jj=j % 2: e.matmul(
                        ps_r1[0:1, :], ones_b[:, 0:1], k2b[jj][:],
                        start=True, stop=True), s_pe)
                    pe_sumsq[j] = s_pe.n

            for i in range(64):
                b.wait("scalar", s_v, v_kraw[i])
                if i >= 2:
                    b.wait("scalar", s_pe, pe_sumsq[i - 2])
                b.run("scalar", lambda e, i4=i % 4, pr=i % 2: e.activation(
                    k2b[pr][:], kraw[i4][:], AF.Square), s_s)
                b.wait("scalar", s_pe, pe_sumsq[i])
                if i >= 2:
                    b.wait("scalar", s_v, v_rr[i - 2])
                b.run("scalar", lambda e, jj=i % 2: e.activation(
                    srow[jj][0:1, :], ps_r1[0:1, :], AF.Sqrt, scale=1.0 / HD,
                    bias=epsc[0:1, 1:2]), s_s)

            for i in range(64 + 3):
                if i < 64:
                    h, th = divmod(i, 2)
                    b.wait("vector", s_pe, pe_proj[i])
                    if i >= 4:
                        b.wait("vector", s_s, sq_inc(i - 4))
                    b.run("vector", lambda e, hh=h, pr=i % 2, i4=i % 4:
                          e.tensor_scalar(
                              kraw[i4][:], ps_p[:, pr * 512:(pr + 1) * 512],
                              bias_sb[:, hh:hh + 1], None, ALU.add), s_v)
                if 0 <= i - 1 < 64:
                    j = i - 1
                    b.wait("vector", s_s, srow_inc(j))
                    if j >= 2:
                        b.wait("vector", s_pe, pe_bc[j - 2])
                    b.run("vector", lambda e, jj=j % 2: e.reciprocal(
                        rrow[jj][0:1, :], srow[jj][0:1, :]), s_v)
                if 0 <= i - 3 < 64:
                    j = i - 3
                    h, th = divmod(j, 2)
                    b.wait("vector", s_pe, pe_bc[j])
                    b.run("vector", lambda e, i4=j % 4, pr=j % 2: e.tensor_tensor(
                        tmpf[pr][:], kraw[i4][:], ps_bc[:], ALU.mult), s_v)
                    if j >= 4:
                        b.wait("vector", stk[h % 2],
                               stk_base[h % 2] + 16 * (h // 2))
                    b.run("vector", lambda e, hh=h, tt=th, pr=j % 2:
                          e.tensor_scalar(
                              kbf[hh % 2][:, tt * 512:(tt + 1) * 512],
                              tmpf[pr][:], g_col[:, 0:1], None, ALU.mult), s_v)

            w_re = w_dram.rearrange("(f p) o -> p f o", p=128)
            for p in range(8):
                if p >= 2:
                    b.wait("gpsimd", s_pe, pe_proj[(p - 2) * 8 + 7])
                else:
                    b.wait("gpsimd", s_pe, pe_entry)
                b.dma("gpsimd", wsb[p % 2][:].rearrange("p (f o) -> p f o", f=DC),
                      w_re[:, :, p * 512:(p + 1) * 512], ldw[p % 2])

            for h in range(H):
                b.wait("sync", s_v, v_kbf[2 * h + 1])
                dst = (cck_in[h * 128:(h + 1) * 128, :] if out_kind == "k"
                       else q_dram[h * 128:(h + 1) * 128, :])
                b.dma("sync", dst, kbf[h % 2][:], stk[h % 2])

        emit_proj(wk, bk_sb, gk_sb, "k")
        for s_ in stk:
            b.wait("gpsimd", s_, s_.n)
        b.run("gpsimd", lambda e: e.collective_compute(
            "AllGather", ALU.bypass,
            replica_groups=[[0, 1, 2, 3], [4, 5, 6, 7]],
            ins=[cck_in.opt()], outs=[cck_out.opt()]), s_cc, 1)

        b.wait("tensor", s_s, s_s.n)
        b.wait("vector", s_s, s_s.n)
        v_pe_base, v_v_base = s_pe.n, s_v.n
        v_st_base = [s.n for s in stv]
        v_ld_base = [ldw[0].n, ldw[1].n]
        def vg_pe(gi):
            return v_pe_base + gi + 1
        def vg_v(gi, t2):
            return v_v_base + gi * 2 + t2 + 1
        w_re = wv.rearrange("(f p) o -> p f o", p=128)
        for p in range(8):
            if p >= 2:
                b.wait("gpsimd", s_pe, vg_pe((p - 1) * 4 - 1))
            else:
                b.wait("gpsimd", s_pe, v_pe_base)
            b.dma("gpsimd", wsb[p % 2][:].rearrange("p (f o) -> p f o", f=DC),
                  w_re[:, :, p * 512:(p + 1) * 512], ldw[p % 2])
        for p in range(8):
            b.wait("tensor", ldw[p % 2], v_ld_base[p % 2] + 16 * (p // 2 + 1))
            for tcg in range(4):
                gi = p * 4 + tcg
                if gi >= 1:
                    b.wait("tensor", s_v, vg_v(gi - 1, 1))
                for f in range(DC):
                    for t2 in range(2):
                        tc = tcg * 2 + t2
                        b.run("tensor", lambda e, ff=f, t2_=t2, tc_=tc, pp=p:
                              e.matmul(ps_v2[:, t2_ * 512:(t2_ + 1) * 512],
                                       xnT_v[:, ff, tc_ * 128:(tc_ + 1) * 128],
                                       wsb[pp % 2][:, ff * 512:(ff + 1) * 512],
                                       start=(ff == 0), stop=(ff == DC - 1)),
                              s_pe if (f == DC - 1 and t2 == 1) else None)
                pe_tgt = s_pe.n
                for t2 in range(2):
                    tc = tcg * 2 + t2
                    b.wait("vector", s_pe, pe_tgt)
                    vslot = (gi * 2 + t2) % 4
                    if gi * 2 + t2 >= 4:
                        b.wait("vector", stv[vslot],
                               v_st_base[vslot] + 16 * ((gi * 2 + t2) // 4))
                    b.run("vector", lambda e, sl=vslot, t2_=t2: e.tensor_copy(
                        vpc[sl][:], ps_v2[:, t2_ * 512:(t2_ + 1) * 512]), s_v)
                    b.wait("sync", s_v, s_v.n)
                    off = (p // 2) * 1024 + (p % 2) * 512
                    dst = ccv_in[tc * 512:(tc + 1) * 512, :].rearrange(
                        "(t f) c -> t (f c)", t=128)[:, off:off + 512]
                    b.dma("sync", dst, vpc[vslot][:], stv[vslot])

        for s_ in stv:
            b.wait("gpsimd", s_, s_.n)
        b.run("gpsimd", lambda e: e.collective_compute(
            "AllGather", ALU.bypass,
            replica_groups=[[0, 1, 2, 3], [4, 5, 6, 7]],
            ins=[ccv_in.opt()], outs=[ccv_out.opt()]), s_cc, 1)

        emit_proj(wq, bq_sb, gq_sb, "q")
        q_st_final = [stk[0].n, stk[1].n]
        b.barrier()

    with ExitStack() as sB:
        eB = sB.enter_context
        ao = eB(nc.sbuf_tensor("ao", [128, H * T], BF16))

        with ExitStack() as sB1:
            e4 = sB1.enter_context
            probsT = [e4(nc.sbuf_tensor(f"probsT{i}", [128, DC * 512], BF16))
                      for i in range(2)]
            khb = [e4(nc.sbuf_tensor(f"khb{i}", [128, 4096], BF16))
                   for i in range(2)]
            vhb = [e4(nc.sbuf_tensor(f"vhb{i}", [128, 4096], BF16))
                   for i in range(2)]
            qhb = [e4(nc.sbuf_tensor(f"qhb{i}", [128, 1024], BF16))
                   for i in range(2)]
            recb = [e4(nc.sbuf_tensor(f"recb{i}", [128, 512], F32))
                    for i in range(2)]
            tta = [e4(nc.sbuf_tensor(f"tta{i}", [128, 512], F32))
                   for i in range(2)]
            grow = e4(nc.sbuf_tensor("grow", [1, 256], F32))
            nbcol = e4(nc.sbuf_tensor("nbcol", [128, 1], F32))
            srow2 = e4(nc.sbuf_tensor("srow2", [1, 512], F32))
            rrow2 = e4(nc.sbuf_tensor("rrow2", [1, 512], F32))
            ps_sc = e4(nc.psum_tensor("ps_sc", [128, 2048], F32))
            ps_pv = e4(nc.psum_tensor("ps_pv", [128, 1024], F32))
            ps_sum = e4(nc.psum_tensor("ps_sum", [128, 512], F32))
            ps_m = e4(nc.psum_tensor("ps_m", [128, 512], F32))

            cc_r = cck_out.rearrange("(r z) c -> r z c", r=4)

            b.run("tensor", lambda e: e.matmul(
                ps_m[0:1, 0:128], gq_sb[:, 0:1], idf_in_sb[:],
                start=True, stop=True), s_pe)
            b.run("tensor", lambda e: e.matmul(
                ps_m[0:1, 128:256], gk_sb[:, 0:1], idf_in_sb[:],
                start=True, stop=True), s_pe)
            gt_tgt = s_pe.n
            b.wait("scalar", s_pe, gt_tgt)
            b.run("scalar", lambda e: e.copy(grow[0:1, :], ps_m[0:1, 0:256]), s_s)
            g_cp = s_s.n
            b.wait("vector", s_s, g_cp)
            b.run("vector", lambda e: e.tensor_tensor(
                grow[0:1, :], grow[0:1, :], grow[0:1, :], ALU.mult), s_v)
            b.run("vector", lambda e: e.reduce_max(
                grow[0:1, 0:1], grow[0:1, 0:128], axis=AX), s_v)
            b.run("vector", lambda e: e.reduce_max(
                grow[0:1, 1:2], grow[0:1, 128:256], axis=AX), s_v)
            b.run("vector", lambda e: e.tensor_tensor(
                grow[0:1, 0:1], grow[0:1, 0:1], grow[0:1, 1:2], ALU.mult), s_v)
            g_m2 = s_v.n
            b.wait("scalar", s_v, g_m2)
            b.run("scalar", lambda e: e.activation(
                grow[0:1, 2:3], grow[0:1, 0:1], AF.Sqrt), s_s)
            g_sq = s_s.n
            b.wait("vector", s_s, g_sq)
            b.run("vector", lambda e: e.tensor_scalar(
                grow[0:1, 2:3], grow[0:1, 2:3], -128.0, None, ALU.mult), s_v)
            nb_tgt = s_v.n
            b.wait("tensor", s_v, nb_tgt)
            b.run("tensor", lambda e: e.matmul(
                ps_m[:, 0:1], ones_f[0:1, :], grow[0:1, 2:3],
                start=True, stop=True), s_pe)
            bc_all = s_pe.n
            b.wait("scalar", s_pe, bc_all)
            b.run("scalar", lambda e: e.copy(nbcol[:], ps_m[:, 0:1]), s_s)
            nbcol_tgt = s_s.n

            pe_base = s_pe.n
            import os
            if os.environ.get("KDBG"):
                print("P4 pe_base", pe_base, "s_base", s_s.n, "v_base", s_v.n)
            per_nh = 19
            def sc_pair_inc(h, nh, mcg):
                return pe_base + (h * 2 + nh) * per_nh + mcg + 1
            def sums_inc(h, nh):
                return pe_base + (h * 2 + nh) * per_nh + 17
            def pv_inc(h, nh):
                return pe_base + (h * 2 + nh) * per_nh + 18
            def bc2_inc(h, nh):
                return pe_base + (h * 2 + nh) * per_nh + 19
            s_base = s_s.n
            def exp_inc(h, nh, mcg):
                return s_base + (h * 2 + nh) * 18 + mcg + 1
            def srow_cp_inc(h, nh):
                return s_base + (h * 2 + nh) * 18 + 17
            def rec_inc(h, nh):
                return s_base + (h * 2 + nh) * 18 + 18
            v_base = s_v.n
            def rr2_inc(h, nh):
                return v_base + (h * 2 + nh) * 3 + 1
            def aot_inc(h, nh):
                return v_base + (h * 2 + nh) * 3 + 2
            def aow_inc(h, nh):
                return v_base + (h * 2 + nh) * 3 + 3
            ldk_base = [ldk[0].n, ldk[1].n]
            ldv_base = [ldv[0].n, ldv[1].n]
            ldq_base = [ldq[0].n, ldq[1].n]

            b.wait("sync", s_cc, s_cc.n)
            b.wait("sync", stk[0], q_st_final[0])
            b.wait("sync", stk[1], q_st_final[1])
            for h in range(H):
                if h >= 2:
                    b.wait("sync", s_pe, pv_inc(h - 2, 1))
                b.dma("sync", khb[h % 2][:].rearrange("p (r t) -> p r t", r=4),
                      cc_r[:, h * 128:(h + 1) * 128, :].transpose([1, 0, 2]),
                      ldk[h % 2])
                for r in range(4):
                    off = (h // 8) * 1024 + (h % 8) * 128
                    vsrc = ccv_out[r * D:(r + 1) * D, :].rearrange(
                        "(tt t f) c -> tt t (f c)", tt=TC, t=128)[
                        :, :, off:off + 128].transpose([1, 0, 2])
                    b.dma("sync",
                          vhb[h % 2][:, r * 1024:(r + 1) * 1024].rearrange(
                              "p (tt d) -> p tt d", tt=TC),
                          vsrc, ldv[h % 2])
                b.dma("sync", qhb[h % 2][:], q_dram[h * 128:(h + 1) * 128, :],
                      ldq[h % 2])

            for h in range(H):
                b.wait("tensor", ldk[h % 2], ldk_base[h % 2] + 16 * (h // 2 + 1))
                b.wait("tensor", ldv[h % 2], ldv_base[h % 2] + 64 * (h // 2 + 1))
                b.wait("tensor", ldq[h % 2], ldq_base[h % 2] + 16 * (h // 2 + 1))
                for nh in range(2):
                    for mcg in range(16):
                        reg = mcg % 2
                        gidx = (h * 2 + nh) * 16 + mcg
                        if gidx >= 2:
                            pm = gidx - 2
                            hh, rem = divmod(pm, 32)
                            nn, mm = divmod(rem, 16)
                            b.wait("tensor", s_s, exp_inc(hh, nn, mm))
                        for j in range(2):
                            mc = mcg * 2 + j
                            b.run("tensor",
                                  lambda e, hh=h, n_=nh, m_=mc, r=reg, jj=j:
                                  e.matmul(ps_sc[:, r * 1024 + jj * 512:
                                                 r * 1024 + (jj + 1) * 512],
                                           khb[hh % 2][:, m_ * 128:(m_ + 1) * 128],
                                           qhb[hh % 2][:, n_ * 512:(n_ + 1) * 512],
                                           start=True, stop=True),
                                  s_pe if j == 1 else None)
                    b.wait("tensor", s_s, exp_inc(h, nh, 15))
                    if h * 2 + nh >= 1:
                        pi = h * 2 + nh - 1
                        b.wait("tensor", s_s, srow_cp_inc(pi // 2, pi % 2))
                    for mc in range(DC):
                        b.run("tensor", lambda e, m_=mc, pb=(h * 2 + nh) % 2:
                              e.matmul(
                                  ps_sum[0:1, :], ones_b[:, 0:1],
                                  probsT[pb][:, m_ * 512:(m_ + 1) * 512],
                                  start=(m_ == 0), stop=(m_ == DC - 1)),
                              s_pe if mc == DC - 1 else None)
                    if h * 2 + nh >= 2:
                        pi = h * 2 + nh - 2
                        b.wait("tensor", s_v, aot_inc(pi // 2, pi % 2))
                    for mc in range(DC):
                        b.run("tensor", lambda e, hh=h, n_=nh, m_=mc,
                              pb=(h * 2 + nh) % 2: e.matmul(
                                  ps_pv[:, (n_ % 2) * 512:(n_ % 2 + 1) * 512],
                                  vhb[hh % 2][:, m_ * 128:(m_ + 1) * 128],
                                  probsT[pb][:, m_ * 512:(m_ + 1) * 512],
                                  start=(m_ == 0), stop=(m_ == DC - 1)),
                              s_pe if mc == DC - 1 else None)
                    b.wait("tensor", s_v, rr2_inc(h, nh))
                    if h * 2 + nh >= 1:
                        pi = h * 2 + nh - 1
                        b.wait("tensor", s_s, rec_inc(pi // 2, pi % 2))
                    else:
                        b.wait("tensor", s_s, nbcol_tgt)
                    b.run("tensor", lambda e: e.matmul(
                        ps_m[:], ones_f[0:1, :], rrow2[0:1, :],
                        start=True, stop=True), s_pe)

                for nh in range(2):
                    for mcg in range(16):
                        b.wait("scalar", s_pe, sc_pair_inc(h, nh, mcg))
                        if mcg == 0:
                            u = h * 2 + nh
                            if u >= 2:
                                pu = u - 2
                                b.wait("scalar", s_pe,
                                       pv_inc(pu // 2, pu % 2))
                            else:
                                b.wait("scalar", s_s, nbcol_tgt)
                        b.run("scalar", lambda e, m_=mcg, pb=(h * 2 + nh) % 2:
                              e.activation(
                                  probsT[pb][:, m_ * 1024:(m_ + 1) * 1024],
                                  ps_sc[:, (m_ % 2) * 1024:(m_ % 2 + 1) * 1024],
                                  AF.Exp, bias=nbcol[:, 0:1]), s_s)
                    b.wait("scalar", s_pe, sums_inc(h, nh))
                    if h * 2 + nh >= 1:
                        pi = h * 2 + nh - 1
                        b.wait("scalar", s_v, rr2_inc(pi // 2, pi % 2))
                    b.run("scalar", lambda e: e.copy(srow2[0:1, :],
                                                     ps_sum[0:1, :]), s_s)
                    b.wait("scalar", s_pe, bc2_inc(h, nh))
                    b.run("scalar", lambda e, n_=nh: e.copy(recb[n_][:],
                                                            ps_m[:]), s_s)

                for nh in range(2):
                    b.wait("vector", s_s, srow_cp_inc(h, nh))
                    if h * 2 + nh >= 1:
                        pi = h * 2 + nh - 1
                        b.wait("vector", s_pe, bc2_inc(pi // 2, pi % 2))
                    b.run("vector", lambda e: e.reciprocal(rrow2[0:1, :],
                                                           srow2[0:1, :]), s_v)
                    b.wait("vector", s_pe, pv_inc(h, nh))
                    b.wait("vector", s_s, rec_inc(h, nh))
                    b.run("vector", lambda e, n_=nh: e.tensor_tensor(
                        tta[n_][:], ps_pv[:, (n_ % 2) * 512:(n_ % 2 + 1) * 512],
                        recb[n_][:], ALU.mult), s_v)
                    b.run("vector", lambda e, hh=h, n_=nh: e.tensor_scalar(
                        ao[:, hh * 1024 + n_ * 512:hh * 1024 + (n_ + 1) * 512],
                        tta[n_][:], bv_sb[:, hh:hh + 1], None, ALU.add), s_v)
        b.barrier()

        with ExitStack() as sB2:
            e5 = sB2.enter_context
            wob = [e5(nc.sbuf_tensor(f"wob{i}", [128, DC * 512], BF16))
                   for i in range(2)]
            opan = [e5(nc.sbuf_tensor(f"opan{i}", [128, 4096], BF16))
                    for i in range(2)]
            xsl = [e5(nc.sbuf_tensor(f"xsl{i}", [128, 512], F32))
                   for i in range(4)]
            x1p = [e5(nc.sbuf_tensor(f"x1p{i}", [128, 512], F32))
                   for i in range(4)]
            ps_o = e5(nc.psum_tensor("ps_o", [128, 1024], F32))
            ps_t5 = e5(nc.psum_tensor("ps_t5", [128, 1024], F32))

            pe5, s5, v5 = s_pe.n, s_s.n, s_v.n
            ld5 = [s.n for s in ldx4]
            ldg5 = [ldw[0].n, ldw[1].n]
            st5 = [s.n for s in stx1]
            def o_proj_inc(p, i8):
                return pe5 + p * 16 + i8 + 1
            def o_tq_inc(p, tau):
                return pe5 + p * 16 + 8 + tau + 1
            def o_ev_inc(p, i8):
                return v5 + p * 16 + i8 + 1
            def x1_inc(p, tau):
                return v5 + p * 16 + 8 + tau + 1

            wo_re = wo.rearrange("(f p) o -> p f o", p=128)
            for p in range(8):
                if p >= 2:
                    b.wait("gpsimd", s_pe, o_proj_inc(p - 2, 7))
                b.dma("gpsimd", wob[p % 2][:].rearrange("p (f o) -> p f o", f=DC),
                      wo_re[:, :, p * 512:(p + 1) * 512], ldw[p % 2])
            for p in range(8):
                for tau in range(8):
                    gi = p * 8 + tau
                    if gi >= 4:
                        pj = gi - 4
                        b.wait("sync", s_v, x1_inc(pj // 8, pj % 8))
                    b.dma("sync", xsl[gi % 4][:],
                          x_in[tau * 128:(tau + 1) * 128,
                               p * 512:(p + 1) * 512], ldx4[gi % 4])
            for p in range(8):
                b.wait("tensor", ldw[p % 2], ldg5[p % 2] + 16 * (p // 2 + 1))
                for i8 in range(8):
                    oc, th = divmod(i8, 2)
                    gi8 = p * 8 + i8
                    if gi8 >= 2:
                        pj = gi8 - 2
                        b.wait("tensor", s_v, o_ev_inc(pj // 8, pj % 8))
                    for hc in range(DC):
                        b.run("tensor", lambda e, pp=p, o=oc, tt=th, c=hc, r=i8 % 2:
                              e.matmul(ps_o[:, r * 512:(r + 1) * 512],
                                       wob[pp % 2][:, c * 512 + o * 128:
                                                   c * 512 + (o + 1) * 128],
                                       ao[:, c * 1024 + tt * 512:
                                          c * 1024 + (tt + 1) * 512],
                                       start=(c == 0), stop=(c == DC - 1)),
                              s_pe if hc == DC - 1 else None)
                b.wait("tensor", s_v, o_ev_inc(p, 7))
                for tau in range(8):
                    gq_i = p * 8 + tau
                    if gq_i >= 2:
                        pj = gq_i - 2
                        b.wait("tensor", s_v, x1_inc(pj // 8, pj % 8))
                    for oc in range(4):
                        b.run("tensor", lambda e, pp=p, t_=tau, o=oc: e.matmul(
                            ps_t5[:, (t_ % 2) * 512 + o * 128:
                                  (t_ % 2) * 512 + (o + 1) * 128],
                            opan[pp % 2][:, o * 1024 + t_ * 128:
                                         o * 1024 + (t_ + 1) * 128],
                            ident[:], start=(o == 0), stop=(o == 3)),
                            s_pe if oc == 3 else None)
            for p in range(8):
                for i8 in range(8):
                    oc, th = divmod(i8, 2)
                    b.wait("vector", s_pe, o_proj_inc(p, i8))
                    if p >= 2 and i8 == 0:
                        b.wait("vector", s_pe, o_tq_inc(p - 2, 7))
                    b.run("vector", lambda e, pp=p, o=oc, tt=th, r=i8 % 2:
                          e.tensor_scalar(
                              opan[pp % 2][:, o * 1024 + tt * 512:
                                           o * 1024 + (tt + 1) * 512],
                              ps_o[:, r * 512:(r + 1) * 512],
                              bo_sb[:, pp * 4 + o:pp * 4 + o + 1], None,
                              ALU.add), s_v)
                for tau in range(8):
                    gi = p * 8 + tau
                    b.wait("vector", s_pe, o_tq_inc(p, tau))
                    b.wait("vector", ldx4[gi % 4], ld5[gi % 4] + 16 * (gi // 4 + 1))
                    if gi >= 4:
                        b.wait("vector", stx1[gi % 4], st5[gi % 4] + 16 * (gi // 4))
                    b.run("vector", lambda e, t_=tau, g=gi: e.tensor_tensor(
                        x1p[g % 4][:],
                        ps_t5[:, (t_ % 2) * 512:((t_ % 2) + 1) * 512],
                        xsl[g % 4][:], ALU.add), s_v)
            for p in range(8):
                for tau in range(8):
                    gi = p * 8 + tau
                    b.wait("scalar", s_v, x1_inc(p, tau))
                    b.dma("scalar",
                          x1_dram[tau * 128:(tau + 1) * 128,
                                  p * 512:(p + 1) * 512],
                          x1p[gi % 4][:], stx1[gi % 4])
        b.barrier()
    x1_st_final = [s.n for s in stx1]

    with ExitStack() as sD:
        eD = sD.enter_context
        xn2T = eD(nc.sbuf_tensor("xn2T", [128, DC * T], BF16))

        with ExitStack() as sD1:
            e6 = sD1.enter_context
            sc2 = dict(
                xt=(e6(nc.sbuf_tensor("xt0b", [128, D], F32)),
                    e6(nc.sbuf_tensor("xt1b", [128, D], F32))),
                xsq=e6(nc.sbuf_tensor("xsqb", [128, D], F32)),
                xnb=(e6(nc.sbuf_tensor("xnb0b", [128, D], BF16)),
                     e6(nc.sbuf_tensor("xnb1b", [128, D], BF16))),
                cols=e6(nc.sbuf_tensor("colsb", [128, 8], F32)),
                ps=e6(nc.psum_tensor("ps_a2", [128, 1024], F32)))
            ln_transpose(x1_dram, xn2T, sc2,
                         pre_wait=list(zip(stx1, x1_st_final)))
        b.barrier()

        if KT == "3":
            b.dma("sync", y_out, x1_dram, sty4[0])
            b.wait("sync", sty4[0], sty4[0].n)
            b.barrier(drain=False)

        if KT != "3":
            mTg_sb = eD(nc.sbuf_tensor("mTg", [32, II], BF16))
            mTu_sb = eD(nc.sbuf_tensor("mTu", [32, II], BF16))
            srg = [eD(nc.sbuf_tensor(f"srg{i}", [1, D], BF16)) for i in range(2)]
            sru = [eD(nc.sbuf_tensor(f"sru{i}", [1, D], BF16)) for i in range(2)]
            Sx_sb = eD(nc.sbuf_tensor("Sx_sb", [32, T], BF16))

            b.dma("gpsimd", srg[0][:], rg_in[0:1, :], ldrg[0])
            b.dma("gpsimd", sru[0][:], ru_in[0:1, :], ldru[0])
            b.dma("gpsimd", srg[1][:], rg_in[1:2, :], ldrg[1])
            b.dma("gpsimd", sru[1][:], ru_in[1:2, :], ldru[1])

            IH = II // 4
            with ExitStack() as sMT:
                eM = sMT.enter_context
                szf = [eM(nc.sbuf_tensor(f"szf{i}", [32, IH], F32)) for i in range(4)]
                for hf in range(4):
                    sl = slice(hf * IH, (hf + 1) * IH)
                    if hf >= 1:
                        b.wait("scalar", s_v, s_v.n)
                        b.wait("scalar", s_gp, s_gp.n)
                    b.dma("scalar", szf[0][:], sgT_in[:, sl], ld_sy)
                    b.dma("scalar", szf[1][:], zgT_in[:, sl], ld_sy)
                    b.dma("scalar", szf[2][:], suT_in[:, sl], ld_sy)
                    b.dma("scalar", szf[3][:], zuT_in[:, sl], ld_sy)
                    tgt = ld_sy.n
                    b.wait("vector", ld_sy, tgt)
                    b.run("vector", lambda e: e.tensor_tensor(
                        szf[1][:], szf[1][:], szf[0][:], ALU.mult), s_v)
                    b.run("vector", lambda e, s=sl: e.tensor_scalar(
                        mTg_sb[:, s], szf[1][:], -1.0, None, ALU.mult), s_v)
                    b.wait("gpsimd", ld_sy, tgt)
                    b.run("gpsimd", lambda e: e.tensor_tensor(
                        szf[3][:], szf[3][:], szf[2][:], ALU.mult), s_gp)
                    b.run("gpsimd", lambda e, s=sl: e.tensor_scalar(
                        mTu_sb[:, s], szf[3][:], -1.0, None, ALU.mult), s_gp)
                mtg_done = s_v.n
                mtu_done = s_gp.n

            xn2T_v = xn2T[:].rearrange("p (f t) -> p f t", f=DC)
            with ExitStack() as sSX:
                ps_sx = sSX.enter_context(nc.psum_tensor("ps_sx", [128, 1024], F32))
                for th in range(2):
                    for g in range(DC):
                        b.run("tensor", lambda e, g_=g, tt=th: e.matmul(
                            ps_sx[0:32, tt * 512:(tt + 1) * 512],
                            csel_sb[:, 127 - g_:159 - g_],
                            xn2T_v[:, g_, tt * 512:(tt + 1) * 512],
                            start=(g_ == 0), stop=(g_ == DC - 1)),
                            s_pe if g == DC - 1 else None)
                sx_done = s_pe.n
                b.wait("scalar", s_pe, sx_done)
                b.run("scalar", lambda e: e.copy(Sx_sb[:], ps_sx[0:32, :]), s_s)
                sx_cp = s_s.n
        if KT == "2":
            b.wait("sync", s_s, sx_cp)
            b.dma("sync", y_out, x1_dram, sty4[0])
            b.wait("sync", sty4[0], sty4[0].n)
            b.barrier(drain=False)

        if KT not in ("2", "3"):
            qgT = [eD(nc.sbuf_tensor(f"qgT{i}", [128, D], U8)) for i in range(2)]
            quT = [eD(nc.sbuf_tensor(f"quT{i}", [128, D], U8)) for i in range(2)]
            dqgp = [eD(nc.sbuf_tensor(f"dqgp{i}", [128, D], BF16)) for i in range(2)]
            dqup = [eD(nc.sbuf_tensor(f"dqup{i}", [128, D], BF16)) for i in range(2)]
            gsb = [eD(nc.sbuf_tensor(f"gsb{i}", [128, 512], BF16)) for i in range(2)]
            usb = [eD(nc.sbuf_tensor(f"usb{i}", [128, 512], BF16)) for i in range(2)]
            hac = [eD(nc.sbuf_tensor(f"hac{i}", [128, 1024], BF16)) for i in range(2)]
            ps_g = eD(nc.psum_tensor("ps_g", [128, 1024], F32))
            ps_u = eD(nc.psum_tensor("ps_u", [128, 1024], F32))
            ps_bg = eD(nc.psum_tensor("ps_bg", [128, 1024], F32))
            ps_bu = eD(nc.psum_tensor("ps_bu", [128, 1024], F32))

            v6, g6, s6 = s_v.n, s_gp.n, s_s.n

            def vq_done(w, ic, g):
                pos = (g // 4) * 8 + (g % 4) + 1 + (0 if w == 'g' else 4)
                return v6 + ic * 64 + pos

            def hac_done(ic, th):
                return g6 + 2 * ic + th + 1

            def silu_done(ic, th):
                return s6 + ic * 4 + 2 * th + 1

            def uev_done(ic, th):
                return s6 + ic * 4 + 2 * th + 2

            bcg_val = [[0] * 8 for _ in range(ICN)]
            bcu_val = [[0] * 8 for _ in range(ICN)]
            gch_val = [[0, 0] for _ in range(ICN)]
            uch_val = [[0, 0] for _ in range(ICN)]

            def emit_bcasts(ic):
                for j in range(8):
                    k = ic * 8 + j
                    if k >= 2:
                        pic, pj = divmod(k - 2, 8)
                        b.wait("tensor", s_v, vq_done('g', pic, 4 * pj + 3))
                    if j == 0:
                        b.wait("tensor", ldrg[ic % 2], 16 * (ic // 2 + 1))
                        if ic == 0:
                            b.wait("tensor", s_s, sx_cp)
                    b.run("tensor", lambda e, i=ic, jj=j, r=k % 2: e.matmul(
                        ps_bg[:, r * 512:(r + 1) * 512], ones_b[0:1, :],
                        srg[i % 2][0:1, jj * 512:(jj + 1) * 512],
                        start=True, stop=True), s_pe)
                    bcg_val[ic][j] = s_pe.n
                    if k >= 2:
                        pic, pj = divmod(k - 2, 8)
                        b.wait("tensor", s_v, vq_done('u', pic, 4 * pj + 3))
                    if j == 0:
                        b.wait("tensor", ldru[ic % 2], 16 * (ic // 2 + 1))
                    b.run("tensor", lambda e, i=ic, jj=j, r=k % 2: e.matmul(
                        ps_bu[:, r * 512:(r + 1) * 512], ones_b[0:1, :],
                        sru[i % 2][0:1, jj * 512:(jj + 1) * 512],
                        start=True, stop=True), s_pe)
                    bcu_val[ic][j] = s_pe.n

            def emit_chain(w, ic, th):
                ps = ps_g if w == 'g' else ps_u
                mT = mTg_sb if w == 'g' else mTu_sb
                dqp = dqgp if w == 'g' else dqup
                if ic >= 1:
                    b.wait("tensor", s_s,
                           silu_done(ic - 1, th) if w == 'g' else uev_done(ic - 1, th))
                elif th == 0:
                    b.wait("tensor", s_v if w == 'g' else s_gp,
                           mtg_done if w == 'g' else mtu_done)
                b.run("tensor", lambda e, i=ic, tt=th, m=mT, p=ps: e.matmul(
                    p[:, tt * 512:(tt + 1) * 512], m[0:32, i * 128:(i + 1) * 128],
                    Sx_sb[0:32, tt * 512:(tt + 1) * 512],
                    start=True, stop=False))
                for g in range(DC):
                    if th == 0 and g % 4 == 0:
                        b.wait("tensor", s_v, vq_done(w, ic, g + 3))
                    b.run("tensor", lambda e, i=ic, tt=th, g_=g, p=ps, d=dqp:
                          e.matmul(p[:, tt * 512:(tt + 1) * 512],
                                   d[i % 2][:, g_ * 128:(g_ + 1) * 128],
                                   xn2T_v[:, g_, tt * 512:(tt + 1) * 512],
                                   start=False, stop=(g_ == DC - 1)),
                          s_pe if g == DC - 1 else None)
                (gch_val if w == 'g' else uch_val)[ic][th] = s_pe.n

            for ic in range(ICN):
                if ic == 0:
                    emit_bcasts(0)
                emit_chain('g', ic, 0)
                if ic + 1 < ICN:
                    emit_bcasts(ic + 1)
                emit_chain('g', ic, 1)
                emit_chain('u', ic, 0)
                emit_chain('u', ic, 1)

            def dq_op(w, ic, g):
                dqp, qT, psb = ((dqgp, qgT, ps_bg) if w == 'g'
                                else (dqup, quT, ps_bu))
                if g % 4 == 0:
                    b.wait("vector", s_pe,
                           (bcg_val if w == 'g' else bcu_val)[ic][g // 4])
                k = ic * 8 + g // 4
                b.run("vector", lambda e, i=ic, g_=g, r=k % 2, d=dqp, q=qT, p=psb:
                      e.tensor_tensor(
                          d[i % 2][:, g_ * 128:(g_ + 1) * 128],
                          q[i % 2][:, g_ * 128:(g_ + 1) * 128],
                          p[:, r * 512 + (g_ % 4) * 128:
                            r * 512 + (g_ % 4 + 1) * 128], ALU.mult), s_v)

            for ic in range(ICN):
                b.wait("vector", ldqg[ic % 2], 16 * (ic // 2 + 1))
                b.wait("vector", ldqu[ic % 2], 16 * (ic // 2 + 1))
                if ic >= 2:
                    b.wait("vector", s_pe, gch_val[ic - 2][1])
                    b.wait("vector", s_pe, uch_val[ic - 2][1])
                for j in range(8):
                    for g in range(4 * j, 4 * j + 4):
                        dq_op('g', ic, g)
                    for g in range(4 * j, 4 * j + 4):
                        dq_op('u', ic, g)

            for ic in range(ICN):
                if ic + 2 < ICN:
                    b.wait("gpsimd", s_pe, bcu_val[ic][7])
                    b.dma("gpsimd", srg[(ic + 2) % 2][:],
                          rg_in[ic + 2:ic + 3, :], ldrg[(ic + 2) % 2])
                    b.dma("gpsimd", sru[(ic + 2) % 2][:],
                          ru_in[ic + 2:ic + 3, :], ldru[(ic + 2) % 2])
                if ic >= 1:
                    for th in range(2):
                        b.wait("gpsimd", s_s, uev_done(ic - 1, th))
                        if ic >= 3:
                            b.wait("gpsimd", sth[(ic - 1) % 2],
                                   16 * ((ic - 1) // 2))
                        b.run("gpsimd", lambda e, i=ic - 1, tt=th:
                              e.tensor_tensor(
                                  hac[i % 2][:, tt * 512:(tt + 1) * 512],
                                  gsb[tt][:], usb[tt][:], ALU.mult), s_gp)
            for th in range(2):
                b.wait("gpsimd", s_s, uev_done(ICN - 1, th))
                b.wait("gpsimd", sth[(ICN - 1) % 2], 16 * ((ICN - 1) // 2))
                b.run("gpsimd", lambda e, i=ICN - 1, tt=th: e.tensor_tensor(
                    hac[i % 2][:, tt * 512:(tt + 1) * 512],
                    gsb[tt][:], usb[tt][:], ALU.mult), s_gp)

            for ic in range(ICN):
                for th in range(2):
                    b.wait("scalar", s_pe, gch_val[ic][th])
                    if ic >= 1:
                        b.wait("scalar", s_gp, hac_done(ic - 1, th))
                    b.run("scalar", lambda e, tt=th: e.activation(
                        gsb[tt][:], ps_g[:, tt * 512:(tt + 1) * 512], AF.Silu), s_s)
                    b.wait("scalar", s_pe, uch_val[ic][th])
                    b.run("scalar", lambda e, tt=th: e.activation(
                        usb[tt][:], ps_u[:, tt * 512:(tt + 1) * 512], AF.Copy), s_s)
                if ic >= 1:
                    b.dma("scalar", hact_dram[(ic - 1) * 128:ic * 128, :],
                          hac[(ic - 1) % 2][:], sth[(ic - 1) % 2])
            b.wait("scalar", s_gp, s_gp.n)
            b.dma("scalar", hact_dram[(ICN - 1) * 128:ICN * 128, :],
                  hac[(ICN - 1) % 2][:], sth[(ICN - 1) % 2])

            b.wait("sync", s_v, mtg_done)
            b.wait("sync", s_gp, mtu_done)
            for ic in range(ICN):
                if ic >= 2:
                    b.wait("sync", s_v, vq_done('g', ic - 2, DC - 1))
                    b.wait("sync", s_v, vq_done('u', ic - 2, DC - 1))
                b.dma("sync", qgT[ic % 2][:], pg_in[ic], ldqg[ic % 2])
                b.dma("sync", quT[ic % 2][:], pu_in[ic], ldqu[ic % 2])
            b.barrier()

    if KT:
        if KT == "1":
            b.dma("sync", y_out, x1_dram, sty4[0])
            b.wait("sync", sty4[0], sty4[0].n)
            b.barrier(drain=False)
        b.finalize_block()
        b.ctx.close()
        return nc

    with ExitStack() as sF:
        eF = sF.enter_context
        szd = {}
        for nm, src in (("sd", sd_in), ("zd", zd_in)):
            szd[nm] = eF(nc.sbuf_tensor(nm + "_sb", [128, 32 * ICN], F32))
            b.dma("sync", szd[nm][:], src, ld_sy)
        b.wait("vector", ld_sy)
        b.run("vector", lambda e: e.tensor_tensor(
            szd["zd"][:], szd["zd"][:], szd["sd"][:], ALU.mult), s_v)
        b.run("vector", lambda e: e.tensor_scalar(
            szd["zd"][:], szd["zd"][:], -1.0, None, ALU.mult), s_v)
        nzd_tgt = s_v.n

        qd_sb = [eF(nc.sbuf_tensor(f"qd{i}", [128, II], I32)) for i in range(2)]
        dqd = [eF(nc.sbuf_tensor(f"dqd{i}", [128, 128], BF16)) for i in range(8)]
        wdc = [eF(nc.sbuf_tensor(f"wdc{i}", [128, II], BF16)) for i in range(2)]
        ps_w7 = eF(nc.psum_tensor("ps_w7", [128, 2048], F32))

        groups = [list(range(g * 4, min(g * 4 + 4, ICN))) for g in range(22)]
        v_ics = [ic for ic in range(ICN) if ic % 4 != 3]
        p_ics = [ic for ic in range(ICN) if ic % 4 == 3]
        v_ord = {ic: i for i, ic in enumerate(v_ics)}
        p_ord = {ic: i for i, ic in enumerate(p_ics)}
        NV, NP = len(v_ics), len(p_ics)
        pe7 = s_pe.n
        def w7q_inc(hc, g):
            return pe7 + hc * 22 + g + 1
        s7 = s_s.n
        def w7e_inc(hc, e):
            return s7 + hc * 11 + e + 1
        v7 = s_v.n
        g7 = s_gp.n
        def dq7_sem_val(hc, ic):
            if ic % 4 == 3:
                return s_gp, g7 + hc * NP + p_ord[ic] + 1
            return s_v, v7 + hc * NV + v_ord[ic] + 1
        ld7 = [ldqd[0].n, ldqd[1].n]
        st7 = [stw[0].n, stw[1].n]

        for hc in range(32):
            if hc >= 2:
                b.wait("sync", s_v, v7 + (hc - 1) * NV)
                b.wait("sync", s_gp, g7 + (hc - 1) * NP)
            b.dma("sync", qd_sb[hc % 2][:], qwd[hc * 128:(hc + 1) * 128, :],
                  ldqd[hc % 2])
            if hc >= 1:
                b.wait("scalar", s_s, w7e_inc(hc - 1, 10))
                b.dma("scalar", wdt_dram[hc - 1, :, :], wdc[(hc - 1) % 2][:],
                      stw[(hc - 1) % 2])
            for eng, ics_e in (("vector", v_ics), ("gpsimd", p_ics)):
                b.wait(eng, ldqd[hc % 2], ld7[hc % 2] + 16 * (hc // 2 + 1))
                if hc == 0:
                    b.wait(eng, s_v, nzd_tgt)
                sem = s_v if eng == "vector" else s_gp
                for ic in ics_e:
                    if ic >= 8:
                        b.wait(eng, s_pe, w7q_inc(hc, (ic - 8) // 4))
                    elif hc >= 1:
                        b.wait(eng, s_pe, w7q_inc(hc - 1, 21))
                    b.run(eng, lambda e, h_=hc, i=ic: e.tensor_scalar(
                        dqd[i % 8][:], qd_sb[h_ % 2][:, i * 128:(i + 1) * 128],
                        szd["sd"][:, h_ * ICN + i:h_ * ICN + i + 1],
                        szd["zd"][:, h_ * ICN + i:h_ * ICN + i + 1],
                        ALU.mult, ALU.add), sem)
            for g, ics in enumerate(groups):
                G = hc * 22 + g
                if G >= 4:
                    pj = G - 4
                    b.wait("tensor", s_s, w7e_inc(pj // 22, (pj % 22) // 2))
                for j, ic in enumerate(ics):
                    dsem, dval = dq7_sem_val(hc, ic)
                    b.wait("tensor", dsem, dval)
                    b.run("tensor", lambda e, i=ic, jj=j, r=G % 4, n_=len(ics):
                          e.matmul(ps_w7[:, r * 512 + jj * 128:
                                         r * 512 + (jj + 1) * 128],
                                   dqd[i % 8][:], ident[:],
                                   start=(jj == 0), stop=(jj == n_ - 1)),
                          s_pe if j == len(ics) - 1 else None)
                if g % 2 == 1:
                    e_ = g // 2
                    b.wait("scalar", s_pe, w7q_inc(hc, g))
                    if e_ == 0 and hc >= 2:
                        b.wait("scalar", stw[hc % 2],
                               st7[hc % 2] + 16 * (hc // 2))
                    g0 = 2 * e_
                    width = (len(groups[g0]) + len(groups[g0 + 1])) * 128
                    r0 = (hc * 22 + g0) % 4
                    b.run("scalar", lambda e, h_=hc, c0=g0 * 512, w_=width,
                          rr=r0: e.copy(
                              wdc[h_ % 2][:, c0:c0 + w_],
                              ps_w7[:, rr * 512:rr * 512 + w_]), s_s)
        b.wait("scalar", s_s, s_s.n)
        b.dma("scalar", wdt_dram[31, :, :], wdc[1][:], stw[1])
        b.barrier()
    wdt_st_final = [stw[0].n, stw[1].n]

    with ExitStack() as sG:
        eG = sG.enter_context
        wdts = [eG(nc.sbuf_tensor(f"wdts{i}", [128, 8 * 512], BF16))
                for i in range(2)]
        hsl = [eG(nc.sbuf_tensor(f"hsl{i}", [128, 4 * 512], BF16))
               for i in range(2)]
        dnsb = eG(nc.sbuf_tensor("dnsb", [128, 8 * 512], BF16))
        x1s = [eG(nc.sbuf_tensor(f"x1s{i}", [128, 1024], F32)) for i in range(4)]
        ypc = [eG(nc.sbuf_tensor(f"ypc{i}", [128, 1024], F32)) for i in range(4)]
        ps_d = eG(nc.psum_tensor("ps_d", [128, 8 * 512], F32))

        hact_r = hact_dram.rearrange("(ic p) t -> ic p t", p=128)
        icbs = [list(range(g * 4, min(g * 4 + 4, ICN))) for g in range(22)]
        pe8 = s_pe.n
        def b_inc(pss, bi):
            return pe8 + pss * 30 + bi + 1
        def tq8_inc(pss, tau, half):
            return pe8 + pss * 30 + 22 + tau * 2 + half + 1
        s8 = s_s.n
        def dne_inc(pss, hc8):
            return s8 + pss * 8 + hc8 + 1
        v8 = s_v.n
        def y_inc(pss, tau):
            return v8 + pss * 8 + 2 * tau + 2
        ld8b = [ldb[0].n, ldb[1].n]
        ld8x = ldx1b.n

        b.wait("sync", stw[0], wdt_st_final[0])
        b.wait("sync", stw[1], wdt_st_final[1])
        b.wait("sync", sth[0], sth[0].n)
        b.wait("sync", sth[1], sth[1].n)
        for s_, v_ in zip(stx1, x1_st_final):
            b.wait("sync", s_, v_)
        for pss in range(8):
            th, grp = divmod(pss, 4)
            for bi, ics in enumerate(icbs):
                gi = pss * 22 + bi
                if gi >= 2:
                    pj = gi - 2
                    b.wait("sync", s_pe, b_inc(pj // 22, pj % 22))
                w = len(ics)
                b.dma("sync",
                      wdts[bi % 2][:, 0:8 * w * 128].rearrange(
                          "p (c i) -> p c i", c=8),
                      wdt_dram[grp * 8:(grp + 1) * 8, :,
                               ics[0] * 128:ics[0] * 128 + w * 128]
                      .transpose([1, 0, 2]), ldb[bi % 2])
                b.dma("sync",
                      hsl[bi % 2][:, 0:w * 512].rearrange(
                          "p (i t) -> p i t", i=w),
                      hact_r[ics[0]:ics[0] + w, :, th * 512:(th + 1) * 512]
                      .transpose([1, 0, 2]), ldb[bi % 2])
            for tau in range(4):
                if pss >= 1:
                    b.wait("sync", s_v, y_inc(pss - 1, tau))
                b.dma("sync", x1s[tau][:],
                      x1_dram[th * 512 + tau * 128:th * 512 + (tau + 1) * 128,
                              grp * 1024:(grp + 1) * 1024], ldx1b)
            for bi, ics in enumerate(icbs):
                gb = pss * 22 + bi
                b.wait("tensor", ldb[bi % 2], ld8b[bi % 2] + 32 * (gb // 2 + 1))
                if bi == 0 and pss >= 1:
                    b.wait("tensor", s_v, y_inc(pss - 1, 3))
                w = len(ics)
                for j, ic in enumerate(ics):
                    for hc8 in range(8):
                        b.run("tensor", lambda e, b_=bi, jj=j, c8=hc8, i=ic,
                              w_=w:
                              e.matmul(ps_d[:, c8 * 512:(c8 + 1) * 512],
                                       wdts[b_ % 2][:, c8 * w_ * 128 + jj * 128:
                                                    c8 * w_ * 128 + (jj + 1) * 128],
                                       hsl[b_ % 2][:, jj * 512:(jj + 1) * 512],
                                       start=(i == 0), stop=(i == ICN - 1)),
                              s_pe if (j == w - 1 and hc8 == 7) else None)
            for hc8 in range(8):
                b.wait("scalar", s_pe, b_inc(pss, 21))
                b.run("scalar", lambda e, c8=hc8: e.copy(
                    dnsb[:, c8 * 512:(c8 + 1) * 512],
                    ps_d[:, c8 * 512:(c8 + 1) * 512]), s_s)
            for tau in range(4):
                for half in range(2):
                    b.wait("tensor", s_s, dne_inc(pss, 7))
                    if tau * 2 + half >= 2:
                        b.wait("tensor", s_v, y_inc(pss, tau - 1))
                    for j in range(4):
                        hc8 = half * 4 + j
                        b.run("tensor", lambda e, c8=hc8, jj=j, hf=half, t_=tau:
                              e.matmul(ps_d[:, hf * 512 + jj * 128:
                                            hf * 512 + (jj + 1) * 128],
                                       dnsb[:, c8 * 512 + t_ * 128:
                                            c8 * 512 + (t_ + 1) * 128],
                                       ident[:], start=(jj == 0), stop=(jj == 3)),
                              s_pe if j == 3 else None)
                b.wait("vector", s_pe, tq8_inc(pss, tau, 1))
                b.wait("vector", ldx1b, ld8x + 64 * (pss + 1))
                if pss >= 1:
                    b.wait("vector", sty4[tau], 16 * pss)
                for half in range(2):
                    b.run("vector", lambda e, t_=tau, hf=half: e.tensor_tensor(
                        ypc[t_][:, hf * 512:(hf + 1) * 512],
                        ps_d[:, hf * 512:(hf + 1) * 512],
                        x1s[t_][:, hf * 512:(hf + 1) * 512], ALU.add), s_v)
                b.wait("scalar", s_v, y_inc(pss, tau))
                b.dma("scalar",
                      y_out[th * 512 + tau * 128:th * 512 + (tau + 1) * 128,
                            grp * 1024:(grp + 1) * 1024],
                      ypc[tau][:], sty4[tau])
        for s_ in sty4:
            b.wait("sync", s_, s_.n)
        b.barrier(drain=False)

    b.finalize_block()
    b.ctx.close()
    return nc


_NC_CACHE = None


def kernel(**inputs):
    global _NC_CACHE
    xs = np.ascontiguousarray(np.asarray(inputs["L0_attn_hidden_states"]))
    ident = np.eye(128, dtype=ml_dtypes.bfloat16)
    ones_b = np.ones((128, 128), dtype=ml_dtypes.bfloat16)
    ones_f = np.ones((128, 128), dtype=np.float32)

    def pcol(v):
        return np.ascontiguousarray(np.asarray(v, np.float32).reshape(32, 128).T)

    def prow_d(v):
        return np.ascontiguousarray(
            np.asarray(v, np.float32).reshape(32, 128, ICN)
            .transpose(1, 0, 2).reshape(128, -1))

    def panels_gu(qw):
        q = np.asarray(qw).astype(np.uint8)
        return np.ascontiguousarray(
            q.reshape(ICN, 128, 32, 128).transpose(0, 3, 2, 1)).reshape(
            ICN, 128, D)

    def srows_gu(s):
        return np.ascontiguousarray(
            np.asarray(s, np.float32).reshape(ICN, 128, 32)
            .transpose(0, 2, 1).reshape(ICN, D))

    csel = np.zeros((128, 255), dtype=ml_dtypes.bfloat16)
    csel[:, 127] = 1

    common = dict(
        wq=np.ascontiguousarray(np.asarray(inputs["L0_attn_Wq"], np.float32)),
        wk=np.ascontiguousarray(np.asarray(inputs["L0_attn_Wk"], np.float32)),
        wv=np.ascontiguousarray(np.asarray(inputs["L0_attn_Wv"], np.float32)),
        wo=np.ascontiguousarray(np.asarray(inputs["L0_attn_Wo"], np.float32)),
        qwd=np.ascontiguousarray(np.asarray(inputs["L0_ffn_qw_down"], np.int32)),
        pg=panels_gu(inputs["L0_ffn_qw_gate"]),
        pu=panels_gu(inputs["L0_ffn_qw_up"]),
        rg=srows_gu(inputs["L0_ffn_s_gate"]),
        ru=srows_gu(inputs["L0_ffn_s_up"]),
        sgT=np.ascontiguousarray(np.asarray(inputs["L0_ffn_s_gate"], np.float32).T),
        zgT=np.ascontiguousarray(np.asarray(inputs["L0_ffn_z_gate"], np.float32).T),
        suT=np.ascontiguousarray(np.asarray(inputs["L0_ffn_s_up"], np.float32).T),
        zuT=np.ascontiguousarray(np.asarray(inputs["L0_ffn_z_up"], np.float32).T),
        csel=csel,
        sd=prow_d(inputs["L0_ffn_s_down"]), zd=prow_d(inputs["L0_ffn_z_down"]),
        bq_p=pcol(inputs["L0_attn_bq"]), bk_p=pcol(inputs["L0_attn_bk"]),
        bv_p=pcol(inputs["L0_attn_bv"]), bo_p=pcol(inputs["L0_attn_bo"]),
        gq_c=np.ascontiguousarray(
            np.asarray(inputs["L0_attn_gq"], np.float32).reshape(128, 1)),
        gk_c=np.ascontiguousarray(
            np.asarray(inputs["L0_attn_gk"], np.float32).reshape(128, 1)),
        ident_bf=ident, ones_bf=ones_b, ones_f=ones_f,
        ident_f=np.eye(128, dtype=np.float32),
    )
    in_maps = []
    for c in range(8):
        bb, r = divmod(c, 4)
        m = dict(common)
        m["x_in"] = np.ascontiguousarray(xs[bb, r * T:(r + 1) * T, :], np.float32)
        in_maps.append(m)

    if _NC_CACHE is None:
        _NC_CACHE = build_program()
    res = run_bass_kernel_spmd(_NC_CACHE, in_maps, core_ids=list(range(8)))
    out = np.empty((B, N, D), np.float32)
    for c in range(8):
        bb, r = divmod(c, 4)
        out[bb, r * T:(r + 1) * T, :] = res.results[c]["y"]
    return out
